# version 02
# baseline (speedup 1.0000x reference)
import numpy as np
import ml_dtypes
import concourse.bass as bass
import concourse.mybir as mybir
from concourse.bass_utils import run_bass_kernel_spmd

F32 = mybir.dt.float32
F32R = mybir.dt.float32r
BF16 = mybir.dt.bfloat16
ALU = mybir.AluOpType
ACT = mybir.ActivationFunctionType
AX = mybir.AxisListType

DEBUG = False

NCORE = 8
HID = 1024
NT = 4096
HALO = 128
NTOT = NT + HALO
NGRP = 8
GW = 512
MAGIC = float(np.float32(1.5 * 2.0 ** 23))
QMAX = 32767.0
CONSTS = [1e-5, 1e-6, 1e-8, -0.5, 0.5, 1.0]
PADL = 128
PADR = 128
NTP = PADL + 2 * NT + PADR
PAIRS = [[0, 1], [2, 3], [4, 5], [6, 7]]

ENGS = ("sync", "gpsimd", "tensor", "vector", "scalar")


class Sched:

    def __init__(self):
        self.ops = {e: [] for e in ENGS}
        self.cnt = {}
        self.lastw = {e: {} for e in ENGS}

    def sem(self, name):
        self.cnt[name] = 0
        return name

    def emit(self, eng, fn, waits=(), inc=None, incby=1):
        ws = []
        for s, v in waits:
            if v > self.lastw[eng].get(s, -1):
                ws.append((s, v))
                self.lastw[eng][s] = v
        self.ops[eng].append((ws, fn, inc, incby))
        if inc is not None:
            self.cnt[inc] += incby
            return self.cnt[inc]
        return None

    last_xfer = None

    def dma(self, eng, out, in_, waits=(), inc="gQ"):
        return self.emit(eng, lambda e, o=out, i=in_: e.dma_start(out=o, in_=i),
                         waits=waits, inc=inc, incby=16)

    def xdma(self, eng, fn, waits=(), inc="gQ"):
        ws = list(waits)
        if self.last_xfer is not None:
            ws.append(self.last_xfer)
        v = self.emit(eng, fn, waits=ws, inc=inc, incby=16)
        self.last_xfer = (inc, v)
        return v

    preamble = None

    def build(self, nc, semhandles):
        issuer = {"gQ": "gpsimd", "sQ": "sync"}
        waited = {S: set() for S in issuer}
        if self.preamble and self.preamble[0] in waited:
            waited[self.preamble[0]].add(self.preamble[1])
        for e2 in ENGS:
            for ws, _fn, _inc, _incby in self.ops[e2]:
                for S, v in ws:
                    if S in waited:
                        waited[S].add(v)
        selfwaits = {e2: {} for e2 in ENGS}
        nbound = 0
        for S, engname in issuer.items():
            cnt = 0
            for idx, (_ws, _fn, inc, incby) in enumerate(self.ops[engname]):
                if inc == S:
                    if cnt > 0 and cnt in waited[S]:
                        selfwaits[engname][idx] = (S, cnt)
                        nbound += 1
                    cnt += incby
        import sys as _sys
        print(f"Sched: inserted {nbound} DMA-boundary self-waits "
              f"(gQ waited={len(waited['gQ'])}, sQ waited={len(waited['sQ'])})",
              file=_sys.stderr)

        with nc.Block() as blk:
            for engname in ENGS:
                ops = self.ops[engname]
                if not ops:
                    continue

                def section(eng, _ops=ops, _eng=engname):
                    if self.preamble and _eng != "gpsimd":
                        eng.wait_ge(semhandles[self.preamble[0]], self.preamble[1])
                    for idx, (ws, fn, inc, incby) in enumerate(_ops):
                        bw = selfwaits[_eng].get(idx)
                        if bw is not None:
                            eng.wait_ge(semhandles[bw[0]], bw[1])
                        for s, v in ws:
                            eng.wait_ge(semhandles[s], v)
                        instr = fn(eng)
                        if inc is not None:
                            assert instr is not None
                            instr.then_inc(semhandles[inc], incby)

                getattr(blk, engname)(section)


def rr(ap, p=128):
    return ap.rearrange("(c p) t -> p c t", p=p)


def build_program(dbg=False):
    nc = bass.Bass("TRN2", target_bir_lowering=False)
    from contextlib import ExitStack
    ctx = ExitStack()
    nc._ctx = ctx
    E = ctx.enter_context

    x_fm = nc.dram_tensor("x_fm", [HID, NTOT], F32, kind="ExternalInput").ap()
    w_up = nc.dram_tensor("w_up", [3072, HID], F32, kind="ExternalInput").ap()
    w_gate = nc.dram_tensor("w_gate", [3072, HID], F32, kind="ExternalInput").ap()
    w_down = nc.dram_tensor("w_down", [HID, 3072], F32, kind="ExternalInput").ap()
    ones_in = nc.dram_tensor("ones_in", [128, 1], F32, kind="ExternalInput").ap()
    cst_in = nc.dram_tensor("cst_in", [128, 16], F32, kind="ExternalInput").ap()
    onesr_in = nc.dram_tensor("onesr_in", [1, 128], F32, kind="ExternalInput").ap()
    out_fm = nc.dram_tensor("out_fm", [HID, NT], F32, kind="ExternalOutput").ap()

    spk = "ExternalOutput" if dbg else "Internal"
    xn_sp = nc.dram_tensor("xn_sp", [HID, NTOT], F32, kind=spk).ap()
    up_sp = nc.dram_tensor("up_sp", [3072, NT], F32, kind=spk).ap()
    gate_sp = nc.dram_tensor("gate_sp", [3072, NT], F32, kind=spk).ap()
    g2p_sp = nc.dram_tensor("g2p_sp", [3072, NT], F32, kind=spk).ap()
    o2p_sp = nc.dram_tensor("o2p_sp", [3072, NT], F32, kind=spk).ap()
    upq_sp = nc.dram_tensor("upq_sp", [3072, HID], BF16, kind=spk).ap()
    gateq_sp = nc.dram_tensor("gateq_sp", [3072, HID], BF16, kind="Internal").ap()
    downq_sp = nc.dram_tensor("downq_sp", [HID, 3072], BF16, kind="Internal").ap()
    cc_in = nc.dram_tensor("cc_in", [1, 8], F32, kind="Internal").ap()
    cc_out = nc.dram_tensor("cc_out", [1, 8], F32, kind="Internal").ap()

    dbg_sc = None
    if dbg:
        dbg_sc = nc.dram_tensor("dbg_sc", [1, 16], F32, kind="ExternalOutput").ap()

    rowb = E(nc.sbuf_tensor("rowb", [1, 4 * GW], F32))
    ones_col = E(nc.sbuf_tensor("ones_col", [128, 1], F32))
    ones_row = E(nc.sbuf_tensor("ones_row", [1, 128], F32))
    amax_xn = E(nc.sbuf_tensor("amax_xn", [128, 1], F32))
    amax_t = E(nc.sbuf_tensor("amax_t", [128, 8], F32))
    amaxw = E(nc.sbuf_tensor("amaxw", [128, 200], F32))
    tmpcol = E(nc.sbuf_tensor("tmpcol", [128, 1], F32))
    redrow = E(nc.sbuf_tensor("redrow", [1, 128], F32))
    scal = E(nc.sbuf_tensor("scal", [1, 16], F32))
    ccs = E(nc.sbuf_tensor("ccs", [1, 8], F32))
    sbc = E(nc.sbuf_tensor("sbc", [128, 12], F32))
    
    ev = [E(nc.sbuf_tensor(f"ev{i}", [128, GW], F32)) for i in range(2)]
    tq = [E(nc.sbuf_tensor(f"tq{i}", [128, GW], F32)) for i in range(6)]

    cstb = E(nc.sbuf_tensor("cstb", [128, 16], F32))
    ffn_es = ExitStack()
    wsc = E(nc.sbuf_tensor("wsc", [128, 5 * 32], F32))
    xg = [ffn_es.enter_context(nc.sbuf_tensor("xg0", [128, 8, GW], F32))]
    x2g = ffn_es.enter_context(nc.sbuf_tensor("x2g", [128, 8, GW], F32))
    xng = [ffn_es.enter_context(nc.sbuf_tensor("xng0", [128, 8, GW], F32))]
    wraw = ffn_es.enter_context(nc.sbuf_tensor("wraw", [128, HID], F32))
    wqt = ffn_es.enter_context(nc.sbuf_tensor("wqt", [128, HID], BF16))
    x1sb = ffn_es.enter_context(nc.sbuf_tensor("x1sb", [128, 8, NT], BF16))
    wsb = ffn_es.enter_context(nc.sbuf_tensor("wsb", [128, 8, 3072], BF16))
    o2q = ffn_es.enter_context(nc.sbuf_tensor("o2q", [128, 12, GW], BF16))
    ps = [E(nc.psum_tensor(f"ps{i}", [128, GW], F32)) for i in range(8)]

    semnames = ["sQ", "gQ", "pe", "ve", "sc", "cc"]
    sh = {}
    s = Sched()
    for n in semnames:
        sh[n] = E(nc.semaphore(n))
        s.sem(n)

    f32r = lambda ap: ap

    def gap(e):
        e.tensor_copy(tmpcol[:], cstb[:, 0:1])

    GRP = [(0, HALO, False)] + [(HALO + i * GW, GW, True) for i in range(NGRP)]

    c_cst = s.dma("gpsimd", cstb[:], cst_in)
    for i, v in enumerate(CONSTS):
        nc.const_aps.register(F32, float(v), cstb[:, i:i + 1])
    s.preamble = ("gQ", c_cst)
    c_ones = s.dma("gpsimd", ones_col[:], ones_in)
    c_onesr = s.dma("gpsimd", ones_row[:], onesr_in)
    s.emit("vector", lambda e: e.memset(amax_xn[:], 0.0), inc="ve")
    s.emit("vector", lambda e: e.memset(amaxw[:], 0.0), inc="ve")
    s.emit("vector", lambda e: e.memset(amax_t[:], 0.0), inc="ve")
    s.emit("vector", lambda e: e.memset(scal[:], 0.0), inc="ve")
    ve_init = s.cnt["ve"]
    lp_units = []
    for t in range(24):
        lp_units.append((rr(w_up)[:, t, :], rr(upq_sp)[:, t, :]))
    for t in range(24):
        lp_units.append((rr(w_gate)[:, t, :], rr(gateq_sp)[:, t, :]))
    for t in range(8):
        for cc3 in range(3):
            lp_units.append((rr(w_down)[:, t, cc3 * HID:(cc3 + 1) * HID],
                             rr(downq_sp)[:, t, cc3 * HID:(cc3 + 1) * HID]))
    lp_prev_v = 0
    lp_prev_sp = 0
    lp_prev_sc = 0
    for ui, (src_ap, dst_ap) in enumerate(lp_units):
        ldw = s.dma("sync", wraw[:], src_ap,
                    waits=[("ve", lp_prev_v)], inc="sQ")
        def lpq_a(e):
            e.tensor_reduce(wsc[:, 0:32], wraw[:].rearrange("p (b k) -> p b k", k=32),
                            axis=AX.X, op=ALU.max, apply_absolute_value=True)
            e.tensor_copy(tmpcol[:], cstb[:, 0:1])
            return e.tensor_scalar(wsc[:, 32:64], wsc[:, 0:32], 1.0 / 7.0, 1e-12,
                                   op0=ALU.mult, op1=ALU.max)
        lv_a = s.emit("vector", lpq_a, waits=[("sQ", ldw), ("gQ", lp_prev_sp),
                                              ("sc", lp_prev_sc)], inc="ve")
        lv_s = s.emit("scalar", lambda e: (
            e.activation(wsc[:, 96:128], wsc[:, 32:64], ACT.Ln),
            e.activation(wsc[:, 64:96], wsc[:, 96:128], ACT.Exp, scale=-1.0))[-1],
            waits=[("ve", lv_a)], inc="sc")
        lp_prev_sc = lv_s
        def lpq_b(e):
            e.tensor_mul(wsc[:, 96:128], wsc[:, 32:64], wsc[:, 64:96])
            gap(e)
            e.tensor_scalar(wsc[:, 96:128], wsc[:, 96:128], -1.0, 2.0,
                            op0=ALU.mult, op1=ALU.add)
            gap(e)
            e.tensor_mul(wsc[:, 128:160], wsc[:, 64:96], wsc[:, 96:128])
            gap(e)
            e.scalar_tensor_tensor(wraw[:].rearrange("p (b k) -> p b k", k=32),
                                   wraw[:].rearrange("p (b k) -> p b k", k=32), 1.0,
                                   wsc[:, 128:160].unsqueeze(2).broadcast_to([128, 32, 32]),
                                   op0=ALU.mult, op1=ALU.mult)
            e.tensor_scalar(wraw[:], wraw[:], MAGIC, None, op0=ALU.add)
            return e.scalar_tensor_tensor(
                wqt[:].rearrange("p (b k) -> p b k", k=32),
                wraw[:].rearrange("p (b k) -> p b k", k=32), -MAGIC,
                wsc[:, 32:64].unsqueeze(2).broadcast_to([128, 32, 32]),
                op0=ALU.add, op1=ALU.mult)
        lp_prev_v = s.emit("vector", lpq_b, waits=[("sc", lv_s)], inc="ve")
        lp_prev_sp = s.dma("gpsimd", dst_ap, wqt[:], waits=[("ve", lp_prev_v)], inc="gQ")
    lpq_done = lp_prev_sp

    g_upw = 0
    for kc in range(8):
        g_upw = s.xdma("sync", lambda e, kc=kc: e.dma_start_transpose(
            wsb[:, kc, :], upq_sp[:, kc * 128:(kc + 1) * 128]),
            waits=[("gQ", lpq_done)], inc="sQ")
    _dummy = s.dma("gpsimd", wsb[:, 0:1, 0:1], upq_sp[0:1, 0:1]) if False else None


    ln_xn = [0] * len(GRP)
    ln_spill = [0] * len(GRP)
    stats_pe = [0] * len(GRP)
    bc_pe = [0] * len(GRP)
    rows_ve = [0] * len(GRP)

    for g, (t0, tw, own) in enumerate(GRP):
        buf = 0
        w = [] if g < 1 else [("ve", ln_xn[g - 1])]
        ld = s.dma("sync", xg[buf][:, :, 0:tw], rr(x_fm)[:, :, t0:t0 + tw],
                   waits=w, inc="sQ")
        sqw = [("sQ", ld)] + ([("ve", ln_xn[g - 1])] if g >= 1 else [])
        sq = s.emit("scalar", lambda e, b=buf, tw=tw: e.activation(
            x2g[:, :, 0:tw], xg[b][:, :, 0:tw], ACT.Square), waits=sqw, inc="sc")
        def statmm(e, b=buf, tw=tw):
            for c in range(8):
                e.matmul(ps[4][0:1, 0:tw], f32r(ones_col[:]), f32r(xg[b][:, c, 0:tw]),
                         start=(c == 0), stop=(c == 7))
            for c in range(8):
                i = e.matmul(ps[5][0:1, 0:tw], f32r(ones_col[:]), f32r(x2g[:, c, 0:tw]),
                             start=(c == 0), stop=(c == 7))
            return i
        mmw = [("sc", sq)] + ([("ve", rows_ve[g - 1])] if g >= 1 else [])
        stats_pe[g] = s.emit("tensor", statmm, waits=mmw, inc="pe")
        def rows1(e, tw=tw):
            e.tensor_scalar(rowb[0:1, 0:tw], ps[4][0:1, 0:tw], 1.0 / HID, None, op0=ALU.mult)
            e.tensor_scalar(rowb[0:1, GW:GW + tw], ps[5][0:1, 0:tw], 1.0 / HID, None, op0=ALU.mult)
            gap(e)
            e.tensor_mul(rowb[0:1, 2 * GW:2 * GW + tw], rowb[0:1, 0:tw], rowb[0:1, 0:tw])
            gap(e)
            return e.tensor_sub(rowb[0:1, GW:GW + tw], rowb[0:1, GW:GW + tw],
                                rowb[0:1, 2 * GW:2 * GW + tw])
        v1 = s.emit("vector", rows1, waits=[("pe", stats_pe[g])], inc="ve")
        def rows2(e, tw=tw):
            e.activation(rowb[0:1, 2 * GW:2 * GW + tw], rowb[0:1, GW:GW + tw], ACT.Ln, bias=1e-5)
            return e.activation(rowb[0:1, 3 * GW:3 * GW + tw], rowb[0:1, 2 * GW:2 * GW + tw],
                                ACT.Exp, scale=-0.5)
        s1 = s.emit("scalar", rows2, waits=[("ve", v1)], inc="sc")
        rows_ve[g] = s.emit("vector", lambda e, tw=tw: e.scalar_tensor_tensor(
            rowb[0:1, 2 * GW:2 * GW + tw], rowb[0:1, 0:tw], -1.0,
            rowb[0:1, 3 * GW:3 * GW + tw], op0=ALU.mult, op1=ALU.mult),
            waits=[("sc", s1)], inc="ve")
        def bc(e, tw=tw):
            e.matmul(ps[6][:, 0:tw], f32r(ones_row[:]), f32r(rowb[0:1, 3 * GW:3 * GW + tw]),
                     start=True, stop=True)
            return e.matmul(ps[7][:, 0:tw], f32r(ones_row[:]), f32r(rowb[0:1, 2 * GW:2 * GW + tw]),
                            start=True, stop=True)
        bcw = [("ve", rows_ve[g])] + ([("ve", ln_xn[g - 1])] if g >= 1 else [])
        bc_pe[g] = s.emit("tensor", bc, waits=bcw, inc="pe")
        gi = g - 1
        def xnmk(e, b=buf, tw=tw, own=own, gi=gi):
            for c in range(8):
                e.tensor_mul(x2g[:, c, 0:tw], xg[b][:, c, 0:tw], ps[6][:, 0:tw])
                e.tensor_add(xng[b][:, c, 0:tw], x2g[:, c, 0:tw], ps[7][:, 0:tw])
            last = None
            if own:
                for c in range(8):
                    last = e.tensor_reduce(amaxw[:, gi * 8 + c:gi * 8 + c + 1],
                                           xng[b][:, c, 0:tw], axis=AX.X,
                                           op=ALU.max, apply_absolute_value=True)
            if last is None:
                last = e.tensor_copy(tmpcol[:], cstb[:, 0:1])
            return last
        xnw = [("pe", bc_pe[g]), ("ve", ve_init)]
        if g >= 1:
            xnw.append(("gQ", ln_spill[g - 1]))
        ln_xn[g] = s.emit("vector", xnmk, waits=xnw, inc="ve")
        ln_spill[g] = s.dma("gpsimd", rr(xn_sp)[:, :, t0:t0 + tw], xng[buf][:, :, 0:tw],
                            waits=[("ve", ln_xn[g])], inc="gQ")

    def ar_round(slot_vals, tag):
        vw = [s.cnt["ve"]]
        for slot, col in slot_vals:
            g1 = s.xdma("gpsimd", lambda e, c=col: e.dma_start(out=redrow[:], in_=c),
                        waits=[("ve", max(vw))])
            vv = s.emit("vector", lambda e, sl=slot: e.tensor_reduce(
                scal[0:1, sl:sl + 1], redrow[:], axis=AX.X, op=ALU.max),
                waits=[("gQ", g1)], inc="ve")
            vw.append(vv)
        gin = s.dma("gpsimd", cc_in, scal[0:1, 0:8], waits=[("ve", max(vw))])
        ccv = s.emit("gpsimd", lambda e: e.collective_compute(
            "AllReduce", ALU.max, replica_groups=[list(range(NCORE))],
            ins=[cc_in], outs=[cc_out]), waits=[("gQ", gin)], inc="cc")
        gout = s.dma("gpsimd", ccs[:], cc_out, waits=[("cc", ccv)])
        return gout

    fin_xn = s.emit("vector", lambda e: (
        e.tensor_copy(tmpcol[:], cstb[:, 0:1]),
        e.tensor_reduce(amax_xn[:], amaxw[:, 0:64], axis=AX.X, op=ALU.max),
        e.tensor_copy(tmpcol[:], cstb[:, 0:1]))[-1],
        waits=[("ve", ln_xn[-1])], inc="ve")
    g_ar1 = ar_round([(0, amax_xn[:, 0:1])], "xn")
    def mkscale(slot, bc0, bc1, wait_gq):
        sv = s.emit("scalar", lambda e, sl=slot: (
            e.activation(scal[0:1, 11:12], ccs[0:1, sl:sl + 1], ACT.Ln),
            e.activation(scal[0:1, 8:9], scal[0:1, 11:12], ACT.Exp, scale=-1.0))[-1],
            waits=[("gQ", wait_gq), ("pe", s.cnt["pe"]), ("ve", s.cnt["ve"])], inc="sc")
        def nrfix(e, sl=slot):
            e.tensor_mul(scal[0:1, 12:13], ccs[0:1, sl:sl + 1], scal[0:1, 8:9])
            gap(e)
            e.tensor_scalar(scal[0:1, 12:13], scal[0:1, 12:13], -1.0, 2.0,
                            op0=ALU.mult, op1=ALU.add)
            gap(e)
            e.tensor_mul(scal[0:1, 13:14], scal[0:1, 8:9], scal[0:1, 12:13])
            gap(e)
            return e.tensor_copy(scal[0:1, 8:9], scal[0:1, 13:14])
        vv = s.emit("vector", nrfix, waits=[("sc", sv)], inc="ve")
        def sc2(e, sl=slot):
            gap(e)
            e.tensor_scalar(scal[0:1, 9:10], scal[0:1, 8:9], QMAX, None, op0=ALU.mult)
            e.tensor_scalar(scal[0:1, 10:11], ccs[0:1, sl:sl + 1], 1.0 / QMAX, None,
                            op0=ALU.mult)
            return gap(e) or e.tensor_copy(tmpcol[:], cstb[:, 0:1])
        vv = s.emit("vector", sc2, waits=[("ve", vv)], inc="ve")
        mm = s.emit("tensor", lambda e: (
            e.matmul(ps[6][:, 0:1], f32r(ones_row[:]), f32r(scal[0:1, 9:10]), start=True, stop=True),
            e.matmul(ps[6][:, 1:2], f32r(ones_row[:]), f32r(scal[0:1, 10:11]), start=True, stop=True))[-1],
            waits=[("ve", vv)], inc="pe")
        return s.emit("vector", lambda e, b0=bc0, b1=bc1: (
            e.tensor_copy(sbc[:, b0:b0 + 1], ps[6][:, 0:1]),
            e.tensor_copy(sbc[:, b1:b1 + 1], ps[6][:, 1:2]))[-1],
            waits=[("pe", mm)], inc="ve")

    sx_ve = mkscale(0, 0, 1, g_ar1)

    x1_ve = [0] * NGRP
    for g in range(NGRP):
        buf = 0
        t0 = HALO + g * GW
        ld = s.dma("sync", xng[buf][:], rr(xn_sp)[:, :, t0:t0 + GW],
                   waits=[("ve", sx_ve if g < 1 else x1_ve[g - 1]),
                          ("gQ", ln_spill[-1])], inc="sQ")
        def x1mk(e, b=buf, g=g):
            for c in range(8):
                e.tensor_scalar(tq[0][:, :], xng[b][:, c, :], sbc[:, 0:1], MAGIC,
                                op0=ALU.mult, op1=ALU.add)
                i = e.tensor_scalar(x1sb[:, c, g * GW:(g + 1) * GW], tq[0][:, :],
                                    -MAGIC, sbc[:, 1:2], op0=ALU.add, op1=ALU.mult)
            return i
        x1_ve[g] = s.emit("vector", x1mk, waits=[("sQ", ld), ("ve", sx_ve)], inc="ve")

    def mm_3072(spill, amax_slot, wready, tag):
        tile_ev = {}
        evacof = {}
        pe_last = 0
        sp_last = 0
        idx = 0
        for oc in range(24):
            for g in range(NGRP):
                bank = idx % 4
                def mmfn(e, oc=oc, g=g, bank=bank):
                    for kc in range(8):
                        i = e.matmul(ps[bank][:, :], wsb[:, kc, oc * 128:(oc + 1) * 128],
                                     x1sb[:, kc, g * GW:(g + 1) * GW],
                                     start=(kc == 0), stop=(kc == 7))
                    return i
                mw = [("ve", x1_ve[g])] + wready
                if idx >= 2:
                    mw.append(tile_ev[idx - 2])
                if idx >= 4:
                    mw.append(evacof[idx - 4])
                pe_last = s.emit("tensor", mmfn, waits=mw, inc="pe")
                eng = "scalar" if (idx % 2 == 0) else "vector"
                eb = ev[idx % 2]
                if eng == "scalar":
                    ecnt = s.emit("scalar", lambda e, b=bank, eb=eb: e.copy(eb[:], ps[b][:, :]),
                                  waits=[("pe", pe_last)], inc="sc")
                    evacof[idx] = ("sc", ecnt)
                else:
                    ecnt = s.emit("vector", lambda e, b=bank, eb=eb: e.tensor_copy(eb[:], ps[b][:, :]),
                                  waits=[("pe", pe_last)], inc="ve")
                    evacof[idx] = ("ve", ecnt)
                am = s.emit("vector", lambda e, eb=eb, ix=idx: e.tensor_reduce(
                    amaxw[:, ix:ix + 1], eb[:], axis=AX.X, op=ALU.max,
                    apply_absolute_value=True), waits=[evacof[idx]], inc="ve")
                sp_last = s.dma("gpsimd", rr(spill)[:, oc, g * GW:(g + 1) * GW], eb[:],
                                waits=[("ve", am)], inc="gQ")
                tile_ev[idx] = ("gQ", sp_last)
                idx += 1
        fin = s.emit("vector", lambda e, sl=amax_slot: (
            e.tensor_copy(tmpcol[:], cstb[:, 0:1]),
            e.tensor_reduce(amax_t[:, sl:sl + 1], amaxw[:, 0:192], axis=AX.X, op=ALU.max),
            e.tensor_copy(tmpcol[:], cstb[:, 0:1]),
            e.memset(amaxw[:, 0:192], 0.0))[-1], inc="ve")
        return pe_last, sp_last

    up_pe, up_sp_done = mm_3072(up_sp, 0, [("sQ", g_upw)], "up")
    g_gw = 0
    for kc in range(8):
        g_gw = s.xdma("sync", lambda e, kc=kc: e.dma_start_transpose(
            wsb[:, kc, :], gateq_sp[:, kc * 128:(kc + 1) * 128]),
            waits=[("pe", up_pe), ("gQ", lpq_done)], inc="sQ")
    gate_pe, gate_sp_done = mm_3072(gate_sp, 1, [("sQ", g_gw)], "gate")
    wd = wsb[:].rearrange("p c o -> p (c o)").rearrange(
        "p (c o) -> p c o", o=HID)
    g_dnw = 0
    for kc in range(24):
        g_dnw = s.xdma("sync", lambda e, kc=kc: e.dma_start_transpose(
            wd[:, kc, :], downq_sp[:, kc * 128:(kc + 1) * 128]),
            waits=[("pe", gate_pe)], inc="sQ")

    g_ar2 = ar_round([(1, amax_t[:, 0:1]), (2, amax_t[:, 1:2])], "upgate")
    sup_ve = mkscale(1, 2, 3, g_ar2)
    sgate_ve = mkscale(2, 4, 5, g_ar2)

    pa_done = 0
    pa_v1_prev = 0
    for oc in range(24):
        for g in range(NGRP):
            ld = s.dma("sync", tq[0][:], rr(gate_sp)[:, oc, g * GW:(g + 1) * GW],
                       waits=[("ve", max(sup_ve, sgate_ve, pa_v1_prev)),
                              ("gQ", gate_sp_done)], inc="sQ")
            def pa_v1(e):
                e.tensor_scalar(tq[1][:], tq[0][:], sbc[:, 4:5], MAGIC, op0=ALU.mult, op1=ALU.add)
                return e.tensor_scalar(tq[2][:], tq[1][:], -MAGIC, sbc[:, 5:6],
                                       op0=ALU.add, op1=ALU.mult)
            v1 = s.emit("vector", pa_v1, waits=[("sQ", ld)], inc="ve")
            pa_v1_prev = v1
            sg = s.emit("scalar", lambda e: e.activation(tq[3][:], tq[2][:], ACT.Sigmoid),
                        waits=[("ve", v1)], inc="sc")
            def pa_v2(e, ix=oc * NGRP + g):
                e.tensor_scalar(tq[4][:], tq[3][:], 65536.0, MAGIC, op0=ALU.mult, op1=ALU.add)
                e.tensor_scalar(tq[1][:], tq[4][:], -MAGIC, 65535.0, op0=ALU.add, op1=ALU.min)
                e.scalar_tensor_tensor(tq[5][:], tq[1][:], 1.0 / 65536.0, tq[2][:],
                                       op0=ALU.mult, op1=ALU.mult)
                return e.tensor_reduce(amaxw[:, ix:ix + 1], tq[5][:], axis=AX.X,
                                       op=ALU.max, apply_absolute_value=True)
            v2 = s.emit("vector", pa_v2, waits=[("sc", sg), ("gQ", pa_done)], inc="ve")
            pa_done = s.dma("gpsimd", rr(g2p_sp)[:, oc, g * GW:(g + 1) * GW], tq[5][:],
                            waits=[("ve", v2)], inc="gQ")

    s.emit("vector", lambda e: (
        e.tensor_copy(tmpcol[:], cstb[:, 0:1]),
        e.tensor_reduce(amax_t[:, 2:3], amaxw[:, 0:192], axis=AX.X, op=ALU.max),
        e.tensor_copy(tmpcol[:], cstb[:, 0:1]),
        e.memset(amaxw[:, 0:192], 0.0))[-1], inc="ve")
    g_ar3 = ar_round([(3, amax_t[:, 2:3])], "g2")
    sg2_ve = mkscale(3, 6, 7, g_ar3)

    pb_done = 0
    pb_v_prev = 0
    for oc in range(24):
        for g in range(NGRP):
            ld1 = s.dma("sync", tq[0][:], rr(g2p_sp)[:, oc, g * GW:(g + 1) * GW],
                        waits=[("ve", max(sg2_ve, pb_v_prev)), ("gQ", pa_done)], inc="sQ")
            ld2 = s.dma("sync", tq[1][:], rr(up_sp)[:, oc, g * GW:(g + 1) * GW],
                        waits=[("gQ", pb_done)], inc="sQ")
            def pb_v(e, ix=oc * NGRP + g):
                e.tensor_scalar(tq[2][:], tq[0][:], sbc[:, 6:7], MAGIC, op0=ALU.mult, op1=ALU.add)
                e.tensor_scalar(tq[3][:], tq[2][:], -MAGIC, sbc[:, 7:8], op0=ALU.add, op1=ALU.mult)
                e.tensor_scalar(tq[4][:], tq[1][:], sbc[:, 2:3], MAGIC, op0=ALU.mult, op1=ALU.add)
                e.tensor_scalar(tq[5][:], tq[4][:], -MAGIC, sbc[:, 3:4], op0=ALU.add, op1=ALU.mult)
                e.tensor_mul(tq[1][:], tq[3][:], tq[5][:])
                return e.tensor_reduce(amaxw[:, ix:ix + 1], tq[1][:], axis=AX.X,
                                       op=ALU.max, apply_absolute_value=True)
            vb = s.emit("vector", pb_v, waits=[("sQ", ld2), ("gQ", pb_done)], inc="ve")
            pb_v_prev = vb
            pb_done = s.dma("gpsimd", rr(o2p_sp)[:, oc, g * GW:(g + 1) * GW], tq[1][:],
                            waits=[("ve", vb)], inc="gQ")

    s.emit("vector", lambda e: (
        e.tensor_copy(tmpcol[:], cstb[:, 0:1]),
        e.tensor_reduce(amax_t[:, 3:4], amaxw[:, 0:192], axis=AX.X, op=ALU.max),
        e.tensor_copy(tmpcol[:], cstb[:, 0:1]))[-1], inc="ve")
    g_ar4 = ar_round([(4, amax_t[:, 3:4])], "o2")
    so2_ve = mkscale(4, 8, 9, g_ar4)

    dn_last = 0
    out_done = 0
    for g in range(NGRP):
        half_pe = []
        for half in range(2):
            qv = 0
            for kk in range(12):
                kc = half * 12 + kk
                ld = s.dma("sync", tq[kc % 2][:], rr(o2p_sp)[:, kc, g * GW:(g + 1) * GW],
                           waits=[("ve", so2_ve), ("gQ", pb_done)] +
                                 ([("ve", qv)] if kk >= 2 else []) +
                                 ([("pe", dn_last)] if kk < 2 else []), inc="sQ")
                def qc(e, kc=kc, kk=kk):
                    e.tensor_scalar(tq[2 + kc % 2][:], tq[kc % 2][:], sbc[:, 8:9], MAGIC,
                                    op0=ALU.mult, op1=ALU.add)
                    return e.tensor_scalar(o2q[:, kk, :], tq[2 + kc % 2][:], -MAGIC,
                                           sbc[:, 9:10], op0=ALU.add, op1=ALU.mult)
                qv = s.emit("vector", qc, waits=[("sQ", ld)], inc="ve")
            for oc in range(8):
                bank = oc
                def dnmm(e, oc=oc, bank=bank, half=half):
                    for kk in range(12):
                        kc = half * 12 + kk
                        i = e.matmul(ps[bank][:, :], wd[:, kc, oc * 128:(oc + 1) * 128],
                                     o2q[:, kk, :], start=(kc == 0), stop=(kc == 23))
                    return i
                mw = [("ve", qv), ("sQ", g_dnw)]
                if half == 1 and oc >= 4:
                    mw.append(("gQ", out_done))
                dn_pe = s.emit("tensor", dnmm, waits=mw, inc="pe")
                dn_last = dn_pe
                half_pe.append(dn_pe)
        xb = xg[0]
        t0 = HALO + g * GW
        ldx = s.dma("sync", xb[:], rr(x_fm)[:, :, t0:t0 + GW], waits=[("ve", qv)], inc="sQ")
        for oc in range(8):
            bank = oc
            rv = s.emit("vector", lambda e, oc=oc, bank=bank, xb=xb: e.tensor_add(
                ev[oc % 2][:], ps[bank][:, :], xb[:, oc, :]),
                waits=[("pe", half_pe[8 + oc]), ("sQ", ldx)] +
                      ([("gQ", out_done)] if oc >= 2 else []), inc="ve")
            out_done = s.dma("gpsimd", rr(out_fm)[:, oc, g * GW:(g + 1) * GW], ev[oc % 2][:],
                             waits=[("ve", rv)], inc="gQ")

    if dbg:
        s.dma("gpsimd", dbg_sc[0:1, 0:12], sbc[0:1, 0:12],
              waits=[("ve", s.cnt["ve"])])

    ATTN(locals())

    s.build(nc, sh)
    return nc




def ATTN(L):
    nc = L["nc"]; s = L["s"]; E = L["E"]; ps = L["ps"]
    gap = L["gap"]; ones_col = L["ones_col"]; ones_row = L["ones_row"]
    xn_sp = L["xn_sp"]; out_fm = L["out_fm"]; x_fm = L["x_fm"]
    cstb = L["cstb"]; tmpcol = L["tmpcol"]; tq = L["tq"]; ev = L["ev"]
    ffn_es = L["ffn_es"]; dbg = L["dbg"]
    from contextlib import ExitStack

    DI = lambda n, sh_, dt_: nc.dram_tensor(n, sh_, dt_, kind="ExternalInput").ap()
    DN = lambda n, sh_, dt_: nc.dram_tensor(n, sh_, dt_, kind="Internal").ap()
    wqkvbT = DI("wqkvbT", [HID, 1540], F32)
    bands_in = DI("bands_in", [6, 512, 256, 128], F32)
    o_wT_in = DI("o_wT_in", [HID, HID], F32)
    g1xT_in = DI("g1xT_in", [HID, 2048], F32)
    g1sT_in = DI("g1sT_in", [64, 2048], F32)
    g1b_in = DI("g1b_in", [128, 16], F32)
    g2T_in = DI("g2T_in", [2048, 32], F32)
    g2b_in = DI("g2b_in", [32, 1], F32)
    vb_in = DI("vb_in", [32, 1], F32)
    ltmask_in = DI("ltmask_in", [4, 32], F32)
    fmask_in = DI("fmask_in", [4, 32], F32)
    ltvec_in = DI("ltvec_in", [4, 1], F32)
    flvec_in = DI("flvec_in", [4, 1], F32)
    onormc_in = DI("onormc_in", [128, 8], F32)
    m_bdtril_in = DI("m_bdtril_in", [128, 512], F32)
    m_ipack_in = DI("m_ipack_in", [128, 512], F32)
    m_sbu_in = DI("m_sbu_in", [128, 128], F32)
    m_triu_in = DI("m_triu_in", [128, 128], F32)
    ident_in = DI("ident_in", [128, 128], F32)
    mh_in = DI("mh_in", [128, 2], F32)

    xnb_sp = DN("xnb_sp", [HID, NT], BF16)
    xnb_ag = DN("xnb_ag", [2, HID, NT], BF16)
    qkv_sp = DN("qkv_sp", [1536, NTP], BF16)
    beta_sp = DN("beta_sp", [2, NTP], F32)
    beta16_sp = DN("beta16_sp", [16, NTP], BF16)
    qc_sp = DN("qc_sp", [512, NTP], BF16)
    kc_sp = DN("kc_sp", [512, NTP], BF16)
    fir4_sp = DN("fir4_sp", [4, 512, NTP], BF16)
    fir4_ag = DN("fir4_ag", [2, 4, 512, NTP], BF16)
    vc_sp = fir4_sp[3]
    khat_sp = DN("khat_sp", [2, 256, GW], BF16)
    vb_sp = DN("vb_sp", [2, 256, GW], BF16)
    do_sp = DN("do_sp", [2, 2, NT, 256], F32)
    do_ag = DN("do_ag", [2, 2, 2, NT, 256], F32)

    fence = [("gQ", s.cnt["gQ"]), ("sQ", s.cnt["sQ"]), ("ve", s.cnt["ve"]),
             ("sc", s.cnt["sc"]), ("pe", s.cnt["pe"])]
    for eng in ENGS:
        s.emit(eng, lambda e: None, waits=fence)

    xcast = s.dma("gpsimd", rr(xnb_sp), rr(xn_sp)[:, :, HALO:], waits=fence)
    agx = s.emit("gpsimd", lambda e: e.collective_compute(
        "AllGather", ALU.bypass, replica_groups=PAIRS,
        ins=[xnb_sp], outs=[xnb_ag]), waits=[("gQ", xcast)], inc="cc")

    ffn_es.close()
    ident = E(nc.sbuf_tensor("ident", [128, 128], BF16))
    identf = E(nc.sbuf_tensor("identf", [128, 128], F32))
    p12_es = ExitStack()
    E12 = p12_es.enter_context
    wp = E12(nc.sbuf_tensor("wp", [128, 8, 1540], BF16))
    xnb = E12(nc.sbuf_tensor("xnb", [128, 8, GW], BF16))
    evb = [E12(nc.sbuf_tensor(f"evb{i}", [128, GW], BF16)) for i in range(4)]
    pdat = [E12(nc.sbuf_tensor(f"pdat{i}", [128, 64 * 66], BF16)) for i in range(2)]
    zt16 = E12(nc.sbuf_tensor("zt16", [128, 256], BF16))
    zt32 = E12(nc.sbuf_tensor("zt32", [128, 256], F32))
    bvt = E12(nc.sbuf_tensor("bvt", [2, GW], F32))
    m_bdtril = E12(nc.sbuf_tensor("m_bdtril", [128, 512], F32))
    m_ipack = E12(nc.sbuf_tensor("m_ipack", [128, 512], BF16))
    m_sbu = E12(nc.sbuf_tensor("m_sbu", [128, 128], F32))
    m_triu = E12(nc.sbuf_tensor("m_triu", [128, 128], F32))

    bndj = [[E12(nc.sbuf_tensor(f"bnd{j}_{i}", [128, 2, 8, 128], BF16)) for i in range(2)]
            for j in range(3)]
    p3_es = ExitStack()
    E3 = p3_es.enter_context
    brow = L["rowb"]
    qf = E3(nc.sbuf_tensor("qf", [128, 2, GW], BF16))
    kf = E3(nc.sbuf_tensor("kf", [128, 2, GW], BF16))
    vf = E3(nc.sbuf_tensor("vf", [128, 2, GW], BF16))
    qh_ = E3(nc.sbuf_tensor("qh_", [128, 2, GW], BF16))
    kh_ = E3(nc.sbuf_tensor("kh_", [128, 2, GW], BF16))
    kbh = E3(nc.sbuf_tensor("kbh", [128, 2, GW], BF16))
    vbf = E3(nc.sbuf_tensor("vbf", [128, 2, GW], BF16))
    ktm = E3(nc.sbuf_tensor("ktm", [128, 4, 256], BF16))
    kbtm = E3(nc.sbuf_tensor("kbtm", [128, 4, 256], BF16))
    vbtm = E3(nc.sbuf_tensor("vbtm", [128, 4, 256], BF16))
    bcol = E3(nc.sbuf_tensor("bcol", [128, 4, 16], BF16))
    bcolf = E3(nc.sbuf_tensor("bcolf", [128, 4, 16], F32))
    sq2 = E3(nc.sbuf_tensor("sq2", [128, 2, GW], F32))
    apack = E3(nc.sbuf_tensor("apack", [128, 512], BF16))
    tpk = [E3(nc.sbuf_tensor(f"tpk{i}", [128, 512], BF16)) for i in range(4)]
    ppk = [E3(nc.sbuf_tensor(f"ppk{i}", [128, 512], BF16)) for i in range(5)]
    prt = [E3(nc.sbuf_tensor(f"prt{i}", [128, 512], BF16)) for i in range(2)]
    utm = E3(nc.sbuf_tensor("utm", [128, 4, 256], BF16))
    wtm = E3(nc.sbuf_tensor("wtm", [128, 4, 256], BF16))
    wnf = E3(nc.sbuf_tensor("wnf", [128, 2, GW], BF16))
    s0b = [E3(nc.sbuf_tensor(f"s0b{i}", [128, 512], BF16)) for i in range(2)]
    y0t = E3(nc.sbuf_tensor("y0t", [128, 256], BF16))
    t1t = E3(nc.sbuf_tensor("t1t", [128, 256], BF16))
    t2t = E3(nc.sbuf_tensor("t2t", [128, 256], BF16))
    uadj = E3(nc.sbuf_tensor("uadj", [128, 256], BF16))
    nnt = E3(nc.sbuf_tensor("nnt", [128, 128], BF16))
    attt = E3(nc.sbuf_tensor("attt", [128, 128], BF16))
    oev = E3(nc.sbuf_tensor("oev", [128, 256], F32))


    zz = s.emit("vector", lambda e: (e.memset(zt16[:], 0.0),
                                     e.memset(zt32[:], 0.0))[-1], inc="ve")
    cl = 0
    for ap_s, ap_d in [(m_bdtril_in, m_bdtril[:]), (m_ipack_in, m_ipack[:]),
                       (m_sbu_in, m_sbu[:]), (m_triu_in, m_triu[:]),
                       (ident_in, ident[:]), (ident_in, identf[:])]:
        cl = s.dma("gpsimd", ap_d, ap_s, waits=fence)
    wpl = s.dma("gpsimd", wp[:], rr(wqkvbT), waits=fence)

    zp = 0
    for oc in range(12):
        zp = s.dma("gpsimd", rr(qkv_sp)[:, oc, 0:PADL], zt16[:, 0:PADL], waits=[("ve", zz)])
        zp = s.dma("gpsimd", rr(qkv_sp)[:, oc, NTP - PADR:NTP], zt16[:, 0:PADR],
                   waits=[("ve", zz)])
    zp = s.dma("gpsimd", beta_sp[0:2, 0:PADL], zt32[0:2, 0:PADL],
               waits=[("ve", zz)])
    zp = s.dma("gpsimd", beta_sp[0:2, NTP - PADR:NTP], zt32[0:2, 0:PADR],
               waits=[("ve", zz)])
    for c0 in range(NTP // 256):
        zp = s.dma("gpsimd", beta16_sp[:, c0 * 256:c0 * 256 + 256],
                   zt16[0:16, 0:256], waits=[("ve", zz)])

    p1sp = zp
    p1ev = {}
    for g in range(16):
        sl, off = g // 8, (g % 8) * GW
        ldr = s.dma("sync", xnb[:], rr(xnb_ag[sl])[:, :, off:off + GW],
                    waits=[("cc", agx), ("ve", p1ev.get(g - 1, 0))], inc="sQ")
        last = 0
        for oc in range(13):
            bank = oc % 2
            mwid = 128 if oc < 12 else 2
            def p1mm(e, oc=oc, bank=bank, mwid=mwid):
                for kc in range(8):
                    i = e.matmul(ps[bank][0:mwid, :], wp[:, kc, oc * 128:oc * 128 + mwid],
                                 xnb[:, kc, :], start=(kc == 0), stop=(kc == 7))
                return i
            pw = [("sQ", ldr), ("gQ", wpl), ("gQ", p1sp)]
            mm = s.emit("tensor", p1mm, waits=pw, inc="pe")
            tcol = PADL + g * GW
            if oc < 12:
                vv = s.emit("vector", lambda e, b=bank, oc=oc: e.tensor_copy(
                    evb[oc % 4][:], ps[b][:, :]), waits=[("pe", mm)], inc="ve")
                last = vv
                p1sp = s.dma("gpsimd", rr(qkv_sp)[:, oc, tcol:tcol + GW],
                             evb[oc % 4][:], waits=[("ve", vv)], inc="gQ")
            else:
                sg = s.emit("scalar", lambda e, b=bank: e.activation(
                    bvt[0:2, :], ps[b][0:2, :], ACT.Sigmoid),
                    waits=[("pe", mm)], inc="sc")
                p1sp = s.dma("gpsimd", beta_sp[0:2, tcol:tcol + GW], bvt[0:2, :],
                             waits=[("sc", sg)], inc="gQ")
                p1sp = s.dma("gpsimd", beta16_sp[0:2, tcol:tcol + GW], bvt[0:2, :],
                             waits=[("sc", sg)], inc="gQ")
        p1ev[g] = last
    p1_done = p1sp

    NW2 = NTP // 128

    def conv_stream(src_sp, jobs, wait_gq, tag):
        done = 0
        pdat_guard = {}
        bank_guard = {}
        spill_last = 0
        evb_guard = {}
        for cg in range(8):
            buf = cg % 2
            trw = [("gQ", wait_gq), ("pe", pdat_guard.get(cg - 2, 0)),
                   ("gQ", spill_last)]
            tr = s.xdma("sync", lambda e, cg=cg, buf=buf, src=src_sp: e.dma_start_transpose(
                pdat[buf][:], src[cg * 64:cg * 64 + 64, :].rearrange(
                    "c (w x) -> (c w) x", x=128)), waits=trw, inc="sQ")
            for c8 in range(8):
                bpar = (cg * 8 + c8) % 2
                bls = []
                for j, (fid, dst, sil) in enumerate(jobs):
                    for hh2 in range(2):
                        bl = s.dma("gpsimd", bndj[j][bpar][:, hh2],
                                   bands_in[fid, cg * 64 + c8 * 8:cg * 64 + c8 * 8 + 8,
                                            hh2 * 128:hh2 * 128 + 128, :]
                                   .rearrange("c p t -> p c t"),
                                   waits=[("pe", bank_guard.get(("b", cg, c8 - 2, j), 0))],
                                   inc="gQ")
                    bank_guard[("b", cg, c8, j)] = 0
                    bls.append(bl)
                mmlast = 0
                for ci in range(8):
                    c = c8 * 8 + ci
                    slot = ci % 4
                    for j, (fid, dst, sil) in enumerate(jobs):
                        bank = 2 + j
                        def cmm(e, buf=buf, c=c, bank=bank, slot=slot, j=j, bpar=bpar,
                                ci=ci):
                            base = c * NW2
                            e.matmul(ps[bank][0:64, slot * 128:slot * 128 + 128],
                                     pdat[buf][:, base:base + 64],
                                     bndj[j][bpar][:, 0, ci, :], start=True, stop=False)
                            return e.matmul(ps[bank][0:64, slot * 128:slot * 128 + 128],
                                            pdat[buf][:, base + 1:base + 65],
                                            bndj[j][bpar][:, 1, ci, :],
                                            start=False, stop=True)
                        mw = [("sQ", tr), ("gQ", bls[j]),
                              bank_guard.get((bank, slot), ("pe", 0))]
                        mm = s.emit("tensor", cmm, waits=mw, inc="pe")
                        mmlast = mm
                        if ci == 7:
                            ew = [("pe", mm), evb_guard.get(j, ("pe", 0))]
                            if sil:
                                ecnt = s.emit("scalar", lambda e, bank=bank, j=j: e.activation(
                                    evb[j][0:64, :], ps[bank][0:64, :], ACT.Silu),
                                    waits=ew, inc="sc")
                                ekey = ("sc", ecnt)
                            else:
                                ecnt = s.emit("vector", lambda e, bank=bank, j=j: e.tensor_copy(
                                    evb[j][0:64, :], ps[bank][0:64, :]),
                                    waits=ew, inc="ve")
                                ekey = ("ve", ecnt)
                            c0 = cg * 64 + c8 * 8 + 4
                            spill_last = s.dma(
                                "gpsimd",
                                dst[c0:c0 + 4, PADL:PADL + 8192].rearrange(
                                    "c (w t) -> w c t", t=128),
                                evb[j][0:64, :].rearrange("w (c t) -> w c t", t=128),
                                waits=[ekey], inc="gQ")
                            for sl_ in range(4):
                                bank_guard[(bank, sl_)] = ("gQ", spill_last)
                            evb_guard[j] = ("gQ", spill_last)
                        elif ci == 3:
                            ew = [("pe", mm), evb_guard.get(j, ("pe", 0))]
                            if sil:
                                ecnt = s.emit("scalar", lambda e, bank=bank, j=j: e.activation(
                                    evb[j][0:64, :], ps[bank][0:64, :], ACT.Silu),
                                    waits=ew, inc="sc")
                                ekey = ("sc", ecnt)
                            else:
                                ecnt = s.emit("vector", lambda e, bank=bank, j=j: e.tensor_copy(
                                    evb[j][0:64, :], ps[bank][0:64, :]),
                                    waits=ew, inc="ve")
                                ekey = ("ve", ecnt)
                            c0 = cg * 64 + c8 * 8
                            spill_last = s.dma(
                                "gpsimd",
                                dst[c0:c0 + 4, PADL:PADL + 8192].rearrange(
                                    "c (w t) -> w c t", t=128),
                                evb[j][0:64, :].rearrange("w (c t) -> w c t", t=128),
                                waits=[ekey], inc="gQ")
                            for sl_ in range(4):
                                bank_guard[(bank, sl_)] = ("gQ", spill_last)
                            evb_guard[j] = ("gQ", spill_last)
                pdat_guard[cg] = mmlast
        return spill_last

    zp2 = 0
    for dst in (qc_sp, kc_sp, fir4_sp[0], fir4_sp[1], fir4_sp[2], fir4_sp[3]):
        for col0, wdt in ((0, PADL), (NTP - PADR, PADR)):
            for cc4 in range(4):
                zp2 = s.dma("gpsimd", rr(dst)[:, cc4, col0:col0 + wdt],
                            zt16[:, 0:wdt], waits=[("ve", zz)])

    qcd = conv_stream(qkv_sp[0:512], [(0, qc_sp, True)], p1_done, "q")
    kcd = conv_stream(qkv_sp[512:1024], [(1, kc_sp, True)], p1_done, "k")
    vcd = conv_stream(qkv_sp[1024:1536], [(2, vc_sp, True)], p1_done, "v")
    fird = conv_stream(vc_sp, [(3, fir4_sp[0], False), (4, fir4_sp[1], False),
                               (5, fir4_sp[2], False)], max(vcd, zp2), "fir")

    agf = s.emit("gpsimd", lambda e: e.collective_compute(
        "AllGather", ALU.bypass, replica_groups=PAIRS,
        ins=[fir4_sp], outs=[fir4_ag]), waits=[("gQ", fird)], inc="cc")

    NSC = 4
    vsent = {}
    dospill = 0
    for h in range(2):
        sinit = s.emit("vector", lambda e, h=h: e.memset(s0b[h][:], 0.0), inc="ve")
        for g in range(16):
            ts0 = PADL + g * GW
            gpar = g % 2
            w_lds = [("gQ", fird), ("pe", s.cnt["pe"]), ("ve", s.cnt["ve"])]
            l1 = s.dma("sync", qf[:], rr(qc_sp[256 * h:256 * h + 256])[:, :, ts0:ts0 + GW],
                       waits=w_lds, inc="sQ")
            l2 = s.dma("sync", kf[:], rr(kc_sp[256 * h:256 * h + 256])[:, :, ts0:ts0 + GW],
                       inc="sQ")
            l3 = s.dma("sync", vf[:], rr(vc_sp[256 * h:256 * h + 256])[:, :, ts0:ts0 + GW],
                       inc="sQ")
            l4 = s.dma("sync", brow[0:1, 0:GW], beta_sp[h:h + 1, ts0:ts0 + GW], inc="sQ")
            sqs = s.emit("scalar", lambda e: e.activation(sq2[:], qf[:], ACT.Square),
                         waits=[("sQ", l1)], inc="sc")
            def ssmm(e):
                for c2 in range(2):
                    i = e.matmul(ps[6][0:1, 0:GW], ones_col[:], sq2[:, c2, :],
                                 start=(c2 == 0), stop=(c2 == 1))
                return i
            mq = s.emit("tensor", ssmm, waits=[("sc", sqs)], inc="pe")
            rq = s.emit("scalar", lambda e: (
                e.activation(brow[0:1, GW:2 * GW], ps[6][0:1, 0:GW], ACT.Ln, bias=1e-6),
                e.activation(brow[0:1, 2 * GW:3 * GW], brow[0:1, GW:2 * GW], ACT.Exp,
                             scale=-0.5))[-1], waits=[("pe", mq)], inc="sc")
            sqs2 = s.emit("scalar", lambda e: e.activation(sq2[:], kf[:], ACT.Square),
                          waits=[("sQ", l2), ("sc", rq)], inc="sc")
            mk = s.emit("tensor", ssmm, waits=[("sc", sqs2), ("sc", rq)], inc="pe")
            rk = s.emit("scalar", lambda e: (
                e.activation(brow[0:1, GW:2 * GW], ps[6][0:1, 0:GW], ACT.Ln, bias=1e-6),
                e.activation(brow[0:1, 3 * GW:4 * GW], brow[0:1, GW:2 * GW], ACT.Exp,
                             scale=-0.5))[-1], waits=[("pe", mk)], inc="sc")
            def bcmm(e):
                e.matmul(ps[6][:, 0:GW], ones_row[:].bitcast(F32), brow[0:1, 2 * GW:3 * GW],
                         start=True, stop=True)
                e.matmul(ps[7][:, 0:GW], ones_row[:].bitcast(F32), brow[0:1, 3 * GW:4 * GW],
                         start=True, stop=True)
                return e.matmul(ps[5][:, 0:GW], ones_row[:].bitcast(F32), brow[0:1, 0:GW],
                                start=True, stop=True)
            bc = s.emit("tensor", bcmm, waits=[("sc", rk)], inc="pe")
            def scl(e):
                for c2 in range(2):
                    e.tensor_mul(qh_[:, c2, :], qf[:, c2, :], ps[6][:, :])
                    e.tensor_mul(kh_[:, c2, :], kf[:, c2, :], ps[7][:, :])
                for c2 in range(2):
                    e.tensor_mul(kbh[:, c2, :], kh_[:, c2, :], ps[5][:, :])
                    i = e.tensor_mul(vbf[:, c2, :], vf[:, c2, :], ps[5][:, :])
                return i
            sc1 = s.emit("vector", scl, waits=[("pe", bc), ("sQ", l3), ("sQ", l4)], inc="ve")
            sp1 = s.dma("gpsimd", rr(khat_sp[gpar]), kh_[:], waits=[("ve", sc1)], inc="gQ")
            sp2 = s.dma("gpsimd", rr(vb_sp[gpar]), vbf[:], waits=[("ve", sc1)], inc="gQ")
            trs = []
            for tt in range(4):
                trs.append(s.xdma("sync", lambda e, tt=tt, gpar=gpar: e.dma_start_transpose(
                    ktm[:, tt, :], khat_sp[gpar][:, tt * 128:tt * 128 + 128]),
                    waits=[("gQ", sp2)], inc="sQ"))
                trs.append(s.xdma("sync", lambda e, tt=tt, gpar=gpar: e.dma_start_transpose(
                    vbtm[:, tt, :], vb_sp[gpar][:, tt * 128:tt * 128 + 128]),
                    waits=[("gQ", sp2)], inc="sQ"))
                trs.append(s.xdma("sync", lambda e, tt=tt, ts0=ts0: e.dma_start_transpose(
                    bcol[:, tt, :], beta16_sp[:, ts0 + tt * 128:ts0 + tt * 128 + 128]),
                    waits=[("gQ", p1_done)], inc="sQ"))
            trd = trs[-1]
            def kbtf(e, h=h):
                e.tensor_copy(bcolf[:].rearrange("p a c -> p (a c)"),
                              bcol[:].rearrange("p a c -> p (a c)"))
                gap(e)
                for tt in range(4):
                    r = e.tensor_scalar(kbtm[:, tt, :], ktm[:, tt, :],
                                        bcolf[:, tt, h:h + 1], None, op0=ALU.mult)
                return r
            kbt = s.emit("vector", kbtf, waits=[("sQ", trd)], inc="ve")
            def amm(e):
                for i in range(16):
                    pk, gb = i // 4, i % 4
                    for c2 in range(2):
                        r = e.matmul(ps[2][32 * gb:32 * gb + 32,
                                           pk * 128 + 32 * gb:pk * 128 + 32 * gb + 32],
                                     kbh[:, c2, 32 * i:32 * i + 32],
                                     kh_[:, c2, 32 * i:32 * i + 32],
                                     start=(c2 == 0), stop=(c2 == 1),
                                     tile_position=(0, 32 * gb))
                return r
            am = s.emit("tensor", amm, waits=[("ve", kbt), ("ve", sc1)], inc="pe")
            av = s.emit("vector", lambda e: e.tensor_mul(apack[:], ps[2][:, :], m_bdtril[:]),
                        waits=[("pe", am)], inc="ve")
            tv = s.emit("vector", lambda e: e.transpose(tpk[0][:], apack[:]),
                        waits=[("ve", av)], inc="ve")
            chain_v = tv
            for lvl in range(1, 5):
                pprev = apack if lvl == 1 else ppk[lvl - 1]
                tprev = tpk[lvl - 1]
                def sqmm(e, pprev=pprev, tprev=tprev, lvl=lvl):
                    r = None
                    for pk in range(4):
                        sl = slice(pk * 128, pk * 128 + 128)
                        r = e.matmul(ps[2][:, sl], tprev[:, sl], pprev[:, sl],
                                     start=True, stop=True)
                        if lvl < 4:
                            r = e.matmul(ps[3][:, sl], pprev[:, sl], tprev[:, sl],
                                         start=True, stop=True)
                    return r
                sq_ = s.emit("tensor", sqmm, waits=[("ve", chain_v)], inc="pe")
                def sqev(e, lvl=lvl):
                    e.tensor_copy(ppk[lvl][:], ps[2][:, :])
                    if lvl < 4:
                        return e.tensor_copy(tpk[lvl][:], ps[3][:, :])
                    return gap(e) or e.tensor_copy(tmpcol[:], cstb[:, 0:1])
                chain_v = s.emit("vector", sqev, waits=[("pe", sq_)], inc="ve")
            pv = s.emit("vector", lambda e: e.tensor_sub(prt[0][:], m_ipack[:], tpk[0][:]),
                        waits=[("ve", chain_v)], inc="ve")
            cur = 0
            for lvl in range(1, 5):
                def prmm(e, lvl=lvl, cur=cur):
                    for pk in range(4):
                        sl = slice(pk * 128, pk * 128 + 128)
                        e.matmul(ps[2][:, sl], ppk[lvl][:, sl], prt[cur][:, sl],
                                 start=True, stop=False)
                        r = e.matmul(ps[2][:, sl], ident[:], prt[cur][:, sl],
                                     start=False, stop=True)
                    return r
                pm = s.emit("tensor", prmm, waits=[("ve", pv)], inc="pe")
                pv = s.emit("vector", lambda e, cur=cur: e.tensor_copy(
                    prt[1 - cur][:], ps[2][:, :]), waits=[("pe", pm)], inc="ve")
                cur = 1 - cur
            def uwmm(e, cur=cur):
                for i in range(16):
                    pk, gb = i // 4, i % 4
                    lh = prt[cur][32 * gb:32 * gb + 32,
                                  pk * 128 + 32 * gb:pk * 128 + 32 * gb + 32]
                    e.matmul(ps[2][32 * gb:32 * gb + 32, (pk % 2) * 256:(pk % 2) * 256 + 256],
                             lh, vbtm[32 * gb:32 * gb + 32, pk, :],
                             start=True, stop=True, tile_position=(32 * gb, 32 * gb))
                    r = e.matmul(ps[3][32 * gb:32 * gb + 32, (pk % 2) * 256:(pk % 2) * 256 + 256],
                                 lh, kbtm[32 * gb:32 * gb + 32, pk, :],
                                 start=True, stop=True, tile_position=(32 * gb, 32 * gb))
                    if pk == 1 or pk == 3:
                        pass
                return r
            uws = []
            for half in range(2):
                def uwmm2(e, cur=cur, half=half):
                    for i in range(8 * half, 8 * half + 8):
                        pk, gb = i // 4, i % 4
                        lh = prt[cur][32 * gb:32 * gb + 32,
                                      pk * 128 + 32 * gb:pk * 128 + 32 * gb + 32]
                        e.matmul(ps[2][32 * gb:32 * gb + 32,
                                       (pk % 2) * 256:(pk % 2) * 256 + 256],
                                 lh, vbtm[32 * gb:32 * gb + 32, pk, :],
                                 start=True, stop=True, tile_position=(32 * gb, 32 * gb))
                        r = e.matmul(ps[3][32 * gb:32 * gb + 32,
                                           (pk % 2) * 256:(pk % 2) * 256 + 256],
                                 lh, kbtm[32 * gb:32 * gb + 32, pk, :],
                                 start=True, stop=True, tile_position=(32 * gb, 32 * gb))
                    return r
                um = s.emit("tensor", uwmm2, waits=[("ve", pv), ("ve", kbt)] +
                            ([("ve", uws[-1])] if uws else []), inc="pe")
                uv = s.emit("vector", lambda e, half=half: (
                    e.tensor_copy(utm[:].rearrange("p a d -> p (a d)")
                                  [:, half * 512:half * 512 + 512], ps[2][:, :]),
                    e.tensor_copy(wtm[:].rearrange("p a d -> p (a d)")
                                  [:, half * 512:half * 512 + 512], ps[3][:, :]))[-1],
                    waits=[("pe", um)], inc="ve")
                uws.append(uv)
            def wtr(e):
                for dkc in range(2):
                    for tt in range(4):
                        r = e.matmul(ps[4][:, tt * 128:tt * 128 + 128],
                                     wtm[:, tt, dkc * 128:dkc * 128 + 128], ident[:],
                                     start=True, stop=True)
                return r
            wv = uws[-1]
            for dkc in range(2):
                def wtr2(e, dkc=dkc):
                    for tt in range(4):
                        r = e.matmul(ps[4][:, tt * 128:tt * 128 + 128],
                                     wtm[:, tt, dkc * 128:dkc * 128 + 128], ident[:],
                                     start=True, stop=True)
                    return r
                wm = s.emit("tensor", wtr2, waits=[("ve", wv)], inc="pe")
                wv = s.emit("vector", lambda e, dkc=dkc: e.tensor_scalar(
                    wnf[:, dkc, :], ps[4][:, :], -1.0, None, op0=ALU.mult),
                    waits=[("pe", wm)], inc="ve")
            prep_done = wv
            for cc in range(4):
                csl = slice(128 * cc, 128 * cc + 128)
                def y0mm(e, cc=cc, h=h, csl=csl):
                    e.matmul(ps[2][:, 0:256], ident[:], utm[:, cc, :], start=True, stop=False)
                    for dkc in range(2):
                        r = e.matmul(ps[2][:, 0:256], wnf[:, dkc, csl],
                                     s0b[h][:, dkc * 256:dkc * 256 + 256],
                                     start=False, stop=(dkc == 1))
                    return r
                m1 = s.emit("tensor", y0mm, waits=[("ve", prep_done),
                                                   ("ve", s.cnt["ve"])], inc="pe")
                v1 = s.emit("vector", lambda e: e.tensor_copy(y0t[:], ps[2][:, 0:256]),
                            waits=[("pe", m1)], inc="ve")
                def ntmm(e, csl=csl):
                    for c2 in range(2):
                        r = e.matmul(ps[3][:, 0:128], kh_[:, c2, csl], wnf[:, c2, csl],
                                     start=(c2 == 0), stop=(c2 == 1))
                    return r
                m2 = s.emit("tensor", ntmm, waits=[("ve", v1)], inc="pe")
                v2 = s.emit("vector", lambda e: e.tensor_mul(nnt[:], ps[3][:, 0:128], m_sbu[:]),
                            waits=[("pe", m2)], inc="ve")
                def neu(e, src, dst):
                    def f(e2):
                        e2.matmul(ps[2][:, 0:256], ident[:], y0t[:], start=True, stop=False)
                        return e2.matmul(ps[2][:, 0:256], nnt[:], src[:], start=False, stop=True)
                    return f
                v_ = v2
                for (srcb, dstb) in ((y0t, t1t), (t1t, t2t), (t2t, uadj)):
                    mn = s.emit("tensor", lambda e, srcb=srcb: (
                        e.matmul(ps[2][:, 0:256], ident[:], y0t[:], start=True, stop=False),
                        e.matmul(ps[2][:, 0:256], nnt[:], srcb[:], start=False, stop=True))[-1],
                        waits=[("ve", v_)], inc="pe")
                    v_ = s.emit("vector", lambda e, dstb=dstb: e.tensor_copy(
                        dstb[:], ps[2][:, 0:256]), waits=[("pe", mn)], inc="ve")
                def atmm(e, csl=csl):
                    for c2 in range(2):
                        r = e.matmul(ps[3][:, 128:256], kh_[:, c2, csl], qh_[:, c2, csl],
                                     start=(c2 == 0), stop=(c2 == 1))
                    return r
                m3 = s.emit("tensor", atmm, waits=[("ve", v_)], inc="pe")
                v3 = s.emit("vector", lambda e: e.tensor_mul(attt[:], ps[3][:, 128:256],
                                                             m_triu[:]),
                            waits=[("pe", m3)], inc="ve")
                def omm(e, h=h, csl=csl):
                    for dkc in range(2):
                        e.matmul(ps[4][:, 0:256], qh_[:, dkc, csl],
                                 s0b[h][:, dkc * 256:dkc * 256 + 256],
                                 start=(dkc == 0), stop=False)
                    return e.matmul(ps[4][:, 0:256], attt[:], uadj[:], start=False, stop=True)
                m4 = s.emit("tensor", omm, waits=[("ve", v3)], inc="pe")
                v4 = s.emit("vector", lambda e: e.tensor_copy(oev[:], ps[4][:, 0:256]),
                            waits=[("pe", m4)], inc="ve")
                t_in_half = (g % 8) * GW + cc * 128
                dospill = s.dma("gpsimd", do_sp[g // 8, h, t_in_half:t_in_half + 128, :],
                                oev[:], waits=[("ve", v4)], inc="gQ")
                def smm(e, cc=cc, h=h, g=g):
                    for dkc in range(2):
                        r = e.matmul(ps[h][:, dkc * 256:dkc * 256 + 256],
                                     ktm[:, cc, dkc * 128:dkc * 128 + 128], uadj[:],
                                     start=(g == 0 and cc == 0 and dkc == 0),
                                     stop=(g == 15 and cc == 3 and dkc == 1),
                                     skip_group_check=True)
                    return r
                m5 = s.emit("tensor", smm, waits=[("ve", v4), ("ve", sinit)], inc="pe")
                sref = s.emit("vector", lambda e, h=h: e.tensor_copy(s0b[h][:], ps[h][:, :]),
                              waits=[("pe", m5)], inc="ve")

    a2f = agf
    a2d = s.emit("gpsimd", lambda e: e.collective_compute(
        "AllGather", ALU.bypass, replica_groups=PAIRS,
        ins=[do_sp], outs=[do_ag]), waits=[("gQ", dospill)], inc="cc")

    p3_es.close()
    p12_es.close()
    g1xw = E(nc.sbuf_tensor("g1xw", [128, 8, 2048], BF16))
    g1sw = E(nc.sbuf_tensor("g1sw", [64, 16, 128], BF16))
    g2w = E(nc.sbuf_tensor("g2w", [128, 16, 32], BF16))
    g1bc = E(nc.sbuf_tensor("g1bc", [128, 16], F32))
    owb = E(nc.sbuf_tensor("owb", [128, 8, HID], BF16))
    owf = E(nc.sbuf_tensor("owf", [128, HID], F32))
    onc = E(nc.sbuf_tensor("onc", [128, 8], F32))
    btile = [E(nc.sbuf_tensor(f"btile{i}", [128, 4, HID], BF16)) for i in range(4)]
    dtile = E(nc.sbuf_tensor("dtile", [128, 4, HID], F32))
    oac2 = E(nc.sbuf_tensor("oac2", [128, HID], F32))
    bt_a = E(nc.sbuf_tensor("bt_a", [128, HID], BF16))
    bt_b = E(nc.sbuf_tensor("bt_b", [128, HID], BF16))
    dt_a = E(nc.sbuf_tensor("dt_a", [128, HID], F32))
    mhc = E(nc.sbuf_tensor("mhc", [128, 2], F32))
    x2t = E(nc.sbuf_tensor("x2t", [128, HID], F32))
    statc = E(nc.sbuf_tensor("statc", [128, 64], F32))
    ss20 = E(nc.sbuf_tensor("ss20", [128, 24], F32))
    statf = E(nc.sbuf_tensor("statf", [64, GW], BF16))
    hdnb = E(nc.sbuf_tensor("hdnb", [128, 16, GW], BF16))
    lgf = E(nc.sbuf_tensor("lgf", [32, GW], F32))
    mrow = E(nc.sbuf_tensor("mrow", [4, 64], F32))
    ltm = E(nc.sbuf_tensor("ltm", [128, 4, 32], F32))
    w20 = E(nc.sbuf_tensor("w20", [128, 4, 32], F32))
    gsc = E(nc.sbuf_tensor("gsc", [32, 8], F32))
    fexp = E(nc.sbuf_tensor("fexp", [128, 32], F32))
    sexp = E(nc.sbuf_tensor("sexp", [128, 32], F32))
    oacc = E(nc.sbuf_tensor("oacc", [128, HID], F32))
    ofm = E(nc.sbuf_tensor("ofm", [128, 8, GW], BF16))
    xnbb = E(nc.sbuf_tensor("xnbb", [128, 8, GW], BF16))
    brw = E(nc.sbuf_tensor("brw", [1, 64], F32))

    stz0 = s.emit("vector", lambda e: e.memset(statc[:], 0.0),
                  waits=[("sc", s.cnt["sc"]), ("pe", s.cnt["pe"]),
                         ("gQ", s.cnt["gQ"]), ("sQ", s.cnt["sQ"])], inc="ve")

    gl0 = s.dma("gpsimd", mhc[:], mh_in)
    gl = s.dma("gpsimd", g1xw[:], rr(g1xT_in))
    gl = s.dma("gpsimd", g1sw[:], g1sT_in.rearrange("r (c t) -> r c t", t=128))
    gl = s.dma("gpsimd", g2w[:], rr(g2T_in))
    gl = s.dma("gpsimd", g1bc[:], g1b_in)
    gl = s.dma("gpsimd", onc[:], onormc_in)
    mk1 = s.dma("sync", mrow[0:4, 0:32], ltmask_in, inc="sQ")
    mk1 = s.dma("sync", mrow[0:4, 32:64], fmask_in, inc="sQ")
    lt1 = s.dma("sync", gsc[0:4, 0:1], ltvec_in, inc="sQ")
    lt1 = s.dma("sync", gsc[0:4, 1:2], flvec_in, inc="sQ")
    lt1 = s.dma("sync", gsc[0:32, 2:3], g2b_in, inc="sQ")
    lt1 = s.dma("sync", gsc[0:32, 7:8], vb_in, inc="sQ")
    owd = 0
    for kc in range(8):
        lw = s.dma("sync", owf[:], rr(o_wT_in)[:, kc, :],
                   waits=[("gQ", gl), ("ve", owd)], inc="sQ")
        owd = s.emit("vector", lambda e, kc=kc: e.tensor_scalar(
            owb[:, kc, :], owf[:], onc[:, kc:kc + 1], None, op0=ALU.mult),
            waits=[("sQ", lw)], inc="ve")
    gc1 = s.emit("scalar", lambda e: (
        e.activation(gsc[0:4, 4:5], gsc[0:4, 0:1], ACT.Exp, scale=-1.0),
        e.activation(gsc[0:4, 5:6], gsc[0:4, 1:2], ACT.Sigmoid))[-1],
        waits=[("sQ", lt1), ("sQ", mk1)], inc="sc")
    def gc2f(e):
        gap(e)
        e.tensor_scalar(gsc[0:4, 6:7], gsc[0:4, 5:6], -4.0, 1.0,
                        op0=ALU.mult, op1=ALU.add)
        gap(e)
        e.tensor_add(gsc[0:32, 3:4], gsc[0:32, 2:3], gsc[0:32, 7:8])
        gap(e)
        e.tensor_copy(gsc[0:32, 2:3], gsc[0:32, 3:4])
        gap(e)
        return e.tensor_copy(tmpcol[:], cstb[:, 0:1])
    gc2 = s.emit("vector", gc2f, waits=[("sc", gc1)], inc="ve")
    def gmms(e):
        e.matmul(ps[6][0:32, 0:1], mrow[0:4, 0:32], gsc[0:4, 4:5], start=True, stop=True)
        e.matmul(ps[6][0:32, 1:2], mrow[0:4, 32:64], gsc[0:4, 5:6], start=True, stop=True)
        return e.matmul(ps[6][0:32, 2:3], mrow[0:4, 0:32], gsc[0:4, 6:7],
                        start=True, stop=True)
    gm1 = s.emit("tensor", gmms, waits=[("ve", gc2)], inc="pe")
    def gev(e):
        e.tensor_copy(gsc[0:32, 4:7], ps[6][0:32, 0:3])
        gap(e)
        e.tensor_mul(gsc[0:32, 3:4], gsc[0:32, 2:3], gsc[0:32, 4:5])
        gap(e)
        return e.tensor_copy(tmpcol[:], cstb[:, 0:1])
    gv1 = s.emit("vector", gev, waits=[("pe", gm1)], inc="ve")
    def growmm(e):
        e.matmul(ps[6][0:1, 0:32], gsc[0:32, 5:6], identf[0:32, 0:32], start=True, stop=True)
        return e.matmul(ps[6][0:1, 32:64], gsc[0:32, 6:7], identf[0:32, 0:32],
                        start=True, stop=True)
    gm2 = s.emit("tensor", growmm, waits=[("ve", gv1)], inc="pe")
    gv2 = s.emit("vector", lambda e: e.tensor_copy(brw[0:1, 0:64], ps[6][0:1, 0:64]),
                 waits=[("pe", gm2)], inc="ve")
    def gbcmm(e):
        e.matmul(ps[6][:, 0:32], ones_row[:].bitcast(F32), brw[0:1, 0:32],
                 start=True, stop=True)
        return e.matmul(ps[6][:, 32:64], ones_row[:].bitcast(F32), brw[0:1, 32:64],
                        start=True, stop=True)
    gm3 = s.emit("tensor", gbcmm, waits=[("ve", gv2)], inc="pe")
    gv3 = s.emit("vector", lambda e: (e.tensor_copy(fexp[:], ps[6][:, 0:32]),
                                      e.tensor_copy(sexp[:], ps[6][:, 32:64]))[-1],
                 waits=[("pe", gm3)], inc="ve")

    STT = [("short", 0), ("long", 1), ("wide", 2), ("delta", 3), ("v", 4)]
    prevtile = {"ve": 0, "sp": 0, "pe": 0}
    for gb in range(8):
        xl = s.dma("gpsimd", xnbb[:], rr(xn_sp)[:, :, HALO + gb * GW:HALO + gb * GW + GW],
                   inc="gQ")
        st_done = []
        for tt in range(4):
            til = gb * 4 + tt
            tw = [("cc", a2f), ("cc", a2d), ("ve", prevtile["ve"]),
                  ("pe", prevtile["pe"])]
            lds = 0
            vcomb = 0
            for f in range(4):
                ca = PADL + til * 128
                cb = PADL + NT + til * 128
                for sl in range(2):
                    lds = s.xdma("sync", lambda e, f=f, sl=sl, ca=ca:
                        e.dma_start_transpose(
                            bt_a[:, sl * 512:sl * 512 + 512],
                            fir4_ag[sl, f, :, ca:ca + 128]),
                        waits=tw + [("ve", vcomb)], inc="sQ")
                    lds = s.xdma("sync", lambda e, f=f, sl=sl, cb=cb:
                        e.dma_start_transpose(
                            bt_b[:, sl * 512:sl * 512 + 512],
                            fir4_ag[sl, f, :, cb:cb + 128]),
                        waits=tw, inc="sQ")
                def fcomb(e, f=f, tt=tt):
                    e.tensor_scalar(oac2[:], bt_a[:], mhc[:, 0:1], None, op0=ALU.mult)
                    return e.scalar_tensor_tensor(btile[f][:, tt, :], bt_b[:],
                                                  mhc[:, 1:2], oac2[:],
                                                  op0=ALU.mult, op1=ALU.add)
                vcomb = s.emit("vector", fcomb, waits=[("sQ", lds), ("gQ", gl0)],
                               inc="ve")
            ldd = 0
            for sl in range(2):
                for lh in range(2):
                    hd = sl * 2 + lh
                    ldd = s.dma("sync", dt_a[:, hd * 256:hd * 256 + 256],
                                do_ag[sl, 0, lh, til * 128:til * 128 + 128, :],
                                waits=tw + [("ve", vcomb)], inc="sQ")
                    ldd = s.dma("sync", oacc[:, hd * 256:hd * 256 + 256],
                                do_ag[sl, 1, lh, til * 128:til * 128 + 128, :],
                                waits=tw, inc="sQ")
            def dcomb(e, tt=tt):
                e.tensor_scalar(dt_a[:], dt_a[:], mhc[:, 0:1], None, op0=ALU.mult)
                gap(e)
                return e.scalar_tensor_tensor(dtile[:, tt, :], oacc[:],
                                              mhc[:, 1:2], dt_a[:],
                                              op0=ALU.mult, op1=ALU.add)
            vcomb = s.emit("vector", dcomb, waits=[("sQ", ldd)], inc="ve")
            ldd_v = ("ve", vcomb)
            lds = lds
            vops = 0
            for T, (nm, ti) in enumerate(STT):
                if ti == 3:
                    seg = dtile[:, tt, :].rearrange("p (a d) -> p a d", d=256)
                    wkey = [("ve", vcomb)]
                else:
                    fi = [0, 1, 2, None, 3][ti]
                    seg = btile[fi][:, tt, :].rearrange("p (a d) -> p a d", d=256)
                    wkey = [("ve", vcomb)]
                def stat1(e, seg=seg, T=T):
                    e.tensor_reduce(statc[:, T * 4:T * 4 + 4], seg, axis=AX.X, op=ALU.add)
                    return e.tensor_reduce(statc[:, 40 + T * 4:40 + T * 4 + 4], seg,
                                           axis=AX.X, op=ALU.max,
                                           apply_absolute_value=True)
                vops = s.emit("vector", stat1, waits=wkey + [("pe", prevtile["pe"])],
                              inc="ve")
                sqs = s.emit("scalar", lambda e, seg=seg: e.activation(
                    x2t[:].rearrange("p (a d) -> p a d", d=256), seg, ACT.Square),
                    waits=wkey + [("ve", vops), ("ve", s.cnt["ve"])], inc="sc")
                vops = s.emit("vector", lambda e, T=T: e.tensor_reduce(
                    ss20[:, T * 4:T * 4 + 4], x2t[:].rearrange("p (a d) -> p a d", d=256),
                    axis=AX.X, op=ALU.add), waits=[("sc", sqs)], inc="ve")
            def stfin(e):
                gap(e)
                e.tensor_scalar(statc[:, 0:20], statc[:, 0:20], 1.0 / 256.0, None,
                                op0=ALU.mult)
                e.tensor_scalar(ss20[:, 0:20], ss20[:, 0:20], 1.0 / 256.0, 1e-8,
                                op0=ALU.mult, op1=ALU.max)
                gap(e)
                return e.tensor_copy(tmpcol[:], cstb[:, 0:1])
            vf1 = s.emit("vector", stfin, waits=[("ve", vops)], inc="ve")
            sf1 = s.emit("scalar", lambda e: (
                e.activation(ss20[:, 0:20], ss20[:, 0:20], ACT.Ln),
                e.activation(statc[:, 20:40], ss20[:, 0:20], ACT.Exp, scale=0.5))[-1],
                waits=[("ve", vf1)], inc="sc")
            stm = s.emit("tensor", lambda e: e.matmul(
                ps[7][0:64, 0:128], statc[:, 0:64], identf[:], start=True, stop=True),
                waits=[("sc", sf1)], inc="pe")
            stv = s.emit("vector", lambda e, tt=tt: e.tensor_copy(
                statf[:, tt * 128:tt * 128 + 128], ps[7][0:64, 0:128]),
                waits=[("pe", stm)], inc="ve")
            st_done.append(stv)
            prevtile["pe"] = stm
            prevtile["ve"] = stv
        g1ev = 0
        for oc in range(16):
            bank = oc % 2
            def g1mm(e, oc=oc, bank=bank):
                for kc in range(8):
                    e.matmul(ps[bank][:, :], g1xw[:, kc, oc * 128:oc * 128 + 128],
                             xnbb[:, kc, :], start=(kc == 0), stop=False)
                return e.matmul(ps[bank][:, :], g1sw[:, oc, :], statf[:],
                                start=False, stop=True)
            gw_ = [("gQ", xl), ("ve", st_done[-1])]
            g1m = s.emit("tensor", g1mm, waits=gw_, inc="pe")
            g1ev = s.emit("scalar", lambda e, oc=oc, bank=bank: e.activation(
                hdnb[:, oc, :], ps[bank][:, :], ACT.Gelu, bias=g1bc[:, oc:oc + 1]),
                waits=[("pe", g1m)], inc="sc")
        def g2mm(e):
            for kc in range(16):
                i = e.matmul(ps[2][0:32, :], g2w[:, kc, :], hdnb[:, kc, :],
                             start=(kc == 0), stop=(kc == 15))
            return i
        g2m = s.emit("tensor", g2mm, waits=[("sc", g1ev)], inc="pe")
        g2e = s.emit("scalar", lambda e: e.activation(
            lgf[:, :], ps[2][0:32, :], ACT.Identity, bias=gsc[0:32, 3:4],
            scale=gsc[0:32, 4:5]), waits=[("pe", g2m), ("ve", gv3)], inc="sc")
        def ltr(e):
            for tt in range(4):
                r = e.matmul(ps[3][:, tt * 32:tt * 32 + 32],
                             lgf[0:32, tt * 128:tt * 128 + 128], identf[0:32, 0:32],
                             start=True, stop=True)
            return r
        lm = s.emit("tensor", ltr, waits=[("sc", g2e)], inc="pe")
        lv = s.emit("vector", lambda e: e.tensor_copy(
            ltm[:].rearrange("p a c -> p (a c)"), ps[3][:, 0:128]),
            waits=[("pe", lm)], inc="ve")
        def smax(e):
            for tt in range(4):
                sl = ltm[:, tt, 0:20].rearrange("p (h j) -> p h j", j=5)
                e.tensor_reduce(w20[:, tt, 20:24], sl, axis=AX.X, op=ALU.max)
            gap(e)
            for tt in range(4):
                for h4 in range(4):
                    e.tensor_scalar(ltm[:, tt, h4 * 5:h4 * 5 + 5],
                                    ltm[:, tt, h4 * 5:h4 * 5 + 5],
                                    w20[:, tt, 20 + h4:21 + h4], None, op0=ALU.subtract)
            return e.tensor_copy(tmpcol[:], cstb[:, 0:1])
        sv1 = s.emit("vector", smax, waits=[("ve", lv)], inc="ve")
        se1 = s.emit("scalar", lambda e: e.activation(
            ltm[:].rearrange("p a c -> p (a c)"), ltm[:].rearrange("p a c -> p (a c)"),
            ACT.Exp), waits=[("ve", sv1)], inc="sc")
        def ssum(e):
            for tt in range(4):
                sl = ltm[:, tt, 0:20].rearrange("p (h j) -> p h j", j=5)
                e.tensor_reduce(w20[:, tt, 20:24], sl, axis=AX.X, op=ALU.add)
            gap(e)
            return e.tensor_copy(tmpcol[:], cstb[:, 0:1])
        sv2 = s.emit("vector", ssum, waits=[("sc", se1)], inc="ve")
        se2 = s.emit("scalar", lambda e: (
            e.activation(w20[:, 0, 24:28], w20[:, 0, 20:24], ACT.Ln),
            e.activation(w20[:, 1, 24:28], w20[:, 1, 20:24], ACT.Ln),
            e.activation(w20[:, 2, 24:28], w20[:, 2, 20:24], ACT.Ln),
            e.activation(w20[:, 3, 24:28], w20[:, 3, 20:24], ACT.Ln),
            e.activation(w20[:, 0, 28:32], w20[:, 0, 24:28], ACT.Exp, scale=-1.0),
            e.activation(w20[:, 1, 28:32], w20[:, 1, 24:28], ACT.Exp, scale=-1.0),
            e.activation(w20[:, 2, 28:32], w20[:, 2, 24:28], ACT.Exp, scale=-1.0),
            e.activation(w20[:, 3, 28:32], w20[:, 3, 24:28], ACT.Exp, scale=-1.0))[-1],
            waits=[("ve", sv2)], inc="sc")
        def wmk(e):
            for tt in range(4):
                for h4 in range(4):
                    e.tensor_scalar(ltm[:, tt, h4 * 5:h4 * 5 + 5],
                                    ltm[:, tt, h4 * 5:h4 * 5 + 5],
                                    w20[:, tt, 28 + h4:29 + h4], None, op0=ALU.mult)
            gap(e)
            for tt in range(4):
                e.scalar_tensor_tensor(ltm[:, tt, 0:20], ltm[:, tt, 0:20], 1.0,
                                       sexp[:, 0:20], op0=ALU.mult, op1=ALU.mult)
            gap(e)
            for tt in range(4):
                e.tensor_add(w20[:, tt, 0:20], ltm[:, tt, 0:20], fexp[:, 0:20])
            gap(e)
            return e.tensor_copy(tmpcol[:], cstb[:, 0:1])
        sv3 = s.emit("vector", wmk, waits=[("sc", se2), ("ve", gv3)], inc="ve")
        for tt in range(4):
            til = gb * 4 + tt
            def ocomb(e, tt=tt):
                for h4 in range(4):
                    e.tensor_scalar(oac2[:, h4 * 256:h4 * 256 + 256],
                                    btile[0][:, tt, h4 * 256:h4 * 256 + 256],
                                    w20[:, tt, h4 * 5:h4 * 5 + 1], None, op0=ALU.mult)
                srcs = [(btile[1][:, tt, :], 1), (btile[2][:, tt, :], 2),
                        (dtile[:, tt, :], 3), (btile[3][:, tt, :], 4)]
                bufs = [oac2, oacc, oac2, oacc, oac2]
                for step, (bsl, j) in enumerate(srcs):
                    src_acc = bufs[step]
                    dst_acc = bufs[step + 1]
                    for h4 in range(4):
                        e.scalar_tensor_tensor(dst_acc[:, h4 * 256:h4 * 256 + 256],
                                               bsl[:, h4 * 256:h4 * 256 + 256],
                                               w20[:, tt, h4 * 5 + j:h4 * 5 + j + 1],
                                               src_acc[:, h4 * 256:h4 * 256 + 256],
                                               op0=ALU.mult, op1=ALU.add)
                return e.tensor_copy(tmpcol[:], cstb[:, 0:1])
            ov = s.emit("vector", ocomb, waits=[("ve", sv3)], inc="ve")
            sq_o = s.emit("scalar", lambda e: e.activation(
                x2t[:].rearrange("p (a d) -> p a d", d=256),
                oac2[:].rearrange("p (a d) -> p a d", d=256), ACT.Square),
                waits=[("ve", ov)], inc="sc")
            rv1 = s.emit("vector", lambda e: e.tensor_reduce(
                w20[:, 0, 20:24] if False else ss20[:, 20:24],
                x2t[:].rearrange("p (a d) -> p a d", d=256), axis=AX.X, op=ALU.add),
                waits=[("sc", sq_o)], inc="ve")
            rs1 = s.emit("scalar", lambda e: (
                e.activation(ss20[:, 20:24], ss20[:, 20:24], ACT.Ln,
                             scale=1.0 / 256.0, bias=1e-5),
                e.activation(ss20[:, 20:24], ss20[:, 20:24], ACT.Exp, scale=-0.5))[-1],
                waits=[("ve", rv1)], inc="sc")
            rv2 = s.emit("vector", lambda e: [
                e.tensor_scalar(oacc[:, h4 * 256:h4 * 256 + 256],
                                oac2[:, h4 * 256:h4 * 256 + 256],
                                ss20[:, 20 + h4:21 + h4], None, op0=ALU.mult)
                for h4 in range(4)][-1], waits=[("sc", rs1)], inc="ve")
            def otr(e):
                for c8 in range(8):
                    r = e.matmul(ps[4 + c8 // 4][:, (c8 % 4) * 128:(c8 % 4) * 128 + 128],
                                 oacc[:, c8 * 128:c8 * 128 + 128], identf[:],
                                 start=True, stop=True)
                return r
            om = s.emit("tensor", otr, waits=[("ve", rv2)], inc="pe")
            ovv = s.emit("vector", lambda e, tt=tt: (
                e.tensor_copy(ofm[:, 0:4, tt * 128:tt * 128 + 128],
                              ps[4][:, :].rearrange("p (c t) -> p c t", t=128)),
                e.tensor_copy(ofm[:, 4:8, tt * 128:tt * 128 + 128],
                              ps[5][:, :].rearrange("p (c t) -> p c t", t=128)))[-1],
                waits=[("pe", om)], inc="ve")
            prevtile["ve"] = max(prevtile["ve"], ovv)
        fvk = {0: 0, 1: 0}
        evg = {0: 0, 1: 0}
        for oc in range(8):
            bank = 2 + oc % 2
            def owmm(e, oc=oc, bank=bank):
                for kc in range(8):
                    i = e.matmul(ps[bank][:, :], owb[:, kc, oc * 128:oc * 128 + 128],
                                 ofm[:, kc, :], start=(kc == 0), stop=(kc == 7))
                return i
            owm = s.emit("tensor", owmm, waits=[("ve", prevtile["ve"]),
                                                ("ve", owd), ("ve", fvk[oc % 2])],
                         inc="pe")
            lo = s.dma("sync", tq[oc % 2][:], rr(out_fm)[:, oc, gb * GW:gb * GW + GW],
                       waits=[("pe", owm), ("ve", fvk[oc % 2])], inc="sQ")
            fv = s.emit("vector", lambda e, oc=oc, bank=bank: e.tensor_add(
                ev[oc % 2][:], ps[bank][:, :], tq[oc % 2][:]),
                waits=[("sQ", lo), ("pe", owm), ("gQ", evg[oc % 2])], inc="ve")
            p3b_sp = s.dma("gpsimd", rr(out_fm)[:, oc, gb * GW:gb * GW + GW],
                           ev[oc % 2][:], waits=[("ve", fv)], inc="gQ")
            fvk[oc % 2] = fv
            evg[oc % 2] = p3b_sp
            prevtile["pe"] = owm

    s.emit("gpsimd", lambda e: None,
           waits=[("gQ", s.cnt["gQ"]), ("sQ", s.cnt["sQ"]), ("ve", s.cnt["ve"]),
                  ("sc", s.cnt["sc"]), ("pe", s.cnt["pe"]), ("cc", s.cnt["cc"])])


_cache = {}


def kernel(**inputs):
    x = np.ascontiguousarray(inputs["L0_attn_hidden_states"], dtype=np.float32)
    B, L, _ = x.shape
    w_up = np.ascontiguousarray(inputs["L0_ffn_w_up"], dtype=np.float32)
    w_gate = np.ascontiguousarray(inputs["L0_ffn_w_gate"], dtype=np.float32)
    w_down = np.ascontiguousarray(inputs["L0_ffn_w_down"], dtype=np.float32)

    qT = np.ascontiguousarray(inputs["L0_attn_q_w"].astype(np.float32).T)
    kT = np.ascontiguousarray(inputs["L0_attn_k_w"].astype(np.float32).T)
    vT = np.ascontiguousarray(inputs["L0_attn_v_w"].astype(np.float32).T)
    bT = np.ascontiguousarray(inputs["L0_attn_b_w"].astype(np.float32).T)
    convs = [np.asarray(inputs[k], np.float32) for k in
             ("L0_attn_convq_w", "L0_attn_convk_w", "L0_attn_convv_w")]
    firs = [np.asarray(inputs[k], np.float32).reshape(HID, -1) for k in
            ("L0_attn_fir_s", "L0_attn_fir_l", "L0_attn_fir_wd")]
    g1w = np.asarray(inputs["L0_attn_g1_w"], np.float32)
    g1b = np.asarray(inputs["L0_attn_g1_b"], np.float32)
    g2w = np.asarray(inputs["L0_attn_g2_w"], np.float32)
    g2b = np.asarray(inputs["L0_attn_g2_b"], np.float32)
    vb = np.asarray(inputs["L0_attn_value_bias"], np.float32)
    ltv = np.asarray(inputs["L0_attn_log_temp"], np.float32)
    flv = np.asarray(inputs["L0_attn_floor_logit"], np.float32)
    onw = np.asarray(inputs["L0_attn_onorm_w"], np.float32)
    owT = np.ascontiguousarray(inputs["L0_attn_o_w"].astype(np.float32).T)

    filt_all = convs + firs
    bands_full = np.zeros((6, HID, 256, 128), np.float32)
    s2 = np.arange(256)[:, None]
    tt_ = np.arange(128)[None, :]
    for f, wf in enumerate(filt_all):
        K = wf.shape[1]
        tau = s2 - 128 - tt_ + K - 1
        valid = (tau >= 0) & (tau < K)
        tcl = np.clip(tau, 0, K - 1)
        bands_full[f] = np.where(valid[None, :, :], wf[:, tcl], 0.0)

    g1xT = np.ascontiguousarray(g1w[:, 0:HID].T)
    g1sT = np.zeros((64, 2048), np.float32)
    for stat in range(3):
        for T in range(5):
            for h in range(4):
                g1sT[stat * 20 + T * 4 + h] = g1w[:, HID + T * 12 + stat * 4 + h]
    g2T = np.zeros((2048, 32), np.float32)
    g2T[:, 0:20] = g2w.T
    g2bc = np.zeros((32, 1), np.float32); g2bc[0:20, 0] = g2b
    vbc = np.zeros((32, 1), np.float32)
    for h in range(4):
        vbc[h * 5 + 4, 0] = vb[h]
    ltmask = np.zeros((4, 32), np.float32)
    fmask = np.zeros((4, 32), np.float32)
    for h in range(4):
        for j in range(5):
            ltmask[h, h * 5 + j] = 1.0
            if j < 4:
                fmask[h, h * 5 + j] = 1.0
    onormc = np.zeros((128, 8), np.float32)
    for kc in range(8):
        onormc[:, kc] = onw[(np.arange(128) + 128 * kc) % 256]
    bd32 = np.tril(np.ones((32, 32), np.float32), -1)
    bd128 = np.zeros((128, 128), np.float32)
    for gblk in range(4):
        bd128[gblk * 32:(gblk + 1) * 32, gblk * 32:(gblk + 1) * 32] = bd32
    m_bdtril = np.tile(bd128, (1, 4))
    m_ipack = np.tile(np.eye(128, dtype=np.float32), (1, 4))
    m_sbu = np.zeros((128, 128), np.float32)
    for a in range(4):
        for bb2 in range(4):
            if a < bb2:
                m_sbu[a * 32:a * 32 + 32, bb2 * 32:bb2 * 32 + 32] = 1.0
    m_triu = np.triu(np.ones((128, 128), np.float32), 0)
    ident = np.eye(128, dtype=np.float32)

    nc = build_program(dbg=DEBUG)

    cstv = np.zeros((128, 16), np.float32)
    for i, v in enumerate(CONSTS):
        cstv[:, i] = np.float32(v)

    in_maps = []
    for c in range(NCORE):
        b, hh = c // 2, c % 2
        t0 = hh * NT
        xs = np.zeros((HID, NTOT), np.float32)
        xs[:, HALO:] = x[b, t0:t0 + NT, :].T
        wq = np.zeros((HID, 1540), np.float32)
        wq[:, 0:512] = qT[:, 512 * hh:512 * hh + 512]
        wq[:, 512:1024] = kT[:, 512 * hh:512 * hh + 512]
        wq[:, 1024:1536] = vT[:, 512 * hh:512 * hh + 512]
        wq[:, 1536:1538] = bT[:, 2 * hh:2 * hh + 2]
        in_maps.append({
            "x_fm": xs, "w_up": w_up, "w_gate": w_gate, "w_down": w_down,
            "ones_in": np.ones((128, 1), np.float32),
            "onesr_in": np.ones((1, 128), np.float32),
            "cst_in": cstv,
            "wqkvbT": wq,
            "bands_in": np.ascontiguousarray(bands_full[:, 512 * hh:512 * hh + 512]),
            "o_wT_in": owT, "g1xT_in": g1xT, "g1sT_in": g1sT,
            "g1b_in": np.ascontiguousarray(g1b.reshape(16, 128).T), "g2T_in": g2T,
            "g2b_in": g2bc, "vb_in": vbc,
            "ltmask_in": ltmask, "fmask_in": fmask,
            "ltvec_in": ltv.reshape(4, 1), "flvec_in": flv.reshape(4, 1),
            "onormc_in": onormc, "m_bdtril_in": m_bdtril, "m_ipack_in": m_ipack,
            "m_sbu_in": m_sbu, "m_triu_in": m_triu, "ident_in": ident,
            "mh_in": np.ascontiguousarray(
                np.tile(np.array([1.0 - hh, float(hh)], np.float32), (128, 1))),
        })

    res = run_bass_kernel_spmd(nc, in_maps, core_ids=list(range(NCORE)))
    out = np.empty((B, L, HID), np.float32)
    for c in range(NCORE):
        b, hh = c // 2, c % 2
        out[b, hh * NT:(hh + 1) * NT, :] = res.results[c]["out_fm"].T
    if DEBUG:
        kernel.dbg = [{k: v for k, v in r.items() if k != "out_fm"} for r in res.results]
    return out


# version 03
# speedup vs baseline: 1.0043x; 1.0043x over previous
import numpy as np
import ml_dtypes
import concourse.bass as bass
import concourse.mybir as mybir
from concourse.bass_utils import run_bass_kernel_spmd

F32 = mybir.dt.float32
F32R = mybir.dt.float32r
BF16 = mybir.dt.bfloat16
ALU = mybir.AluOpType
ACT = mybir.ActivationFunctionType
AX = mybir.AxisListType

DEBUG = False

NCORE = 8
HID = 1024
NT = 4096
HALO = 128
NTOT = NT + HALO
NGRP = 8
GW = 512
MAGIC = float(np.float32(1.5 * 2.0 ** 23))
QMAX = 32767.0
CONSTS = [1e-5, 1e-6, 1e-8, -0.5, 0.5, 1.0]
PADL = 128
PADR = 128
NTP = PADL + 2 * NT + PADR
PAIRS = [[0, 1], [2, 3], [4, 5], [6, 7]]

ENGS = ("sync", "gpsimd", "tensor", "vector", "scalar")


class Sched:

    def __init__(self):
        self.ops = {e: [] for e in ENGS}
        self.cnt = {}
        self.lastw = {e: {} for e in ENGS}

    def sem(self, name):
        self.cnt[name] = 0
        return name

    def emit(self, eng, fn, waits=(), inc=None, incby=1):
        ws = []
        for s, v in waits:
            if v > self.lastw[eng].get(s, -1):
                ws.append((s, v))
                self.lastw[eng][s] = v
        self.ops[eng].append((ws, fn, inc, incby))
        if inc is not None:
            self.cnt[inc] += incby
            return self.cnt[inc]
        return None

    last_xfer = None

    def dma(self, eng, out, in_, waits=(), inc="gQ"):
        return self.emit(eng, lambda e, o=out, i=in_: e.dma_start(out=o, in_=i),
                         waits=waits, inc=inc, incby=16)

    def xdma(self, eng, fn, waits=(), inc="gQ"):
        ws = list(waits)
        if self.last_xfer is not None:
            ws.append(self.last_xfer)
        v = self.emit(eng, fn, waits=ws, inc=inc, incby=16)
        self.last_xfer = (inc, v)
        return v

    preamble = None

    def build(self, nc, semhandles):
        issuer = {"gQ": "gpsimd", "sQ": "sync"}
        waited = {S: set() for S in issuer}
        if self.preamble and self.preamble[0] in waited:
            waited[self.preamble[0]].add(self.preamble[1] // 16)
        for e2 in ENGS:
            for ws, _fn, _inc, _incby in self.ops[e2]:
                for S, v in ws:
                    if S in waited:
                        waited[S].add(v // 16)
        wsorted = {S: sorted(waited[S]) for S in issuer}

        import bisect
        selfwaits = {e2: {} for e2 in ENGS}
        nbound = 0
        pcs = {e2: 0 for e2 in ENGS}
        cnt_sim = {sem: 0 for sem in self.cnt}
        issued = {S: 0 for S in issuer}
        P = {e2: dict.fromkeys(issuer, 0) for e2 in ENGS}
        Pinc = {sem: {} for sem in self.cnt}
        okP = {S: [] for S in issuer}
        sem_of = {engname: S for S, engname in issuer.items()}
        live = [e2 for e2 in ENGS if self.ops[e2]]
        while live:
            progress = False
            for e2 in list(live):
                while pcs[e2] < len(self.ops[e2]):
                    ws, _fn, inc, incby = self.ops[e2][pcs[e2]]
                    ok = True
                    for S, v in ws:
                        if S in issuer:
                            if issued[S] < v:
                                ok = False
                                break
                        elif cnt_sim[S] < v:
                            ok = False
                            break
                    if not ok:
                        break
                    for S, v in ws:
                        if S in issuer:
                            m = v // 16
                            if m > P[e2][S]:
                                P[e2][S] = m
                            if m > 0:
                                snap = okP[S][m - 1]
                                for S2 in issuer:
                                    if snap[S2] > P[e2][S2]:
                                        P[e2][S2] = snap[S2]
                        else:
                            snap = Pinc[S].get(v)
                            if snap:
                                for S2 in issuer:
                                    if snap[S2] > P[e2][S2]:
                                        P[e2][S2] = snap[S2]
                    if inc is not None and inc in issuer:
                        S = inc
                        j = issued[S] // 16 + 1
                        wl = wsorted[S]
                        i_hi = bisect.bisect_left(wl, j)
                        if i_hi > 0 and wl[i_hi - 1] > P[e2][S]:
                            mstar = wl[i_hi - 1]
                            selfwaits[e2][pcs[e2]] = (S, mstar * 16)
                            nbound += 1
                            P[e2][S] = mstar
                            snap = okP[S][mstar - 1]
                            for S2 in issuer:
                                if snap[S2] > P[e2][S2]:
                                    P[e2][S2] = snap[S2]
                        okP[S].append(dict(P[e2]))
                        issued[S] += 16
                    elif inc is not None:
                        cnt_sim[inc] += incby
                        Pinc[inc][cnt_sim[inc]] = dict(P[e2])
                    pcs[e2] += 1
                    progress = True
                if pcs[e2] >= len(self.ops[e2]):
                    live.remove(e2)
            if not progress and live:
                raise RuntimeError(f"boundary pass: schedule deadlock at {pcs}")
        import sys as _sys
        print(f"Sched: inserted {nbound} DMA-boundary self-waits "
              f"(gQ waited={len(waited['gQ'])}, sQ waited={len(waited['sQ'])})",
              file=_sys.stderr)

        with nc.Block() as blk:
            for engname in ENGS:
                ops = self.ops[engname]
                if not ops:
                    continue

                def section(eng, _ops=ops, _eng=engname):
                    if self.preamble and _eng != "gpsimd":
                        eng.wait_ge(semhandles[self.preamble[0]], self.preamble[1])
                    for idx, (ws, fn, inc, incby) in enumerate(_ops):
                        bw = selfwaits[_eng].get(idx)
                        if bw is not None:
                            eng.wait_ge(semhandles[bw[0]], bw[1])
                        for s, v in ws:
                            eng.wait_ge(semhandles[s], v)
                        instr = fn(eng)
                        if inc is not None:
                            assert instr is not None
                            instr.then_inc(semhandles[inc], incby)

                getattr(blk, engname)(section)


def rr(ap, p=128):
    return ap.rearrange("(c p) t -> p c t", p=p)


def build_program(dbg=False):
    nc = bass.Bass("TRN2", target_bir_lowering=False)
    from contextlib import ExitStack
    ctx = ExitStack()
    nc._ctx = ctx
    E = ctx.enter_context

    x_fm = nc.dram_tensor("x_fm", [HID, NTOT], F32, kind="ExternalInput").ap()
    w_up = nc.dram_tensor("w_up", [3072, HID], F32, kind="ExternalInput").ap()
    w_gate = nc.dram_tensor("w_gate", [3072, HID], F32, kind="ExternalInput").ap()
    w_down = nc.dram_tensor("w_down", [HID, 3072], F32, kind="ExternalInput").ap()
    ones_in = nc.dram_tensor("ones_in", [128, 1], F32, kind="ExternalInput").ap()
    cst_in = nc.dram_tensor("cst_in", [128, 16], F32, kind="ExternalInput").ap()
    onesr_in = nc.dram_tensor("onesr_in", [1, 128], F32, kind="ExternalInput").ap()
    out_fm = nc.dram_tensor("out_fm", [HID, NT], F32, kind="ExternalOutput").ap()

    spk = "ExternalOutput" if dbg else "Internal"
    xn_sp = nc.dram_tensor("xn_sp", [HID, NTOT], F32, kind=spk).ap()
    up_sp = nc.dram_tensor("up_sp", [3072, NT], F32, kind=spk).ap()
    gate_sp = nc.dram_tensor("gate_sp", [3072, NT], F32, kind=spk).ap()
    g2p_sp = nc.dram_tensor("g2p_sp", [3072, NT], F32, kind=spk).ap()
    o2p_sp = nc.dram_tensor("o2p_sp", [3072, NT], F32, kind=spk).ap()
    upq_sp = nc.dram_tensor("upq_sp", [3072, HID], BF16, kind=spk).ap()
    gateq_sp = nc.dram_tensor("gateq_sp", [3072, HID], BF16, kind="Internal").ap()
    downq_sp = nc.dram_tensor("downq_sp", [HID, 3072], BF16, kind="Internal").ap()
    cc_in = nc.dram_tensor("cc_in", [1, 8], F32, kind="Internal").ap()
    cc_out = nc.dram_tensor("cc_out", [1, 8], F32, kind="Internal").ap()

    dbg_sc = None
    if dbg:
        dbg_sc = nc.dram_tensor("dbg_sc", [1, 16], F32, kind="ExternalOutput").ap()

    rowb = E(nc.sbuf_tensor("rowb", [1, 4 * GW], F32))
    ones_col = E(nc.sbuf_tensor("ones_col", [128, 1], F32))
    ones_row = E(nc.sbuf_tensor("ones_row", [1, 128], F32))
    amax_xn = E(nc.sbuf_tensor("amax_xn", [128, 1], F32))
    amax_t = E(nc.sbuf_tensor("amax_t", [128, 8], F32))
    amaxw = E(nc.sbuf_tensor("amaxw", [128, 200], F32))
    tmpcol = E(nc.sbuf_tensor("tmpcol", [128, 1], F32))
    redrow = E(nc.sbuf_tensor("redrow", [1, 128], F32))
    scal = E(nc.sbuf_tensor("scal", [1, 16], F32))
    ccs = E(nc.sbuf_tensor("ccs", [1, 8], F32))
    sbc = E(nc.sbuf_tensor("sbc", [128, 12], F32))
    
    ev = [E(nc.sbuf_tensor(f"ev{i}", [128, GW], F32)) for i in range(2)]
    tq = [E(nc.sbuf_tensor(f"tq{i}", [128, GW], F32)) for i in range(6)]

    cstb = E(nc.sbuf_tensor("cstb", [128, 16], F32))
    ffn_es = ExitStack()
    wsc = E(nc.sbuf_tensor("wsc", [128, 5 * 32], F32))
    xg = [ffn_es.enter_context(nc.sbuf_tensor("xg0", [128, 8, GW], F32))]
    x2g = ffn_es.enter_context(nc.sbuf_tensor("x2g", [128, 8, GW], F32))
    xng = [ffn_es.enter_context(nc.sbuf_tensor("xng0", [128, 8, GW], F32))]
    wraw = ffn_es.enter_context(nc.sbuf_tensor("wraw", [128, HID], F32))
    wqt = ffn_es.enter_context(nc.sbuf_tensor("wqt", [128, HID], BF16))
    x1sb = ffn_es.enter_context(nc.sbuf_tensor("x1sb", [128, 8, NT], BF16))
    wsb = ffn_es.enter_context(nc.sbuf_tensor("wsb", [128, 8, 3072], BF16))
    o2q = ffn_es.enter_context(nc.sbuf_tensor("o2q", [128, 12, GW], BF16))
    ps = [E(nc.psum_tensor(f"ps{i}", [128, GW], F32)) for i in range(8)]

    semnames = ["sQ", "gQ", "pe", "ve", "sc", "cc"]
    sh = {}
    s = Sched()
    for n in semnames:
        sh[n] = E(nc.semaphore(n))
        s.sem(n)

    f32r = lambda ap: ap

    def gap(e):
        e.tensor_copy(tmpcol[:], cstb[:, 0:1])

    GRP = [(0, HALO, False)] + [(HALO + i * GW, GW, True) for i in range(NGRP)]

    c_cst = s.dma("gpsimd", cstb[:], cst_in)
    for i, v in enumerate(CONSTS):
        nc.const_aps.register(F32, float(v), cstb[:, i:i + 1])
    s.preamble = ("gQ", c_cst)
    c_ones = s.dma("gpsimd", ones_col[:], ones_in)
    c_onesr = s.dma("gpsimd", ones_row[:], onesr_in)
    s.emit("vector", lambda e: e.memset(amax_xn[:], 0.0), inc="ve")
    s.emit("vector", lambda e: e.memset(amaxw[:], 0.0), inc="ve")
    s.emit("vector", lambda e: e.memset(amax_t[:], 0.0), inc="ve")
    s.emit("vector", lambda e: e.memset(scal[:], 0.0), inc="ve")
    ve_init = s.cnt["ve"]
    lp_units = []
    for t in range(24):
        lp_units.append((rr(w_up)[:, t, :], rr(upq_sp)[:, t, :]))
    for t in range(24):
        lp_units.append((rr(w_gate)[:, t, :], rr(gateq_sp)[:, t, :]))
    for t in range(8):
        for cc3 in range(3):
            lp_units.append((rr(w_down)[:, t, cc3 * HID:(cc3 + 1) * HID],
                             rr(downq_sp)[:, t, cc3 * HID:(cc3 + 1) * HID]))
    lp_prev_v = 0
    lp_prev_sp = 0
    lp_prev_sc = 0
    for ui, (src_ap, dst_ap) in enumerate(lp_units):
        ldw = s.dma("sync", wraw[:], src_ap,
                    waits=[("ve", lp_prev_v)], inc="sQ")
        def lpq_a(e):
            e.tensor_reduce(wsc[:, 0:32], wraw[:].rearrange("p (b k) -> p b k", k=32),
                            axis=AX.X, op=ALU.max, apply_absolute_value=True)
            e.tensor_copy(tmpcol[:], cstb[:, 0:1])
            return e.tensor_scalar(wsc[:, 32:64], wsc[:, 0:32], 1.0 / 7.0, 1e-12,
                                   op0=ALU.mult, op1=ALU.max)
        lv_a = s.emit("vector", lpq_a, waits=[("sQ", ldw), ("gQ", lp_prev_sp),
                                              ("sc", lp_prev_sc)], inc="ve")
        lv_s = s.emit("scalar", lambda e: (
            e.activation(wsc[:, 96:128], wsc[:, 32:64], ACT.Ln),
            e.activation(wsc[:, 64:96], wsc[:, 96:128], ACT.Exp, scale=-1.0))[-1],
            waits=[("ve", lv_a)], inc="sc")
        lp_prev_sc = lv_s
        def lpq_b(e):
            e.tensor_mul(wsc[:, 96:128], wsc[:, 32:64], wsc[:, 64:96])
            gap(e)
            e.tensor_scalar(wsc[:, 96:128], wsc[:, 96:128], -1.0, 2.0,
                            op0=ALU.mult, op1=ALU.add)
            gap(e)
            e.tensor_mul(wsc[:, 128:160], wsc[:, 64:96], wsc[:, 96:128])
            gap(e)
            e.scalar_tensor_tensor(wraw[:].rearrange("p (b k) -> p b k", k=32),
                                   wraw[:].rearrange("p (b k) -> p b k", k=32), 1.0,
                                   wsc[:, 128:160].unsqueeze(2).broadcast_to([128, 32, 32]),
                                   op0=ALU.mult, op1=ALU.mult)
            e.tensor_scalar(wraw[:], wraw[:], MAGIC, None, op0=ALU.add)
            return e.scalar_tensor_tensor(
                wqt[:].rearrange("p (b k) -> p b k", k=32),
                wraw[:].rearrange("p (b k) -> p b k", k=32), -MAGIC,
                wsc[:, 32:64].unsqueeze(2).broadcast_to([128, 32, 32]),
                op0=ALU.add, op1=ALU.mult)
        lp_prev_v = s.emit("vector", lpq_b, waits=[("sc", lv_s)], inc="ve")
        lp_prev_sp = s.dma("gpsimd", dst_ap, wqt[:], waits=[("ve", lp_prev_v)], inc="gQ")
    lpq_done = lp_prev_sp

    g_upw = 0
    for kc in range(8):
        g_upw = s.xdma("sync", lambda e, kc=kc: e.dma_start_transpose(
            wsb[:, kc, :], upq_sp[:, kc * 128:(kc + 1) * 128]),
            waits=[("gQ", lpq_done)], inc="sQ")
    _dummy = s.dma("gpsimd", wsb[:, 0:1, 0:1], upq_sp[0:1, 0:1]) if False else None


    ln_xn = [0] * len(GRP)
    ln_spill = [0] * len(GRP)
    stats_pe = [0] * len(GRP)
    bc_pe = [0] * len(GRP)
    rows_ve = [0] * len(GRP)

    for g, (t0, tw, own) in enumerate(GRP):
        buf = 0
        w = [] if g < 1 else [("ve", ln_xn[g - 1])]
        ld = s.dma("sync", xg[buf][:, :, 0:tw], rr(x_fm)[:, :, t0:t0 + tw],
                   waits=w, inc="sQ")
        sqw = [("sQ", ld)] + ([("ve", ln_xn[g - 1])] if g >= 1 else [])
        sq = s.emit("scalar", lambda e, b=buf, tw=tw: e.activation(
            x2g[:, :, 0:tw], xg[b][:, :, 0:tw], ACT.Square), waits=sqw, inc="sc")
        def statmm(e, b=buf, tw=tw):
            for c in range(8):
                e.matmul(ps[4][0:1, 0:tw], f32r(ones_col[:]), f32r(xg[b][:, c, 0:tw]),
                         start=(c == 0), stop=(c == 7))
            for c in range(8):
                i = e.matmul(ps[5][0:1, 0:tw], f32r(ones_col[:]), f32r(x2g[:, c, 0:tw]),
                             start=(c == 0), stop=(c == 7))
            return i
        mmw = [("sc", sq)] + ([("ve", rows_ve[g - 1])] if g >= 1 else [])
        stats_pe[g] = s.emit("tensor", statmm, waits=mmw, inc="pe")
        def rows1(e, tw=tw):
            e.tensor_scalar(rowb[0:1, 0:tw], ps[4][0:1, 0:tw], 1.0 / HID, None, op0=ALU.mult)
            e.tensor_scalar(rowb[0:1, GW:GW + tw], ps[5][0:1, 0:tw], 1.0 / HID, None, op0=ALU.mult)
            gap(e)
            e.tensor_mul(rowb[0:1, 2 * GW:2 * GW + tw], rowb[0:1, 0:tw], rowb[0:1, 0:tw])
            gap(e)
            return e.tensor_sub(rowb[0:1, GW:GW + tw], rowb[0:1, GW:GW + tw],
                                rowb[0:1, 2 * GW:2 * GW + tw])
        v1 = s.emit("vector", rows1, waits=[("pe", stats_pe[g])], inc="ve")
        def rows2(e, tw=tw):
            e.activation(rowb[0:1, 2 * GW:2 * GW + tw], rowb[0:1, GW:GW + tw], ACT.Ln, bias=1e-5)
            return e.activation(rowb[0:1, 3 * GW:3 * GW + tw], rowb[0:1, 2 * GW:2 * GW + tw],
                                ACT.Exp, scale=-0.5)
        s1 = s.emit("scalar", rows2, waits=[("ve", v1)], inc="sc")
        rows_ve[g] = s.emit("vector", lambda e, tw=tw: e.scalar_tensor_tensor(
            rowb[0:1, 2 * GW:2 * GW + tw], rowb[0:1, 0:tw], -1.0,
            rowb[0:1, 3 * GW:3 * GW + tw], op0=ALU.mult, op1=ALU.mult),
            waits=[("sc", s1)], inc="ve")
        def bc(e, tw=tw):
            e.matmul(ps[6][:, 0:tw], f32r(ones_row[:]), f32r(rowb[0:1, 3 * GW:3 * GW + tw]),
                     start=True, stop=True)
            return e.matmul(ps[7][:, 0:tw], f32r(ones_row[:]), f32r(rowb[0:1, 2 * GW:2 * GW + tw]),
                            start=True, stop=True)
        bcw = [("ve", rows_ve[g])] + ([("ve", ln_xn[g - 1])] if g >= 1 else [])
        bc_pe[g] = s.emit("tensor", bc, waits=bcw, inc="pe")
        gi = g - 1
        def xnmk(e, b=buf, tw=tw, own=own, gi=gi):
            for c in range(8):
                e.tensor_mul(x2g[:, c, 0:tw], xg[b][:, c, 0:tw], ps[6][:, 0:tw])
                e.tensor_add(xng[b][:, c, 0:tw], x2g[:, c, 0:tw], ps[7][:, 0:tw])
            last = None
            if own:
                for c in range(8):
                    last = e.tensor_reduce(amaxw[:, gi * 8 + c:gi * 8 + c + 1],
                                           xng[b][:, c, 0:tw], axis=AX.X,
                                           op=ALU.max, apply_absolute_value=True)
            if last is None:
                last = e.tensor_copy(tmpcol[:], cstb[:, 0:1])
            return last
        xnw = [("pe", bc_pe[g]), ("ve", ve_init)]
        if g >= 1:
            xnw.append(("gQ", ln_spill[g - 1]))
        ln_xn[g] = s.emit("vector", xnmk, waits=xnw, inc="ve")
        ln_spill[g] = s.dma("gpsimd", rr(xn_sp)[:, :, t0:t0 + tw], xng[buf][:, :, 0:tw],
                            waits=[("ve", ln_xn[g])], inc="gQ")

    def ar_round(slot_vals, tag):
        vw = [s.cnt["ve"]]
        for slot, col in slot_vals:
            g1 = s.xdma("gpsimd", lambda e, c=col: e.dma_start(out=redrow[:], in_=c),
                        waits=[("ve", max(vw))])
            vv = s.emit("vector", lambda e, sl=slot: e.tensor_reduce(
                scal[0:1, sl:sl + 1], redrow[:], axis=AX.X, op=ALU.max),
                waits=[("gQ", g1)], inc="ve")
            vw.append(vv)
        gin = s.dma("gpsimd", cc_in, scal[0:1, 0:8], waits=[("ve", max(vw))])
        ccv = s.emit("gpsimd", lambda e: e.collective_compute(
            "AllReduce", ALU.max, replica_groups=[list(range(NCORE))],
            ins=[cc_in], outs=[cc_out]), waits=[("gQ", gin)], inc="cc")
        gout = s.dma("gpsimd", ccs[:], cc_out, waits=[("cc", ccv)])
        return gout

    fin_xn = s.emit("vector", lambda e: (
        e.tensor_copy(tmpcol[:], cstb[:, 0:1]),
        e.tensor_reduce(amax_xn[:], amaxw[:, 0:64], axis=AX.X, op=ALU.max),
        e.tensor_copy(tmpcol[:], cstb[:, 0:1]))[-1],
        waits=[("ve", ln_xn[-1])], inc="ve")
    g_ar1 = ar_round([(0, amax_xn[:, 0:1])], "xn")
    def mkscale(slot, bc0, bc1, wait_gq):
        sv = s.emit("scalar", lambda e, sl=slot: (
            e.activation(scal[0:1, 11:12], ccs[0:1, sl:sl + 1], ACT.Ln),
            e.activation(scal[0:1, 8:9], scal[0:1, 11:12], ACT.Exp, scale=-1.0))[-1],
            waits=[("gQ", wait_gq), ("pe", s.cnt["pe"]), ("ve", s.cnt["ve"])], inc="sc")
        def nrfix(e, sl=slot):
            e.tensor_mul(scal[0:1, 12:13], ccs[0:1, sl:sl + 1], scal[0:1, 8:9])
            gap(e)
            e.tensor_scalar(scal[0:1, 12:13], scal[0:1, 12:13], -1.0, 2.0,
                            op0=ALU.mult, op1=ALU.add)
            gap(e)
            e.tensor_mul(scal[0:1, 13:14], scal[0:1, 8:9], scal[0:1, 12:13])
            gap(e)
            return e.tensor_copy(scal[0:1, 8:9], scal[0:1, 13:14])
        vv = s.emit("vector", nrfix, waits=[("sc", sv)], inc="ve")
        def sc2(e, sl=slot):
            gap(e)
            e.tensor_scalar(scal[0:1, 9:10], scal[0:1, 8:9], QMAX, None, op0=ALU.mult)
            e.tensor_scalar(scal[0:1, 10:11], ccs[0:1, sl:sl + 1], 1.0 / QMAX, None,
                            op0=ALU.mult)
            return gap(e) or e.tensor_copy(tmpcol[:], cstb[:, 0:1])
        vv = s.emit("vector", sc2, waits=[("ve", vv)], inc="ve")
        mm = s.emit("tensor", lambda e: (
            e.matmul(ps[6][:, 0:1], f32r(ones_row[:]), f32r(scal[0:1, 9:10]), start=True, stop=True),
            e.matmul(ps[6][:, 1:2], f32r(ones_row[:]), f32r(scal[0:1, 10:11]), start=True, stop=True))[-1],
            waits=[("ve", vv)], inc="pe")
        return s.emit("vector", lambda e, b0=bc0, b1=bc1: (
            e.tensor_copy(sbc[:, b0:b0 + 1], ps[6][:, 0:1]),
            e.tensor_copy(sbc[:, b1:b1 + 1], ps[6][:, 1:2]))[-1],
            waits=[("pe", mm)], inc="ve")

    sx_ve = mkscale(0, 0, 1, g_ar1)

    x1_ve = [0] * NGRP
    for g in range(NGRP):
        buf = 0
        t0 = HALO + g * GW
        ld = s.dma("sync", xng[buf][:], rr(xn_sp)[:, :, t0:t0 + GW],
                   waits=[("ve", sx_ve if g < 1 else x1_ve[g - 1]),
                          ("gQ", ln_spill[-1])], inc="sQ")
        def x1mk(e, b=buf, g=g):
            for c in range(8):
                e.tensor_scalar(tq[0][:, :], xng[b][:, c, :], sbc[:, 0:1], MAGIC,
                                op0=ALU.mult, op1=ALU.add)
                i = e.tensor_scalar(x1sb[:, c, g * GW:(g + 1) * GW], tq[0][:, :],
                                    -MAGIC, sbc[:, 1:2], op0=ALU.add, op1=ALU.mult)
            return i
        x1_ve[g] = s.emit("vector", x1mk, waits=[("sQ", ld), ("ve", sx_ve)], inc="ve")

    def mm_3072(spill, amax_slot, wready, tag):
        tile_ev = {}
        evacof = {}
        pe_last = 0
        sp_last = 0
        idx = 0
        for oc in range(24):
            for g in range(NGRP):
                bank = idx % 4
                def mmfn(e, oc=oc, g=g, bank=bank):
                    for kc in range(8):
                        i = e.matmul(ps[bank][:, :], wsb[:, kc, oc * 128:(oc + 1) * 128],
                                     x1sb[:, kc, g * GW:(g + 1) * GW],
                                     start=(kc == 0), stop=(kc == 7))
                    return i
                mw = [("ve", x1_ve[g])] + wready
                if idx >= 2:
                    mw.append(tile_ev[idx - 2])
                if idx >= 4:
                    mw.append(evacof[idx - 4])
                pe_last = s.emit("tensor", mmfn, waits=mw, inc="pe")
                eng = "scalar" if (idx % 2 == 0) else "vector"
                eb = ev[idx % 2]
                if eng == "scalar":
                    ecnt = s.emit("scalar", lambda e, b=bank, eb=eb: e.copy(eb[:], ps[b][:, :]),
                                  waits=[("pe", pe_last)], inc="sc")
                    evacof[idx] = ("sc", ecnt)
                else:
                    ecnt = s.emit("vector", lambda e, b=bank, eb=eb: e.tensor_copy(eb[:], ps[b][:, :]),
                                  waits=[("pe", pe_last)], inc="ve")
                    evacof[idx] = ("ve", ecnt)
                am = s.emit("vector", lambda e, eb=eb, ix=idx: e.tensor_reduce(
                    amaxw[:, ix:ix + 1], eb[:], axis=AX.X, op=ALU.max,
                    apply_absolute_value=True), waits=[evacof[idx]], inc="ve")
                sp_last = s.dma("gpsimd", rr(spill)[:, oc, g * GW:(g + 1) * GW], eb[:],
                                waits=[("ve", am)], inc="gQ")
                tile_ev[idx] = ("gQ", sp_last)
                idx += 1
        fin = s.emit("vector", lambda e, sl=amax_slot: (
            e.tensor_copy(tmpcol[:], cstb[:, 0:1]),
            e.tensor_reduce(amax_t[:, sl:sl + 1], amaxw[:, 0:192], axis=AX.X, op=ALU.max),
            e.tensor_copy(tmpcol[:], cstb[:, 0:1]),
            e.memset(amaxw[:, 0:192], 0.0))[-1], inc="ve")
        return pe_last, sp_last

    up_pe, up_sp_done = mm_3072(up_sp, 0, [("sQ", g_upw)], "up")
    g_gw = 0
    for kc in range(8):
        g_gw = s.xdma("sync", lambda e, kc=kc: e.dma_start_transpose(
            wsb[:, kc, :], gateq_sp[:, kc * 128:(kc + 1) * 128]),
            waits=[("pe", up_pe), ("gQ", lpq_done)], inc="sQ")
    gate_pe, gate_sp_done = mm_3072(gate_sp, 1, [("sQ", g_gw)], "gate")
    wd = wsb[:].rearrange("p c o -> p (c o)").rearrange(
        "p (c o) -> p c o", o=HID)
    g_dnw = 0
    for kc in range(24):
        g_dnw = s.xdma("sync", lambda e, kc=kc: e.dma_start_transpose(
            wd[:, kc, :], downq_sp[:, kc * 128:(kc + 1) * 128]),
            waits=[("pe", gate_pe)], inc="sQ")

    g_ar2 = ar_round([(1, amax_t[:, 0:1]), (2, amax_t[:, 1:2])], "upgate")
    sup_ve = mkscale(1, 2, 3, g_ar2)
    sgate_ve = mkscale(2, 4, 5, g_ar2)

    pa_done = 0
    pa_v1_prev = 0
    for oc in range(24):
        for g in range(NGRP):
            ld = s.dma("sync", tq[0][:], rr(gate_sp)[:, oc, g * GW:(g + 1) * GW],
                       waits=[("ve", max(sup_ve, sgate_ve, pa_v1_prev)),
                              ("gQ", gate_sp_done)], inc="sQ")
            def pa_v1(e):
                e.tensor_scalar(tq[1][:], tq[0][:], sbc[:, 4:5], MAGIC, op0=ALU.mult, op1=ALU.add)
                return e.tensor_scalar(tq[2][:], tq[1][:], -MAGIC, sbc[:, 5:6],
                                       op0=ALU.add, op1=ALU.mult)
            v1 = s.emit("vector", pa_v1, waits=[("sQ", ld)], inc="ve")
            pa_v1_prev = v1
            sg = s.emit("scalar", lambda e: e.activation(tq[3][:], tq[2][:], ACT.Sigmoid),
                        waits=[("ve", v1)], inc="sc")
            def pa_v2(e, ix=oc * NGRP + g):
                e.tensor_scalar(tq[4][:], tq[3][:], 65536.0, MAGIC, op0=ALU.mult, op1=ALU.add)
                e.tensor_scalar(tq[1][:], tq[4][:], -MAGIC, 65535.0, op0=ALU.add, op1=ALU.min)
                e.scalar_tensor_tensor(tq[5][:], tq[1][:], 1.0 / 65536.0, tq[2][:],
                                       op0=ALU.mult, op1=ALU.mult)
                return e.tensor_reduce(amaxw[:, ix:ix + 1], tq[5][:], axis=AX.X,
                                       op=ALU.max, apply_absolute_value=True)
            v2 = s.emit("vector", pa_v2, waits=[("sc", sg), ("gQ", pa_done)], inc="ve")
            pa_done = s.dma("gpsimd", rr(g2p_sp)[:, oc, g * GW:(g + 1) * GW], tq[5][:],
                            waits=[("ve", v2)], inc="gQ")

    s.emit("vector", lambda e: (
        e.tensor_copy(tmpcol[:], cstb[:, 0:1]),
        e.tensor_reduce(amax_t[:, 2:3], amaxw[:, 0:192], axis=AX.X, op=ALU.max),
        e.tensor_copy(tmpcol[:], cstb[:, 0:1]),
        e.memset(amaxw[:, 0:192], 0.0))[-1], inc="ve")
    g_ar3 = ar_round([(3, amax_t[:, 2:3])], "g2")
    sg2_ve = mkscale(3, 6, 7, g_ar3)

    pb_done = 0
    pb_v_prev = 0
    for oc in range(24):
        for g in range(NGRP):
            ld1 = s.dma("sync", tq[0][:], rr(g2p_sp)[:, oc, g * GW:(g + 1) * GW],
                        waits=[("ve", max(sg2_ve, pb_v_prev)), ("gQ", pa_done)], inc="sQ")
            ld2 = s.dma("sync", tq[1][:], rr(up_sp)[:, oc, g * GW:(g + 1) * GW],
                        waits=[("gQ", pb_done)], inc="sQ")
            def pb_v(e, ix=oc * NGRP + g):
                e.tensor_scalar(tq[2][:], tq[0][:], sbc[:, 6:7], MAGIC, op0=ALU.mult, op1=ALU.add)
                e.tensor_scalar(tq[3][:], tq[2][:], -MAGIC, sbc[:, 7:8], op0=ALU.add, op1=ALU.mult)
                e.tensor_scalar(tq[4][:], tq[1][:], sbc[:, 2:3], MAGIC, op0=ALU.mult, op1=ALU.add)
                e.tensor_scalar(tq[5][:], tq[4][:], -MAGIC, sbc[:, 3:4], op0=ALU.add, op1=ALU.mult)
                e.tensor_mul(tq[1][:], tq[3][:], tq[5][:])
                return e.tensor_reduce(amaxw[:, ix:ix + 1], tq[1][:], axis=AX.X,
                                       op=ALU.max, apply_absolute_value=True)
            vb = s.emit("vector", pb_v, waits=[("sQ", ld2), ("gQ", pb_done)], inc="ve")
            pb_v_prev = vb
            pb_done = s.dma("gpsimd", rr(o2p_sp)[:, oc, g * GW:(g + 1) * GW], tq[1][:],
                            waits=[("ve", vb)], inc="gQ")

    s.emit("vector", lambda e: (
        e.tensor_copy(tmpcol[:], cstb[:, 0:1]),
        e.tensor_reduce(amax_t[:, 3:4], amaxw[:, 0:192], axis=AX.X, op=ALU.max),
        e.tensor_copy(tmpcol[:], cstb[:, 0:1]))[-1], inc="ve")
    g_ar4 = ar_round([(4, amax_t[:, 3:4])], "o2")
    so2_ve = mkscale(4, 8, 9, g_ar4)

    dn_last = 0
    out_done = 0
    for g in range(NGRP):
        half_pe = []
        for half in range(2):
            qv = 0
            for kk in range(12):
                kc = half * 12 + kk
                ld = s.dma("sync", tq[kc % 2][:], rr(o2p_sp)[:, kc, g * GW:(g + 1) * GW],
                           waits=[("ve", so2_ve), ("gQ", pb_done)] +
                                 ([("ve", qv)] if kk >= 2 else []) +
                                 ([("pe", dn_last)] if kk < 2 else []), inc="sQ")
                def qc(e, kc=kc, kk=kk):
                    e.tensor_scalar(tq[2 + kc % 2][:], tq[kc % 2][:], sbc[:, 8:9], MAGIC,
                                    op0=ALU.mult, op1=ALU.add)
                    return e.tensor_scalar(o2q[:, kk, :], tq[2 + kc % 2][:], -MAGIC,
                                           sbc[:, 9:10], op0=ALU.add, op1=ALU.mult)
                qv = s.emit("vector", qc, waits=[("sQ", ld)], inc="ve")
            for oc in range(8):
                bank = oc
                def dnmm(e, oc=oc, bank=bank, half=half):
                    for kk in range(12):
                        kc = half * 12 + kk
                        i = e.matmul(ps[bank][:, :], wd[:, kc, oc * 128:(oc + 1) * 128],
                                     o2q[:, kk, :], start=(kc == 0), stop=(kc == 23))
                    return i
                mw = [("ve", qv), ("sQ", g_dnw)]
                if half == 1 and oc >= 4:
                    mw.append(("gQ", out_done))
                dn_pe = s.emit("tensor", dnmm, waits=mw, inc="pe")
                dn_last = dn_pe
                half_pe.append(dn_pe)
        xb = xg[0]
        t0 = HALO + g * GW
        ldx = s.dma("sync", xb[:], rr(x_fm)[:, :, t0:t0 + GW], waits=[("ve", qv)], inc="sQ")
        for oc in range(8):
            bank = oc
            rv = s.emit("vector", lambda e, oc=oc, bank=bank, xb=xb: e.tensor_add(
                ev[oc % 2][:], ps[bank][:, :], xb[:, oc, :]),
                waits=[("pe", half_pe[8 + oc]), ("sQ", ldx)] +
                      ([("gQ", out_done)] if oc >= 2 else []), inc="ve")
            out_done = s.dma("gpsimd", rr(out_fm)[:, oc, g * GW:(g + 1) * GW], ev[oc % 2][:],
                             waits=[("ve", rv)], inc="gQ")

    if dbg:
        s.dma("gpsimd", dbg_sc[0:1, 0:12], sbc[0:1, 0:12],
              waits=[("ve", s.cnt["ve"])])

    ATTN(locals())

    s.build(nc, sh)
    return nc




def ATTN(L):
    nc = L["nc"]; s = L["s"]; E = L["E"]; ps = L["ps"]
    gap = L["gap"]; ones_col = L["ones_col"]; ones_row = L["ones_row"]
    xn_sp = L["xn_sp"]; out_fm = L["out_fm"]; x_fm = L["x_fm"]
    cstb = L["cstb"]; tmpcol = L["tmpcol"]; tq = L["tq"]; ev = L["ev"]
    ffn_es = L["ffn_es"]; dbg = L["dbg"]
    from contextlib import ExitStack

    DI = lambda n, sh_, dt_: nc.dram_tensor(n, sh_, dt_, kind="ExternalInput").ap()
    DN = lambda n, sh_, dt_: nc.dram_tensor(n, sh_, dt_, kind="Internal").ap()
    wqkvbT = DI("wqkvbT", [HID, 1540], F32)
    bands_in = DI("bands_in", [6, 512, 256, 128], F32)
    o_wT_in = DI("o_wT_in", [HID, HID], F32)
    g1xT_in = DI("g1xT_in", [HID, 2048], F32)
    g1sT_in = DI("g1sT_in", [64, 2048], F32)
    g1b_in = DI("g1b_in", [128, 16], F32)
    g2T_in = DI("g2T_in", [2048, 32], F32)
    g2b_in = DI("g2b_in", [32, 1], F32)
    vb_in = DI("vb_in", [32, 1], F32)
    ltmask_in = DI("ltmask_in", [4, 32], F32)
    fmask_in = DI("fmask_in", [4, 32], F32)
    ltvec_in = DI("ltvec_in", [4, 1], F32)
    flvec_in = DI("flvec_in", [4, 1], F32)
    onormc_in = DI("onormc_in", [128, 8], F32)
    m_bdtril_in = DI("m_bdtril_in", [128, 512], F32)
    m_ipack_in = DI("m_ipack_in", [128, 512], F32)
    m_sbu_in = DI("m_sbu_in", [128, 128], F32)
    m_triu_in = DI("m_triu_in", [128, 128], F32)
    ident_in = DI("ident_in", [128, 128], F32)
    mh_in = DI("mh_in", [128, 2], F32)

    xnb_sp = DN("xnb_sp", [HID, NT], BF16)
    xnb_ag = DN("xnb_ag", [2, HID, NT], BF16)
    qkv_sp = DN("qkv_sp", [1536, NTP], BF16)
    beta_sp = DN("beta_sp", [2, NTP], F32)
    beta16_sp = DN("beta16_sp", [16, NTP], BF16)
    qc_sp = DN("qc_sp", [512, NTP], BF16)
    kc_sp = DN("kc_sp", [512, NTP], BF16)
    fir4_sp = DN("fir4_sp", [4, 512, NTP], BF16)
    fir4_ag = DN("fir4_ag", [2, 4, 512, NTP], BF16)
    vc_sp = fir4_sp[3]
    khat_sp = DN("khat_sp", [2, 256, GW], BF16)
    vb_sp = DN("vb_sp", [2, 256, GW], BF16)
    do_sp = DN("do_sp", [2, 2, NT, 256], F32)
    do_ag = DN("do_ag", [2, 2, 2, NT, 256], F32)

    fence = [("gQ", s.cnt["gQ"]), ("sQ", s.cnt["sQ"]), ("ve", s.cnt["ve"]),
             ("sc", s.cnt["sc"]), ("pe", s.cnt["pe"])]
    for eng in ENGS:
        s.emit(eng, lambda e: None, waits=fence)

    xcast = s.dma("gpsimd", rr(xnb_sp), rr(xn_sp)[:, :, HALO:], waits=fence)
    agx = s.emit("gpsimd", lambda e: e.collective_compute(
        "AllGather", ALU.bypass, replica_groups=PAIRS,
        ins=[xnb_sp], outs=[xnb_ag]), waits=[("gQ", xcast)], inc="cc")

    ffn_es.close()
    ident = E(nc.sbuf_tensor("ident", [128, 128], BF16))
    identf = E(nc.sbuf_tensor("identf", [128, 128], F32))
    p12_es = ExitStack()
    E12 = p12_es.enter_context
    wp = E12(nc.sbuf_tensor("wp", [128, 8, 1540], BF16))
    xnb = E12(nc.sbuf_tensor("xnb", [128, 8, GW], BF16))
    evb = [E12(nc.sbuf_tensor(f"evb{i}", [128, GW], BF16)) for i in range(4)]
    pdat = [E12(nc.sbuf_tensor(f"pdat{i}", [128, 64 * 66], BF16)) for i in range(2)]
    zt16 = E12(nc.sbuf_tensor("zt16", [128, 256], BF16))
    zt32 = E12(nc.sbuf_tensor("zt32", [128, 256], F32))
    bvt = E12(nc.sbuf_tensor("bvt", [2, GW], F32))
    m_bdtril = E12(nc.sbuf_tensor("m_bdtril", [128, 512], F32))
    m_ipack = E12(nc.sbuf_tensor("m_ipack", [128, 512], BF16))
    m_sbu = E12(nc.sbuf_tensor("m_sbu", [128, 128], F32))
    m_triu = E12(nc.sbuf_tensor("m_triu", [128, 128], F32))

    bndj = [[E12(nc.sbuf_tensor(f"bnd{j}_{i}", [128, 2, 8, 128], BF16)) for i in range(2)]
            for j in range(3)]
    p3_es = ExitStack()
    E3 = p3_es.enter_context
    brow = L["rowb"]
    qf = E3(nc.sbuf_tensor("qf", [128, 2, GW], BF16))
    kf = E3(nc.sbuf_tensor("kf", [128, 2, GW], BF16))
    vf = E3(nc.sbuf_tensor("vf", [128, 2, GW], BF16))
    qh_ = E3(nc.sbuf_tensor("qh_", [128, 2, GW], BF16))
    kh_ = E3(nc.sbuf_tensor("kh_", [128, 2, GW], BF16))
    kbh = E3(nc.sbuf_tensor("kbh", [128, 2, GW], BF16))
    vbf = E3(nc.sbuf_tensor("vbf", [128, 2, GW], BF16))
    ktm = E3(nc.sbuf_tensor("ktm", [128, 4, 256], BF16))
    kbtm = E3(nc.sbuf_tensor("kbtm", [128, 4, 256], BF16))
    vbtm = E3(nc.sbuf_tensor("vbtm", [128, 4, 256], BF16))
    bcol = E3(nc.sbuf_tensor("bcol", [128, 4, 16], BF16))
    bcolf = E3(nc.sbuf_tensor("bcolf", [128, 4, 16], F32))
    sq2 = E3(nc.sbuf_tensor("sq2", [128, 2, GW], F32))
    apack = E3(nc.sbuf_tensor("apack", [128, 512], BF16))
    tpk = [E3(nc.sbuf_tensor(f"tpk{i}", [128, 512], BF16)) for i in range(4)]
    ppk = [E3(nc.sbuf_tensor(f"ppk{i}", [128, 512], BF16)) for i in range(5)]
    prt = [E3(nc.sbuf_tensor(f"prt{i}", [128, 512], BF16)) for i in range(2)]
    utm = E3(nc.sbuf_tensor("utm", [128, 4, 256], BF16))
    wtm = E3(nc.sbuf_tensor("wtm", [128, 4, 256], BF16))
    wnf = E3(nc.sbuf_tensor("wnf", [128, 2, GW], BF16))
    s0b = [E3(nc.sbuf_tensor(f"s0b{i}", [128, 512], BF16)) for i in range(2)]
    y0t = E3(nc.sbuf_tensor("y0t", [128, 256], BF16))
    t1t = E3(nc.sbuf_tensor("t1t", [128, 256], BF16))
    t2t = E3(nc.sbuf_tensor("t2t", [128, 256], BF16))
    uadj = E3(nc.sbuf_tensor("uadj", [128, 256], BF16))
    nnt = E3(nc.sbuf_tensor("nnt", [128, 128], BF16))
    attt = E3(nc.sbuf_tensor("attt", [128, 128], BF16))
    oev = E3(nc.sbuf_tensor("oev", [128, 256], F32))


    zz = s.emit("vector", lambda e: (e.memset(zt16[:], 0.0),
                                     e.memset(zt32[:], 0.0))[-1], inc="ve")
    cl = 0
    for ap_s, ap_d in [(m_bdtril_in, m_bdtril[:]), (m_ipack_in, m_ipack[:]),
                       (m_sbu_in, m_sbu[:]), (m_triu_in, m_triu[:]),
                       (ident_in, ident[:]), (ident_in, identf[:])]:
        cl = s.dma("gpsimd", ap_d, ap_s, waits=fence)
    wpl = s.dma("gpsimd", wp[:], rr(wqkvbT), waits=fence)

    zp = 0
    for oc in range(12):
        zp = s.dma("gpsimd", rr(qkv_sp)[:, oc, 0:PADL], zt16[:, 0:PADL], waits=[("ve", zz)])
        zp = s.dma("gpsimd", rr(qkv_sp)[:, oc, NTP - PADR:NTP], zt16[:, 0:PADR],
                   waits=[("ve", zz)])
    zp = s.dma("gpsimd", beta_sp[0:2, 0:PADL], zt32[0:2, 0:PADL],
               waits=[("ve", zz)])
    zp = s.dma("gpsimd", beta_sp[0:2, NTP - PADR:NTP], zt32[0:2, 0:PADR],
               waits=[("ve", zz)])
    for c0 in range(NTP // 256):
        zp = s.dma("gpsimd", beta16_sp[:, c0 * 256:c0 * 256 + 256],
                   zt16[0:16, 0:256], waits=[("ve", zz)])

    p1sp = zp
    p1ev = {}
    for g in range(16):
        sl, off = g // 8, (g % 8) * GW
        ldr = s.dma("sync", xnb[:], rr(xnb_ag[sl])[:, :, off:off + GW],
                    waits=[("cc", agx), ("ve", p1ev.get(g - 1, 0))], inc="sQ")
        last = 0
        for oc in range(13):
            bank = oc % 2
            mwid = 128 if oc < 12 else 2
            def p1mm(e, oc=oc, bank=bank, mwid=mwid):
                for kc in range(8):
                    i = e.matmul(ps[bank][0:mwid, :], wp[:, kc, oc * 128:oc * 128 + mwid],
                                 xnb[:, kc, :], start=(kc == 0), stop=(kc == 7))
                return i
            pw = [("sQ", ldr), ("gQ", wpl), ("gQ", p1sp)]
            mm = s.emit("tensor", p1mm, waits=pw, inc="pe")
            tcol = PADL + g * GW
            if oc < 12:
                vv = s.emit("vector", lambda e, b=bank, oc=oc: e.tensor_copy(
                    evb[oc % 4][:], ps[b][:, :]), waits=[("pe", mm)], inc="ve")
                last = vv
                p1sp = s.dma("gpsimd", rr(qkv_sp)[:, oc, tcol:tcol + GW],
                             evb[oc % 4][:], waits=[("ve", vv)], inc="gQ")
            else:
                sg = s.emit("scalar", lambda e, b=bank: e.activation(
                    bvt[0:2, :], ps[b][0:2, :], ACT.Sigmoid),
                    waits=[("pe", mm)], inc="sc")
                p1sp = s.dma("gpsimd", beta_sp[0:2, tcol:tcol + GW], bvt[0:2, :],
                             waits=[("sc", sg)], inc="gQ")
                p1sp = s.dma("gpsimd", beta16_sp[0:2, tcol:tcol + GW], bvt[0:2, :],
                             waits=[("sc", sg)], inc="gQ")
        p1ev[g] = last
    p1_done = p1sp

    NW2 = NTP // 128

    def conv_stream(src_sp, jobs, wait_gq, tag):
        done = 0
        pdat_guard = {}
        bank_guard = {}
        spill_last = 0
        evb_guard = {}
        for cg in range(8):
            buf = cg % 2
            trw = [("gQ", wait_gq), ("pe", pdat_guard.get(cg - 2, 0)),
                   ("gQ", spill_last)]
            tr = s.xdma("sync", lambda e, cg=cg, buf=buf, src=src_sp: e.dma_start_transpose(
                pdat[buf][:], src[cg * 64:cg * 64 + 64, :].rearrange(
                    "c (w x) -> (c w) x", x=128)), waits=trw, inc="sQ")
            for c8 in range(8):
                bpar = (cg * 8 + c8) % 2
                bls = []
                for j, (fid, dst, sil) in enumerate(jobs):
                    for hh2 in range(2):
                        bl = s.dma("gpsimd", bndj[j][bpar][:, hh2],
                                   bands_in[fid, cg * 64 + c8 * 8:cg * 64 + c8 * 8 + 8,
                                            hh2 * 128:hh2 * 128 + 128, :]
                                   .rearrange("c p t -> p c t"),
                                   waits=[("pe", bank_guard.get(("b", cg, c8 - 2, j), 0))],
                                   inc="gQ")
                    bank_guard[("b", cg, c8, j)] = 0
                    bls.append(bl)
                mmlast = 0
                for ci in range(8):
                    c = c8 * 8 + ci
                    slot = ci % 4
                    for j, (fid, dst, sil) in enumerate(jobs):
                        bank = 2 + j
                        def cmm(e, buf=buf, c=c, bank=bank, slot=slot, j=j, bpar=bpar,
                                ci=ci):
                            base = c * NW2
                            e.matmul(ps[bank][0:64, slot * 128:slot * 128 + 128],
                                     pdat[buf][:, base:base + 64],
                                     bndj[j][bpar][:, 0, ci, :], start=True, stop=False)
                            return e.matmul(ps[bank][0:64, slot * 128:slot * 128 + 128],
                                            pdat[buf][:, base + 1:base + 65],
                                            bndj[j][bpar][:, 1, ci, :],
                                            start=False, stop=True)
                        mw = [("sQ", tr), ("gQ", bls[j]),
                              bank_guard.get((bank, slot), ("pe", 0))]
                        mm = s.emit("tensor", cmm, waits=mw, inc="pe")
                        mmlast = mm
                        if ci == 7:
                            ew = [("pe", mm), evb_guard.get(j, ("pe", 0))]
                            if sil:
                                ecnt = s.emit("scalar", lambda e, bank=bank, j=j: e.activation(
                                    evb[j][0:64, :], ps[bank][0:64, :], ACT.Silu),
                                    waits=ew, inc="sc")
                                ekey = ("sc", ecnt)
                            else:
                                ecnt = s.emit("vector", lambda e, bank=bank, j=j: e.tensor_copy(
                                    evb[j][0:64, :], ps[bank][0:64, :]),
                                    waits=ew, inc="ve")
                                ekey = ("ve", ecnt)
                            c0 = cg * 64 + c8 * 8 + 4
                            spill_last = s.dma(
                                "gpsimd",
                                dst[c0:c0 + 4, PADL:PADL + 8192].rearrange(
                                    "c (w t) -> w c t", t=128),
                                evb[j][0:64, :].rearrange("w (c t) -> w c t", t=128),
                                waits=[ekey], inc="gQ")
                            for sl_ in range(4):
                                bank_guard[(bank, sl_)] = ("gQ", spill_last)
                            evb_guard[j] = ("gQ", spill_last)
                        elif ci == 3:
                            ew = [("pe", mm), evb_guard.get(j, ("pe", 0))]
                            if sil:
                                ecnt = s.emit("scalar", lambda e, bank=bank, j=j: e.activation(
                                    evb[j][0:64, :], ps[bank][0:64, :], ACT.Silu),
                                    waits=ew, inc="sc")
                                ekey = ("sc", ecnt)
                            else:
                                ecnt = s.emit("vector", lambda e, bank=bank, j=j: e.tensor_copy(
                                    evb[j][0:64, :], ps[bank][0:64, :]),
                                    waits=ew, inc="ve")
                                ekey = ("ve", ecnt)
                            c0 = cg * 64 + c8 * 8
                            spill_last = s.dma(
                                "gpsimd",
                                dst[c0:c0 + 4, PADL:PADL + 8192].rearrange(
                                    "c (w t) -> w c t", t=128),
                                evb[j][0:64, :].rearrange("w (c t) -> w c t", t=128),
                                waits=[ekey], inc="gQ")
                            for sl_ in range(4):
                                bank_guard[(bank, sl_)] = ("gQ", spill_last)
                            evb_guard[j] = ("gQ", spill_last)
                pdat_guard[cg] = mmlast
        return spill_last

    zp2 = 0
    for dst in (qc_sp, kc_sp, fir4_sp[0], fir4_sp[1], fir4_sp[2], fir4_sp[3]):
        for col0, wdt in ((0, PADL), (NTP - PADR, PADR)):
            for cc4 in range(4):
                zp2 = s.dma("gpsimd", rr(dst)[:, cc4, col0:col0 + wdt],
                            zt16[:, 0:wdt], waits=[("ve", zz)])

    qcd = conv_stream(qkv_sp[0:512], [(0, qc_sp, True)], p1_done, "q")
    kcd = conv_stream(qkv_sp[512:1024], [(1, kc_sp, True)], p1_done, "k")
    vcd = conv_stream(qkv_sp[1024:1536], [(2, vc_sp, True)], p1_done, "v")
    fird = conv_stream(vc_sp, [(3, fir4_sp[0], False), (4, fir4_sp[1], False),
                               (5, fir4_sp[2], False)], max(vcd, zp2), "fir")

    agf = s.emit("gpsimd", lambda e: e.collective_compute(
        "AllGather", ALU.bypass, replica_groups=PAIRS,
        ins=[fir4_sp], outs=[fir4_ag]), waits=[("gQ", fird)], inc="cc")

    NSC = 4
    vsent = {}
    dospill = 0
    for h in range(2):
        sinit = s.emit("vector", lambda e, h=h: e.memset(s0b[h][:], 0.0), inc="ve")
        for g in range(16):
            ts0 = PADL + g * GW
            gpar = g % 2
            w_lds = [("gQ", fird), ("pe", s.cnt["pe"]), ("ve", s.cnt["ve"])]
            l1 = s.dma("sync", qf[:], rr(qc_sp[256 * h:256 * h + 256])[:, :, ts0:ts0 + GW],
                       waits=w_lds, inc="sQ")
            l2 = s.dma("sync", kf[:], rr(kc_sp[256 * h:256 * h + 256])[:, :, ts0:ts0 + GW],
                       inc="sQ")
            l3 = s.dma("sync", vf[:], rr(vc_sp[256 * h:256 * h + 256])[:, :, ts0:ts0 + GW],
                       inc="sQ")
            l4 = s.dma("sync", brow[0:1, 0:GW], beta_sp[h:h + 1, ts0:ts0 + GW], inc="sQ")
            sqs = s.emit("scalar", lambda e: e.activation(sq2[:], qf[:], ACT.Square),
                         waits=[("sQ", l1)], inc="sc")
            def ssmm(e):
                for c2 in range(2):
                    i = e.matmul(ps[6][0:1, 0:GW], ones_col[:], sq2[:, c2, :],
                                 start=(c2 == 0), stop=(c2 == 1))
                return i
            mq = s.emit("tensor", ssmm, waits=[("sc", sqs)], inc="pe")
            rq = s.emit("scalar", lambda e: (
                e.activation(brow[0:1, GW:2 * GW], ps[6][0:1, 0:GW], ACT.Ln, bias=1e-6),
                e.activation(brow[0:1, 2 * GW:3 * GW], brow[0:1, GW:2 * GW], ACT.Exp,
                             scale=-0.5))[-1], waits=[("pe", mq)], inc="sc")
            sqs2 = s.emit("scalar", lambda e: e.activation(sq2[:], kf[:], ACT.Square),
                          waits=[("sQ", l2), ("sc", rq)], inc="sc")
            mk = s.emit("tensor", ssmm, waits=[("sc", sqs2), ("sc", rq)], inc="pe")
            rk = s.emit("scalar", lambda e: (
                e.activation(brow[0:1, GW:2 * GW], ps[6][0:1, 0:GW], ACT.Ln, bias=1e-6),
                e.activation(brow[0:1, 3 * GW:4 * GW], brow[0:1, GW:2 * GW], ACT.Exp,
                             scale=-0.5))[-1], waits=[("pe", mk)], inc="sc")
            def bcmm(e):
                e.matmul(ps[6][:, 0:GW], ones_row[:].bitcast(F32), brow[0:1, 2 * GW:3 * GW],
                         start=True, stop=True)
                e.matmul(ps[7][:, 0:GW], ones_row[:].bitcast(F32), brow[0:1, 3 * GW:4 * GW],
                         start=True, stop=True)
                return e.matmul(ps[5][:, 0:GW], ones_row[:].bitcast(F32), brow[0:1, 0:GW],
                                start=True, stop=True)
            bc = s.emit("tensor", bcmm, waits=[("sc", rk)], inc="pe")
            def scl(e):
                for c2 in range(2):
                    e.tensor_mul(qh_[:, c2, :], qf[:, c2, :], ps[6][:, :])
                    e.tensor_mul(kh_[:, c2, :], kf[:, c2, :], ps[7][:, :])
                for c2 in range(2):
                    e.tensor_mul(kbh[:, c2, :], kh_[:, c2, :], ps[5][:, :])
                    i = e.tensor_mul(vbf[:, c2, :], vf[:, c2, :], ps[5][:, :])
                return i
            sc1 = s.emit("vector", scl, waits=[("pe", bc), ("sQ", l3), ("sQ", l4)], inc="ve")
            sp1 = s.dma("gpsimd", rr(khat_sp[gpar]), kh_[:], waits=[("ve", sc1)], inc="gQ")
            sp2 = s.dma("gpsimd", rr(vb_sp[gpar]), vbf[:], waits=[("ve", sc1)], inc="gQ")
            trs = []
            for tt in range(4):
                trs.append(s.xdma("sync", lambda e, tt=tt, gpar=gpar: e.dma_start_transpose(
                    ktm[:, tt, :], khat_sp[gpar][:, tt * 128:tt * 128 + 128]),
                    waits=[("gQ", sp2)], inc="sQ"))
                trs.append(s.xdma("sync", lambda e, tt=tt, gpar=gpar: e.dma_start_transpose(
                    vbtm[:, tt, :], vb_sp[gpar][:, tt * 128:tt * 128 + 128]),
                    waits=[("gQ", sp2)], inc="sQ"))
                trs.append(s.xdma("sync", lambda e, tt=tt, ts0=ts0: e.dma_start_transpose(
                    bcol[:, tt, :], beta16_sp[:, ts0 + tt * 128:ts0 + tt * 128 + 128]),
                    waits=[("gQ", p1_done)], inc="sQ"))
            trd = trs[-1]
            def kbtf(e, h=h):
                e.tensor_copy(bcolf[:].rearrange("p a c -> p (a c)"),
                              bcol[:].rearrange("p a c -> p (a c)"))
                gap(e)
                for tt in range(4):
                    r = e.tensor_scalar(kbtm[:, tt, :], ktm[:, tt, :],
                                        bcolf[:, tt, h:h + 1], None, op0=ALU.mult)
                return r
            kbt = s.emit("vector", kbtf, waits=[("sQ", trd)], inc="ve")
            def amm(e):
                for i in range(16):
                    pk, gb = i // 4, i % 4
                    for c2 in range(2):
                        r = e.matmul(ps[2][32 * gb:32 * gb + 32,
                                           pk * 128 + 32 * gb:pk * 128 + 32 * gb + 32],
                                     kbh[:, c2, 32 * i:32 * i + 32],
                                     kh_[:, c2, 32 * i:32 * i + 32],
                                     start=(c2 == 0), stop=(c2 == 1),
                                     tile_position=(0, 32 * gb))
                return r
            am = s.emit("tensor", amm, waits=[("ve", kbt), ("ve", sc1)], inc="pe")
            av = s.emit("vector", lambda e: e.tensor_mul(apack[:], ps[2][:, :], m_bdtril[:]),
                        waits=[("pe", am)], inc="ve")
            tv = s.emit("vector", lambda e: e.transpose(tpk[0][:], apack[:]),
                        waits=[("ve", av)], inc="ve")
            chain_v = tv
            for lvl in range(1, 5):
                pprev = apack if lvl == 1 else ppk[lvl - 1]
                tprev = tpk[lvl - 1]
                def sqmm(e, pprev=pprev, tprev=tprev, lvl=lvl):
                    r = None
                    for pk in range(4):
                        sl = slice(pk * 128, pk * 128 + 128)
                        r = e.matmul(ps[2][:, sl], tprev[:, sl], pprev[:, sl],
                                     start=True, stop=True)
                        if lvl < 4:
                            r = e.matmul(ps[3][:, sl], pprev[:, sl], tprev[:, sl],
                                         start=True, stop=True)
                    return r
                sq_ = s.emit("tensor", sqmm, waits=[("ve", chain_v)], inc="pe")
                def sqev(e, lvl=lvl):
                    e.tensor_copy(ppk[lvl][:], ps[2][:, :])
                    if lvl < 4:
                        return e.tensor_copy(tpk[lvl][:], ps[3][:, :])
                    return gap(e) or e.tensor_copy(tmpcol[:], cstb[:, 0:1])
                chain_v = s.emit("vector", sqev, waits=[("pe", sq_)], inc="ve")
            pv = s.emit("vector", lambda e: e.tensor_sub(prt[0][:], m_ipack[:], tpk[0][:]),
                        waits=[("ve", chain_v)], inc="ve")
            cur = 0
            for lvl in range(1, 5):
                def prmm(e, lvl=lvl, cur=cur):
                    for pk in range(4):
                        sl = slice(pk * 128, pk * 128 + 128)
                        e.matmul(ps[2][:, sl], ppk[lvl][:, sl], prt[cur][:, sl],
                                 start=True, stop=False)
                        r = e.matmul(ps[2][:, sl], ident[:], prt[cur][:, sl],
                                     start=False, stop=True)
                    return r
                pm = s.emit("tensor", prmm, waits=[("ve", pv)], inc="pe")
                pv = s.emit("vector", lambda e, cur=cur: e.tensor_copy(
                    prt[1 - cur][:], ps[2][:, :]), waits=[("pe", pm)], inc="ve")
                cur = 1 - cur
            def uwmm(e, cur=cur):
                for i in range(16):
                    pk, gb = i // 4, i % 4
                    lh = prt[cur][32 * gb:32 * gb + 32,
                                  pk * 128 + 32 * gb:pk * 128 + 32 * gb + 32]
                    e.matmul(ps[2][32 * gb:32 * gb + 32, (pk % 2) * 256:(pk % 2) * 256 + 256],
                             lh, vbtm[32 * gb:32 * gb + 32, pk, :],
                             start=True, stop=True, tile_position=(32 * gb, 32 * gb))
                    r = e.matmul(ps[3][32 * gb:32 * gb + 32, (pk % 2) * 256:(pk % 2) * 256 + 256],
                                 lh, kbtm[32 * gb:32 * gb + 32, pk, :],
                                 start=True, stop=True, tile_position=(32 * gb, 32 * gb))
                    if pk == 1 or pk == 3:
                        pass
                return r
            uws = []
            for half in range(2):
                def uwmm2(e, cur=cur, half=half):
                    for i in range(8 * half, 8 * half + 8):
                        pk, gb = i // 4, i % 4
                        lh = prt[cur][32 * gb:32 * gb + 32,
                                      pk * 128 + 32 * gb:pk * 128 + 32 * gb + 32]
                        e.matmul(ps[2][32 * gb:32 * gb + 32,
                                       (pk % 2) * 256:(pk % 2) * 256 + 256],
                                 lh, vbtm[32 * gb:32 * gb + 32, pk, :],
                                 start=True, stop=True, tile_position=(32 * gb, 32 * gb))
                        r = e.matmul(ps[3][32 * gb:32 * gb + 32,
                                           (pk % 2) * 256:(pk % 2) * 256 + 256],
                                 lh, kbtm[32 * gb:32 * gb + 32, pk, :],
                                 start=True, stop=True, tile_position=(32 * gb, 32 * gb))
                    return r
                um = s.emit("tensor", uwmm2, waits=[("ve", pv), ("ve", kbt)] +
                            ([("ve", uws[-1])] if uws else []), inc="pe")
                uv = s.emit("vector", lambda e, half=half: (
                    e.tensor_copy(utm[:].rearrange("p a d -> p (a d)")
                                  [:, half * 512:half * 512 + 512], ps[2][:, :]),
                    e.tensor_copy(wtm[:].rearrange("p a d -> p (a d)")
                                  [:, half * 512:half * 512 + 512], ps[3][:, :]))[-1],
                    waits=[("pe", um)], inc="ve")
                uws.append(uv)
            def wtr(e):
                for dkc in range(2):
                    for tt in range(4):
                        r = e.matmul(ps[4][:, tt * 128:tt * 128 + 128],
                                     wtm[:, tt, dkc * 128:dkc * 128 + 128], ident[:],
                                     start=True, stop=True)
                return r
            wv = uws[-1]
            for dkc in range(2):
                def wtr2(e, dkc=dkc):
                    for tt in range(4):
                        r = e.matmul(ps[4][:, tt * 128:tt * 128 + 128],
                                     wtm[:, tt, dkc * 128:dkc * 128 + 128], ident[:],
                                     start=True, stop=True)
                    return r
                wm = s.emit("tensor", wtr2, waits=[("ve", wv)], inc="pe")
                wv = s.emit("vector", lambda e, dkc=dkc: e.tensor_scalar(
                    wnf[:, dkc, :], ps[4][:, :], -1.0, None, op0=ALU.mult),
                    waits=[("pe", wm)], inc="ve")
            prep_done = wv
            for cc in range(4):
                csl = slice(128 * cc, 128 * cc + 128)
                def y0mm(e, cc=cc, h=h, csl=csl):
                    e.matmul(ps[2][:, 0:256], ident[:], utm[:, cc, :], start=True, stop=False)
                    for dkc in range(2):
                        r = e.matmul(ps[2][:, 0:256], wnf[:, dkc, csl],
                                     s0b[h][:, dkc * 256:dkc * 256 + 256],
                                     start=False, stop=(dkc == 1))
                    return r
                m1 = s.emit("tensor", y0mm, waits=[("ve", prep_done),
                                                   ("ve", s.cnt["ve"])], inc="pe")
                v1 = s.emit("vector", lambda e: e.tensor_copy(y0t[:], ps[2][:, 0:256]),
                            waits=[("pe", m1)], inc="ve")
                def ntmm(e, csl=csl):
                    for c2 in range(2):
                        r = e.matmul(ps[3][:, 0:128], kh_[:, c2, csl], wnf[:, c2, csl],
                                     start=(c2 == 0), stop=(c2 == 1))
                    return r
                m2 = s.emit("tensor", ntmm, waits=[("ve", v1)], inc="pe")
                v2 = s.emit("vector", lambda e: e.tensor_mul(nnt[:], ps[3][:, 0:128], m_sbu[:]),
                            waits=[("pe", m2)], inc="ve")
                def neu(e, src, dst):
                    def f(e2):
                        e2.matmul(ps[2][:, 0:256], ident[:], y0t[:], start=True, stop=False)
                        return e2.matmul(ps[2][:, 0:256], nnt[:], src[:], start=False, stop=True)
                    return f
                v_ = v2
                for (srcb, dstb) in ((y0t, t1t), (t1t, t2t), (t2t, uadj)):
                    mn = s.emit("tensor", lambda e, srcb=srcb: (
                        e.matmul(ps[2][:, 0:256], ident[:], y0t[:], start=True, stop=False),
                        e.matmul(ps[2][:, 0:256], nnt[:], srcb[:], start=False, stop=True))[-1],
                        waits=[("ve", v_)], inc="pe")
                    v_ = s.emit("vector", lambda e, dstb=dstb: e.tensor_copy(
                        dstb[:], ps[2][:, 0:256]), waits=[("pe", mn)], inc="ve")
                def atmm(e, csl=csl):
                    for c2 in range(2):
                        r = e.matmul(ps[3][:, 128:256], kh_[:, c2, csl], qh_[:, c2, csl],
                                     start=(c2 == 0), stop=(c2 == 1))
                    return r
                m3 = s.emit("tensor", atmm, waits=[("ve", v_)], inc="pe")
                v3 = s.emit("vector", lambda e: e.tensor_mul(attt[:], ps[3][:, 128:256],
                                                             m_triu[:]),
                            waits=[("pe", m3)], inc="ve")
                def omm(e, h=h, csl=csl):
                    for dkc in range(2):
                        e.matmul(ps[4][:, 0:256], qh_[:, dkc, csl],
                                 s0b[h][:, dkc * 256:dkc * 256 + 256],
                                 start=(dkc == 0), stop=False)
                    return e.matmul(ps[4][:, 0:256], attt[:], uadj[:], start=False, stop=True)
                m4 = s.emit("tensor", omm, waits=[("ve", v3)], inc="pe")
                v4 = s.emit("vector", lambda e: e.tensor_copy(oev[:], ps[4][:, 0:256]),
                            waits=[("pe", m4)], inc="ve")
                t_in_half = (g % 8) * GW + cc * 128
                dospill = s.dma("gpsimd", do_sp[g // 8, h, t_in_half:t_in_half + 128, :],
                                oev[:], waits=[("ve", v4)], inc="gQ")
                def smm(e, cc=cc, h=h, g=g):
                    for dkc in range(2):
                        r = e.matmul(ps[h][:, dkc * 256:dkc * 256 + 256],
                                     ktm[:, cc, dkc * 128:dkc * 128 + 128], uadj[:],
                                     start=(g == 0 and cc == 0 and dkc == 0),
                                     stop=(g == 15 and cc == 3 and dkc == 1),
                                     skip_group_check=True)
                    return r
                m5 = s.emit("tensor", smm, waits=[("ve", v4), ("ve", sinit)], inc="pe")
                sref = s.emit("vector", lambda e, h=h: e.tensor_copy(s0b[h][:], ps[h][:, :]),
                              waits=[("pe", m5)], inc="ve")

    a2f = agf
    a2d = s.emit("gpsimd", lambda e: e.collective_compute(
        "AllGather", ALU.bypass, replica_groups=PAIRS,
        ins=[do_sp], outs=[do_ag]), waits=[("gQ", dospill)], inc="cc")

    p3_es.close()
    p12_es.close()
    g1xw = E(nc.sbuf_tensor("g1xw", [128, 8, 2048], BF16))
    g1sw = E(nc.sbuf_tensor("g1sw", [64, 16, 128], BF16))
    g2w = E(nc.sbuf_tensor("g2w", [128, 16, 32], BF16))
    g1bc = E(nc.sbuf_tensor("g1bc", [128, 16], F32))
    owb = E(nc.sbuf_tensor("owb", [128, 8, HID], BF16))
    owf = E(nc.sbuf_tensor("owf", [128, HID], F32))
    onc = E(nc.sbuf_tensor("onc", [128, 8], F32))
    btile = [E(nc.sbuf_tensor(f"btile{i}", [128, 4, HID], BF16)) for i in range(4)]
    dtile = E(nc.sbuf_tensor("dtile", [128, 4, HID], F32))
    oac2 = E(nc.sbuf_tensor("oac2", [128, HID], F32))
    bt_a = E(nc.sbuf_tensor("bt_a", [128, HID], BF16))
    bt_b = E(nc.sbuf_tensor("bt_b", [128, HID], BF16))
    dt_a = E(nc.sbuf_tensor("dt_a", [128, HID], F32))
    mhc = E(nc.sbuf_tensor("mhc", [128, 2], F32))
    x2t = E(nc.sbuf_tensor("x2t", [128, HID], F32))
    statc = E(nc.sbuf_tensor("statc", [128, 64], F32))
    ss20 = E(nc.sbuf_tensor("ss20", [128, 24], F32))
    statf = E(nc.sbuf_tensor("statf", [64, GW], BF16))
    hdnb = E(nc.sbuf_tensor("hdnb", [128, 16, GW], BF16))
    lgf = E(nc.sbuf_tensor("lgf", [32, GW], F32))
    mrow = E(nc.sbuf_tensor("mrow", [4, 64], F32))
    ltm = E(nc.sbuf_tensor("ltm", [128, 4, 32], F32))
    w20 = E(nc.sbuf_tensor("w20", [128, 4, 32], F32))
    gsc = E(nc.sbuf_tensor("gsc", [32, 8], F32))
    fexp = E(nc.sbuf_tensor("fexp", [128, 32], F32))
    sexp = E(nc.sbuf_tensor("sexp", [128, 32], F32))
    oacc = E(nc.sbuf_tensor("oacc", [128, HID], F32))
    ofm = E(nc.sbuf_tensor("ofm", [128, 8, GW], BF16))
    xnbb = E(nc.sbuf_tensor("xnbb", [128, 8, GW], BF16))
    brw = E(nc.sbuf_tensor("brw", [1, 64], F32))

    stz0 = s.emit("vector", lambda e: e.memset(statc[:], 0.0),
                  waits=[("sc", s.cnt["sc"]), ("pe", s.cnt["pe"]),
                         ("gQ", s.cnt["gQ"]), ("sQ", s.cnt["sQ"])], inc="ve")

    gl0 = s.dma("gpsimd", mhc[:], mh_in)
    gl = s.dma("gpsimd", g1xw[:], rr(g1xT_in))
    gl = s.dma("gpsimd", g1sw[:], g1sT_in.rearrange("r (c t) -> r c t", t=128))
    gl = s.dma("gpsimd", g2w[:], rr(g2T_in))
    gl = s.dma("gpsimd", g1bc[:], g1b_in)
    gl = s.dma("gpsimd", onc[:], onormc_in)
    mk1 = s.dma("sync", mrow[0:4, 0:32], ltmask_in, inc="sQ")
    mk1 = s.dma("sync", mrow[0:4, 32:64], fmask_in, inc="sQ")
    lt1 = s.dma("sync", gsc[0:4, 0:1], ltvec_in, inc="sQ")
    lt1 = s.dma("sync", gsc[0:4, 1:2], flvec_in, inc="sQ")
    lt1 = s.dma("sync", gsc[0:32, 2:3], g2b_in, inc="sQ")
    lt1 = s.dma("sync", gsc[0:32, 7:8], vb_in, inc="sQ")
    owd = 0
    for kc in range(8):
        lw = s.dma("sync", owf[:], rr(o_wT_in)[:, kc, :],
                   waits=[("gQ", gl), ("ve", owd)], inc="sQ")
        owd = s.emit("vector", lambda e, kc=kc: e.tensor_scalar(
            owb[:, kc, :], owf[:], onc[:, kc:kc + 1], None, op0=ALU.mult),
            waits=[("sQ", lw)], inc="ve")
    gc1 = s.emit("scalar", lambda e: (
        e.activation(gsc[0:4, 4:5], gsc[0:4, 0:1], ACT.Exp, scale=-1.0),
        e.activation(gsc[0:4, 5:6], gsc[0:4, 1:2], ACT.Sigmoid))[-1],
        waits=[("sQ", lt1), ("sQ", mk1)], inc="sc")
    def gc2f(e):
        gap(e)
        e.tensor_scalar(gsc[0:4, 6:7], gsc[0:4, 5:6], -4.0, 1.0,
                        op0=ALU.mult, op1=ALU.add)
        gap(e)
        e.tensor_add(gsc[0:32, 3:4], gsc[0:32, 2:3], gsc[0:32, 7:8])
        gap(e)
        e.tensor_copy(gsc[0:32, 2:3], gsc[0:32, 3:4])
        gap(e)
        return e.tensor_copy(tmpcol[:], cstb[:, 0:1])
    gc2 = s.emit("vector", gc2f, waits=[("sc", gc1)], inc="ve")
    def gmms(e):
        e.matmul(ps[6][0:32, 0:1], mrow[0:4, 0:32], gsc[0:4, 4:5], start=True, stop=True)
        e.matmul(ps[6][0:32, 1:2], mrow[0:4, 32:64], gsc[0:4, 5:6], start=True, stop=True)
        return e.matmul(ps[6][0:32, 2:3], mrow[0:4, 0:32], gsc[0:4, 6:7],
                        start=True, stop=True)
    gm1 = s.emit("tensor", gmms, waits=[("ve", gc2)], inc="pe")
    def gev(e):
        e.tensor_copy(gsc[0:32, 4:7], ps[6][0:32, 0:3])
        gap(e)
        e.tensor_mul(gsc[0:32, 3:4], gsc[0:32, 2:3], gsc[0:32, 4:5])
        gap(e)
        return e.tensor_copy(tmpcol[:], cstb[:, 0:1])
    gv1 = s.emit("vector", gev, waits=[("pe", gm1)], inc="ve")
    def growmm(e):
        e.matmul(ps[6][0:1, 0:32], gsc[0:32, 5:6], identf[0:32, 0:32], start=True, stop=True)
        return e.matmul(ps[6][0:1, 32:64], gsc[0:32, 6:7], identf[0:32, 0:32],
                        start=True, stop=True)
    gm2 = s.emit("tensor", growmm, waits=[("ve", gv1)], inc="pe")
    gv2 = s.emit("vector", lambda e: e.tensor_copy(brw[0:1, 0:64], ps[6][0:1, 0:64]),
                 waits=[("pe", gm2)], inc="ve")
    def gbcmm(e):
        e.matmul(ps[6][:, 0:32], ones_row[:].bitcast(F32), brw[0:1, 0:32],
                 start=True, stop=True)
        return e.matmul(ps[6][:, 32:64], ones_row[:].bitcast(F32), brw[0:1, 32:64],
                        start=True, stop=True)
    gm3 = s.emit("tensor", gbcmm, waits=[("ve", gv2)], inc="pe")
    gv3 = s.emit("vector", lambda e: (e.tensor_copy(fexp[:], ps[6][:, 0:32]),
                                      e.tensor_copy(sexp[:], ps[6][:, 32:64]))[-1],
                 waits=[("pe", gm3)], inc="ve")

    STT = [("short", 0), ("long", 1), ("wide", 2), ("delta", 3), ("v", 4)]
    prevtile = {"ve": 0, "sp": 0, "pe": 0}
    for gb in range(8):
        xl = s.dma("gpsimd", xnbb[:], rr(xn_sp)[:, :, HALO + gb * GW:HALO + gb * GW + GW],
                   inc="gQ")
        st_done = []
        for tt in range(4):
            til = gb * 4 + tt
            tw = [("cc", a2f), ("cc", a2d), ("ve", prevtile["ve"]),
                  ("pe", prevtile["pe"])]
            lds = 0
            vcomb = 0
            for f in range(4):
                ca = PADL + til * 128
                cb = PADL + NT + til * 128
                for sl in range(2):
                    lds = s.xdma("sync", lambda e, f=f, sl=sl, ca=ca:
                        e.dma_start_transpose(
                            bt_a[:, sl * 512:sl * 512 + 512],
                            fir4_ag[sl, f, :, ca:ca + 128]),
                        waits=tw + [("ve", vcomb)], inc="sQ")
                    lds = s.xdma("sync", lambda e, f=f, sl=sl, cb=cb:
                        e.dma_start_transpose(
                            bt_b[:, sl * 512:sl * 512 + 512],
                            fir4_ag[sl, f, :, cb:cb + 128]),
                        waits=tw, inc="sQ")
                def fcomb(e, f=f, tt=tt):
                    e.tensor_scalar(oac2[:], bt_a[:], mhc[:, 0:1], None, op0=ALU.mult)
                    return e.scalar_tensor_tensor(btile[f][:, tt, :], bt_b[:],
                                                  mhc[:, 1:2], oac2[:],
                                                  op0=ALU.mult, op1=ALU.add)
                vcomb = s.emit("vector", fcomb, waits=[("sQ", lds), ("gQ", gl0)],
                               inc="ve")
            ldd = 0
            for sl in range(2):
                for lh in range(2):
                    hd = sl * 2 + lh
                    ldd = s.dma("sync", dt_a[:, hd * 256:hd * 256 + 256],
                                do_ag[sl, 0, lh, til * 128:til * 128 + 128, :],
                                waits=tw + [("ve", vcomb)], inc="sQ")
                    ldd = s.dma("sync", oacc[:, hd * 256:hd * 256 + 256],
                                do_ag[sl, 1, lh, til * 128:til * 128 + 128, :],
                                waits=tw, inc="sQ")
            def dcomb(e, tt=tt):
                e.tensor_scalar(dt_a[:], dt_a[:], mhc[:, 0:1], None, op0=ALU.mult)
                gap(e)
                return e.scalar_tensor_tensor(dtile[:, tt, :], oacc[:],
                                              mhc[:, 1:2], dt_a[:],
                                              op0=ALU.mult, op1=ALU.add)
            vcomb = s.emit("vector", dcomb, waits=[("sQ", ldd)], inc="ve")
            ldd_v = ("ve", vcomb)
            lds = lds
            vops = 0
            for T, (nm, ti) in enumerate(STT):
                if ti == 3:
                    seg = dtile[:, tt, :].rearrange("p (a d) -> p a d", d=256)
                    wkey = [("ve", vcomb)]
                else:
                    fi = [0, 1, 2, None, 3][ti]
                    seg = btile[fi][:, tt, :].rearrange("p (a d) -> p a d", d=256)
                    wkey = [("ve", vcomb)]
                def stat1(e, seg=seg, T=T):
                    e.tensor_reduce(statc[:, T * 4:T * 4 + 4], seg, axis=AX.X, op=ALU.add)
                    return e.tensor_reduce(statc[:, 40 + T * 4:40 + T * 4 + 4], seg,
                                           axis=AX.X, op=ALU.max,
                                           apply_absolute_value=True)
                vops = s.emit("vector", stat1, waits=wkey + [("pe", prevtile["pe"])],
                              inc="ve")
                sqs = s.emit("scalar", lambda e, seg=seg: e.activation(
                    x2t[:].rearrange("p (a d) -> p a d", d=256), seg, ACT.Square),
                    waits=wkey + [("ve", vops), ("ve", s.cnt["ve"])], inc="sc")
                vops = s.emit("vector", lambda e, T=T: e.tensor_reduce(
                    ss20[:, T * 4:T * 4 + 4], x2t[:].rearrange("p (a d) -> p a d", d=256),
                    axis=AX.X, op=ALU.add), waits=[("sc", sqs)], inc="ve")
            def stfin(e):
                gap(e)
                e.tensor_scalar(statc[:, 0:20], statc[:, 0:20], 1.0 / 256.0, None,
                                op0=ALU.mult)
                e.tensor_scalar(ss20[:, 0:20], ss20[:, 0:20], 1.0 / 256.0, 1e-8,
                                op0=ALU.mult, op1=ALU.max)
                gap(e)
                return e.tensor_copy(tmpcol[:], cstb[:, 0:1])
            vf1 = s.emit("vector", stfin, waits=[("ve", vops)], inc="ve")
            sf1 = s.emit("scalar", lambda e: (
                e.activation(ss20[:, 0:20], ss20[:, 0:20], ACT.Ln),
                e.activation(statc[:, 20:40], ss20[:, 0:20], ACT.Exp, scale=0.5))[-1],
                waits=[("ve", vf1)], inc="sc")
            stm = s.emit("tensor", lambda e: e.matmul(
                ps[7][0:64, 0:128], statc[:, 0:64], identf[:], start=True, stop=True),
                waits=[("sc", sf1)], inc="pe")
            stv = s.emit("vector", lambda e, tt=tt: e.tensor_copy(
                statf[:, tt * 128:tt * 128 + 128], ps[7][0:64, 0:128]),
                waits=[("pe", stm)], inc="ve")
            st_done.append(stv)
            prevtile["pe"] = stm
            prevtile["ve"] = stv
        g1ev = 0
        for oc in range(16):
            bank = oc % 2
            def g1mm(e, oc=oc, bank=bank):
                for kc in range(8):
                    e.matmul(ps[bank][:, :], g1xw[:, kc, oc * 128:oc * 128 + 128],
                             xnbb[:, kc, :], start=(kc == 0), stop=False)
                return e.matmul(ps[bank][:, :], g1sw[:, oc, :], statf[:],
                                start=False, stop=True)
            gw_ = [("gQ", xl), ("ve", st_done[-1])]
            g1m = s.emit("tensor", g1mm, waits=gw_, inc="pe")
            g1ev = s.emit("scalar", lambda e, oc=oc, bank=bank: e.activation(
                hdnb[:, oc, :], ps[bank][:, :], ACT.Gelu, bias=g1bc[:, oc:oc + 1]),
                waits=[("pe", g1m)], inc="sc")
        def g2mm(e):
            for kc in range(16):
                i = e.matmul(ps[2][0:32, :], g2w[:, kc, :], hdnb[:, kc, :],
                             start=(kc == 0), stop=(kc == 15))
            return i
        g2m = s.emit("tensor", g2mm, waits=[("sc", g1ev)], inc="pe")
        g2e = s.emit("scalar", lambda e: e.activation(
            lgf[:, :], ps[2][0:32, :], ACT.Identity, bias=gsc[0:32, 3:4],
            scale=gsc[0:32, 4:5]), waits=[("pe", g2m), ("ve", gv3)], inc="sc")
        def ltr(e):
            for tt in range(4):
                r = e.matmul(ps[3][:, tt * 32:tt * 32 + 32],
                             lgf[0:32, tt * 128:tt * 128 + 128], identf[0:32, 0:32],
                             start=True, stop=True)
            return r
        lm = s.emit("tensor", ltr, waits=[("sc", g2e)], inc="pe")
        lv = s.emit("vector", lambda e: e.tensor_copy(
            ltm[:].rearrange("p a c -> p (a c)"), ps[3][:, 0:128]),
            waits=[("pe", lm)], inc="ve")
        def smax(e):
            for tt in range(4):
                sl = ltm[:, tt, 0:20].rearrange("p (h j) -> p h j", j=5)
                e.tensor_reduce(w20[:, tt, 20:24], sl, axis=AX.X, op=ALU.max)
            gap(e)
            for tt in range(4):
                for h4 in range(4):
                    e.tensor_scalar(ltm[:, tt, h4 * 5:h4 * 5 + 5],
                                    ltm[:, tt, h4 * 5:h4 * 5 + 5],
                                    w20[:, tt, 20 + h4:21 + h4], None, op0=ALU.subtract)
            return e.tensor_copy(tmpcol[:], cstb[:, 0:1])
        sv1 = s.emit("vector", smax, waits=[("ve", lv)], inc="ve")
        se1 = s.emit("scalar", lambda e: e.activation(
            ltm[:].rearrange("p a c -> p (a c)"), ltm[:].rearrange("p a c -> p (a c)"),
            ACT.Exp), waits=[("ve", sv1)], inc="sc")
        def ssum(e):
            for tt in range(4):
                sl = ltm[:, tt, 0:20].rearrange("p (h j) -> p h j", j=5)
                e.tensor_reduce(w20[:, tt, 20:24], sl, axis=AX.X, op=ALU.add)
            gap(e)
            return e.tensor_copy(tmpcol[:], cstb[:, 0:1])
        sv2 = s.emit("vector", ssum, waits=[("sc", se1)], inc="ve")
        se2 = s.emit("scalar", lambda e: (
            e.activation(w20[:, 0, 24:28], w20[:, 0, 20:24], ACT.Ln),
            e.activation(w20[:, 1, 24:28], w20[:, 1, 20:24], ACT.Ln),
            e.activation(w20[:, 2, 24:28], w20[:, 2, 20:24], ACT.Ln),
            e.activation(w20[:, 3, 24:28], w20[:, 3, 20:24], ACT.Ln),
            e.activation(w20[:, 0, 28:32], w20[:, 0, 24:28], ACT.Exp, scale=-1.0),
            e.activation(w20[:, 1, 28:32], w20[:, 1, 24:28], ACT.Exp, scale=-1.0),
            e.activation(w20[:, 2, 28:32], w20[:, 2, 24:28], ACT.Exp, scale=-1.0),
            e.activation(w20[:, 3, 28:32], w20[:, 3, 24:28], ACT.Exp, scale=-1.0))[-1],
            waits=[("ve", sv2)], inc="sc")
        def wmk(e):
            for tt in range(4):
                for h4 in range(4):
                    e.tensor_scalar(ltm[:, tt, h4 * 5:h4 * 5 + 5],
                                    ltm[:, tt, h4 * 5:h4 * 5 + 5],
                                    w20[:, tt, 28 + h4:29 + h4], None, op0=ALU.mult)
            gap(e)
            for tt in range(4):
                e.scalar_tensor_tensor(ltm[:, tt, 0:20], ltm[:, tt, 0:20], 1.0,
                                       sexp[:, 0:20], op0=ALU.mult, op1=ALU.mult)
            gap(e)
            for tt in range(4):
                e.tensor_add(w20[:, tt, 0:20], ltm[:, tt, 0:20], fexp[:, 0:20])
            gap(e)
            return e.tensor_copy(tmpcol[:], cstb[:, 0:1])
        sv3 = s.emit("vector", wmk, waits=[("sc", se2), ("ve", gv3)], inc="ve")
        for tt in range(4):
            til = gb * 4 + tt
            def ocomb(e, tt=tt):
                for h4 in range(4):
                    e.tensor_scalar(oac2[:, h4 * 256:h4 * 256 + 256],
                                    btile[0][:, tt, h4 * 256:h4 * 256 + 256],
                                    w20[:, tt, h4 * 5:h4 * 5 + 1], None, op0=ALU.mult)
                srcs = [(btile[1][:, tt, :], 1), (btile[2][:, tt, :], 2),
                        (dtile[:, tt, :], 3), (btile[3][:, tt, :], 4)]
                bufs = [oac2, oacc, oac2, oacc, oac2]
                for step, (bsl, j) in enumerate(srcs):
                    src_acc = bufs[step]
                    dst_acc = bufs[step + 1]
                    for h4 in range(4):
                        e.scalar_tensor_tensor(dst_acc[:, h4 * 256:h4 * 256 + 256],
                                               bsl[:, h4 * 256:h4 * 256 + 256],
                                               w20[:, tt, h4 * 5 + j:h4 * 5 + j + 1],
                                               src_acc[:, h4 * 256:h4 * 256 + 256],
                                               op0=ALU.mult, op1=ALU.add)
                return e.tensor_copy(tmpcol[:], cstb[:, 0:1])
            ov = s.emit("vector", ocomb, waits=[("ve", sv3)], inc="ve")
            sq_o = s.emit("scalar", lambda e: e.activation(
                x2t[:].rearrange("p (a d) -> p a d", d=256),
                oac2[:].rearrange("p (a d) -> p a d", d=256), ACT.Square),
                waits=[("ve", ov)], inc="sc")
            rv1 = s.emit("vector", lambda e: e.tensor_reduce(
                w20[:, 0, 20:24] if False else ss20[:, 20:24],
                x2t[:].rearrange("p (a d) -> p a d", d=256), axis=AX.X, op=ALU.add),
                waits=[("sc", sq_o)], inc="ve")
            rs1 = s.emit("scalar", lambda e: (
                e.activation(ss20[:, 20:24], ss20[:, 20:24], ACT.Ln,
                             scale=1.0 / 256.0, bias=1e-5),
                e.activation(ss20[:, 20:24], ss20[:, 20:24], ACT.Exp, scale=-0.5))[-1],
                waits=[("ve", rv1)], inc="sc")
            rv2 = s.emit("vector", lambda e: [
                e.tensor_scalar(oacc[:, h4 * 256:h4 * 256 + 256],
                                oac2[:, h4 * 256:h4 * 256 + 256],
                                ss20[:, 20 + h4:21 + h4], None, op0=ALU.mult)
                for h4 in range(4)][-1], waits=[("sc", rs1)], inc="ve")
            def otr(e):
                for c8 in range(8):
                    r = e.matmul(ps[4 + c8 // 4][:, (c8 % 4) * 128:(c8 % 4) * 128 + 128],
                                 oacc[:, c8 * 128:c8 * 128 + 128], identf[:],
                                 start=True, stop=True)
                return r
            om = s.emit("tensor", otr, waits=[("ve", rv2)], inc="pe")
            ovv = s.emit("vector", lambda e, tt=tt: (
                e.tensor_copy(ofm[:, 0:4, tt * 128:tt * 128 + 128],
                              ps[4][:, :].rearrange("p (c t) -> p c t", t=128)),
                e.tensor_copy(ofm[:, 4:8, tt * 128:tt * 128 + 128],
                              ps[5][:, :].rearrange("p (c t) -> p c t", t=128)))[-1],
                waits=[("pe", om)], inc="ve")
            prevtile["ve"] = max(prevtile["ve"], ovv)
        fvk = {0: 0, 1: 0}
        evg = {0: 0, 1: 0}
        for oc in range(8):
            bank = 2 + oc % 2
            def owmm(e, oc=oc, bank=bank):
                for kc in range(8):
                    i = e.matmul(ps[bank][:, :], owb[:, kc, oc * 128:oc * 128 + 128],
                                 ofm[:, kc, :], start=(kc == 0), stop=(kc == 7))
                return i
            owm = s.emit("tensor", owmm, waits=[("ve", prevtile["ve"]),
                                                ("ve", owd), ("ve", fvk[oc % 2])],
                         inc="pe")
            lo = s.dma("sync", tq[oc % 2][:], rr(out_fm)[:, oc, gb * GW:gb * GW + GW],
                       waits=[("pe", owm), ("ve", fvk[oc % 2])], inc="sQ")
            fv = s.emit("vector", lambda e, oc=oc, bank=bank: e.tensor_add(
                ev[oc % 2][:], ps[bank][:, :], tq[oc % 2][:]),
                waits=[("sQ", lo), ("pe", owm), ("gQ", evg[oc % 2])], inc="ve")
            p3b_sp = s.dma("gpsimd", rr(out_fm)[:, oc, gb * GW:gb * GW + GW],
                           ev[oc % 2][:], waits=[("ve", fv)], inc="gQ")
            fvk[oc % 2] = fv
            evg[oc % 2] = p3b_sp
            prevtile["pe"] = owm

    s.emit("gpsimd", lambda e: None,
           waits=[("gQ", s.cnt["gQ"]), ("sQ", s.cnt["sQ"]), ("ve", s.cnt["ve"]),
                  ("sc", s.cnt["sc"]), ("pe", s.cnt["pe"]), ("cc", s.cnt["cc"])])


_cache = {}


def kernel(**inputs):
    x = np.ascontiguousarray(inputs["L0_attn_hidden_states"], dtype=np.float32)
    B, L, _ = x.shape
    w_up = np.ascontiguousarray(inputs["L0_ffn_w_up"], dtype=np.float32)
    w_gate = np.ascontiguousarray(inputs["L0_ffn_w_gate"], dtype=np.float32)
    w_down = np.ascontiguousarray(inputs["L0_ffn_w_down"], dtype=np.float32)

    qT = np.ascontiguousarray(inputs["L0_attn_q_w"].astype(np.float32).T)
    kT = np.ascontiguousarray(inputs["L0_attn_k_w"].astype(np.float32).T)
    vT = np.ascontiguousarray(inputs["L0_attn_v_w"].astype(np.float32).T)
    bT = np.ascontiguousarray(inputs["L0_attn_b_w"].astype(np.float32).T)
    convs = [np.asarray(inputs[k], np.float32) for k in
             ("L0_attn_convq_w", "L0_attn_convk_w", "L0_attn_convv_w")]
    firs = [np.asarray(inputs[k], np.float32).reshape(HID, -1) for k in
            ("L0_attn_fir_s", "L0_attn_fir_l", "L0_attn_fir_wd")]
    g1w = np.asarray(inputs["L0_attn_g1_w"], np.float32)
    g1b = np.asarray(inputs["L0_attn_g1_b"], np.float32)
    g2w = np.asarray(inputs["L0_attn_g2_w"], np.float32)
    g2b = np.asarray(inputs["L0_attn_g2_b"], np.float32)
    vb = np.asarray(inputs["L0_attn_value_bias"], np.float32)
    ltv = np.asarray(inputs["L0_attn_log_temp"], np.float32)
    flv = np.asarray(inputs["L0_attn_floor_logit"], np.float32)
    onw = np.asarray(inputs["L0_attn_onorm_w"], np.float32)
    owT = np.ascontiguousarray(inputs["L0_attn_o_w"].astype(np.float32).T)

    filt_all = convs + firs
    bands_full = np.zeros((6, HID, 256, 128), np.float32)
    s2 = np.arange(256)[:, None]
    tt_ = np.arange(128)[None, :]
    for f, wf in enumerate(filt_all):
        K = wf.shape[1]
        tau = s2 - 128 - tt_ + K - 1
        valid = (tau >= 0) & (tau < K)
        tcl = np.clip(tau, 0, K - 1)
        bands_full[f] = np.where(valid[None, :, :], wf[:, tcl], 0.0)

    g1xT = np.ascontiguousarray(g1w[:, 0:HID].T)
    g1sT = np.zeros((64, 2048), np.float32)
    for stat in range(3):
        for T in range(5):
            for h in range(4):
                g1sT[stat * 20 + T * 4 + h] = g1w[:, HID + T * 12 + stat * 4 + h]
    g2T = np.zeros((2048, 32), np.float32)
    g2T[:, 0:20] = g2w.T
    g2bc = np.zeros((32, 1), np.float32); g2bc[0:20, 0] = g2b
    vbc = np.zeros((32, 1), np.float32)
    for h in range(4):
        vbc[h * 5 + 4, 0] = vb[h]
    ltmask = np.zeros((4, 32), np.float32)
    fmask = np.zeros((4, 32), np.float32)
    for h in range(4):
        for j in range(5):
            ltmask[h, h * 5 + j] = 1.0
            if j < 4:
                fmask[h, h * 5 + j] = 1.0
    onormc = np.zeros((128, 8), np.float32)
    for kc in range(8):
        onormc[:, kc] = onw[(np.arange(128) + 128 * kc) % 256]
    bd32 = np.tril(np.ones((32, 32), np.float32), -1)
    bd128 = np.zeros((128, 128), np.float32)
    for gblk in range(4):
        bd128[gblk * 32:(gblk + 1) * 32, gblk * 32:(gblk + 1) * 32] = bd32
    m_bdtril = np.tile(bd128, (1, 4))
    m_ipack = np.tile(np.eye(128, dtype=np.float32), (1, 4))
    m_sbu = np.zeros((128, 128), np.float32)
    for a in range(4):
        for bb2 in range(4):
            if a < bb2:
                m_sbu[a * 32:a * 32 + 32, bb2 * 32:bb2 * 32 + 32] = 1.0
    m_triu = np.triu(np.ones((128, 128), np.float32), 0)
    ident = np.eye(128, dtype=np.float32)

    nc = build_program(dbg=DEBUG)

    cstv = np.zeros((128, 16), np.float32)
    for i, v in enumerate(CONSTS):
        cstv[:, i] = np.float32(v)

    in_maps = []
    for c in range(NCORE):
        b, hh = c // 2, c % 2
        t0 = hh * NT
        xs = np.zeros((HID, NTOT), np.float32)
        xs[:, HALO:] = x[b, t0:t0 + NT, :].T
        wq = np.zeros((HID, 1540), np.float32)
        wq[:, 0:512] = qT[:, 512 * hh:512 * hh + 512]
        wq[:, 512:1024] = kT[:, 512 * hh:512 * hh + 512]
        wq[:, 1024:1536] = vT[:, 512 * hh:512 * hh + 512]
        wq[:, 1536:1538] = bT[:, 2 * hh:2 * hh + 2]
        in_maps.append({
            "x_fm": xs, "w_up": w_up, "w_gate": w_gate, "w_down": w_down,
            "ones_in": np.ones((128, 1), np.float32),
            "onesr_in": np.ones((1, 128), np.float32),
            "cst_in": cstv,
            "wqkvbT": wq,
            "bands_in": np.ascontiguousarray(bands_full[:, 512 * hh:512 * hh + 512]),
            "o_wT_in": owT, "g1xT_in": g1xT, "g1sT_in": g1sT,
            "g1b_in": np.ascontiguousarray(g1b.reshape(16, 128).T), "g2T_in": g2T,
            "g2b_in": g2bc, "vb_in": vbc,
            "ltmask_in": ltmask, "fmask_in": fmask,
            "ltvec_in": ltv.reshape(4, 1), "flvec_in": flv.reshape(4, 1),
            "onormc_in": onormc, "m_bdtril_in": m_bdtril, "m_ipack_in": m_ipack,
            "m_sbu_in": m_sbu, "m_triu_in": m_triu, "ident_in": ident,
            "mh_in": np.ascontiguousarray(
                np.tile(np.array([1.0 - hh, float(hh)], np.float32), (128, 1))),
        })

    res = run_bass_kernel_spmd(nc, in_maps, core_ids=list(range(NCORE)))
    out = np.empty((B, L, HID), np.float32)
    for c in range(NCORE):
        b, hh = c // 2, c % 2
        out[b, hh * NT:(hh + 1) * NT, :] = res.results[c]["out_fm"].T
    if DEBUG:
        kernel.dbg = [{k: v for k, v in r.items() if k != "out_fm"} for r in res.results]
    return out


# version 04
# speedup vs baseline: 1.0494x; 1.0449x over previous
import numpy as np
import ml_dtypes
import concourse.bass as bass
import concourse.mybir as mybir
from concourse.bass_utils import run_bass_kernel_spmd

F32 = mybir.dt.float32
F32R = mybir.dt.float32r
BF16 = mybir.dt.bfloat16
ALU = mybir.AluOpType
ACT = mybir.ActivationFunctionType
AX = mybir.AxisListType

DEBUG = False

NCORE = 8
HID = 1024
NT = 4096
HALO = 128
NTOT = NT + HALO
NGRP = 8
GW = 512
MAGIC = float(np.float32(1.5 * 2.0 ** 23))
QMAX = 32767.0
CONSTS = [1e-5, 1e-6, 1e-8, -0.5, 0.5, 1.0]
PADL = 128
PADR = 128
NTP = PADL + 2 * NT + PADR
PAIRS = [[0, 1], [2, 3], [4, 5], [6, 7]]

ENGS = ("sync", "gpsimd", "tensor", "vector", "scalar")


class Sched:

    def __init__(self):
        self.ops = {e: [] for e in ENGS}
        self.cnt = {}
        self.lastw = {e: {} for e in ENGS}

    def sem(self, name):
        self.cnt[name] = 0
        return name

    def emit(self, eng, fn, waits=(), inc=None, incby=1):
        ws = []
        for s, v in waits:
            if v > self.lastw[eng].get(s, -1):
                ws.append((s, v))
                self.lastw[eng][s] = v
        self.ops[eng].append((ws, fn, inc, incby))
        if inc is not None:
            self.cnt[inc] += incby
            return self.cnt[inc]
        return None

    last_xfer = None

    def dma(self, eng, out, in_, waits=(), inc="gQ"):
        return self.emit(eng, lambda e, o=out, i=in_: e.dma_start(out=o, in_=i),
                         waits=waits, inc=inc, incby=16)

    def xdma(self, eng, fn, waits=(), inc="gQ"):
        assert eng == "sync"
        return self.emit(eng, fn, waits=waits, inc=inc, incby=16)

    preamble = None

    def build(self, nc, semhandles):
        issuer = {"gQ": "gpsimd", "sQ": "sync"}
        waited = {S: set() for S in issuer}
        if self.preamble and self.preamble[0] in waited:
            waited[self.preamble[0]].add(self.preamble[1] // 16)
        for e2 in ENGS:
            for ws, _fn, _inc, _incby in self.ops[e2]:
                for S, v in ws:
                    if S in waited:
                        waited[S].add(v // 16)
        wsorted = {S: sorted(waited[S]) for S in issuer}

        import bisect
        selfwaits = {e2: {} for e2 in ENGS}
        nbound = 0
        pcs = {e2: 0 for e2 in ENGS}
        cnt_sim = {sem: 0 for sem in self.cnt}
        issued = {S: 0 for S in issuer}
        P = {e2: dict.fromkeys(issuer, 0) for e2 in ENGS}
        Pinc = {sem: {} for sem in self.cnt}
        okP = {S: [] for S in issuer}
        sem_of = {engname: S for S, engname in issuer.items()}
        live = [e2 for e2 in ENGS if self.ops[e2]]
        while live:
            progress = False
            for e2 in list(live):
                while pcs[e2] < len(self.ops[e2]):
                    ws, _fn, inc, incby = self.ops[e2][pcs[e2]]
                    ok = True
                    for S, v in ws:
                        if S in issuer:
                            if issued[S] < v:
                                ok = False
                                break
                        elif cnt_sim[S] < v:
                            ok = False
                            break
                    if not ok:
                        break
                    for S, v in ws:
                        if S in issuer:
                            m = v // 16
                            if m > P[e2][S]:
                                P[e2][S] = m
                            if m > 0:
                                snap = okP[S][m - 1]
                                for S2 in issuer:
                                    if snap[S2] > P[e2][S2]:
                                        P[e2][S2] = snap[S2]
                        else:
                            snap = Pinc[S].get(v)
                            if snap:
                                for S2 in issuer:
                                    if snap[S2] > P[e2][S2]:
                                        P[e2][S2] = snap[S2]
                    if inc is not None and inc in issuer:
                        S = inc
                        j = issued[S] // 16 + 1
                        wl = wsorted[S]
                        i_hi = bisect.bisect_left(wl, j)
                        if i_hi > 0 and wl[i_hi - 1] > P[e2][S]:
                            mstar = wl[i_hi - 1]
                            selfwaits[e2][pcs[e2]] = (S, mstar * 16)
                            nbound += 1
                            P[e2][S] = mstar
                            snap = okP[S][mstar - 1]
                            for S2 in issuer:
                                if snap[S2] > P[e2][S2]:
                                    P[e2][S2] = snap[S2]
                        okP[S].append(dict(P[e2]))
                        issued[S] += 16
                    elif inc is not None:
                        cnt_sim[inc] += incby
                        Pinc[inc][cnt_sim[inc]] = dict(P[e2])
                    pcs[e2] += 1
                    progress = True
                if pcs[e2] >= len(self.ops[e2]):
                    live.remove(e2)
            if not progress and live:
                raise RuntimeError(f"boundary pass: schedule deadlock at {pcs}")
        import sys as _sys
        print(f"Sched: inserted {nbound} DMA-boundary self-waits "
              f"(gQ waited={len(waited['gQ'])}, sQ waited={len(waited['sQ'])})",
              file=_sys.stderr)

        with nc.Block() as blk:
            for engname in ENGS:
                ops = self.ops[engname]
                if not ops:
                    continue

                def section(eng, _ops=ops, _eng=engname):
                    if self.preamble and _eng != "gpsimd":
                        eng.wait_ge(semhandles[self.preamble[0]], self.preamble[1])
                    for idx, (ws, fn, inc, incby) in enumerate(_ops):
                        bw = selfwaits[_eng].get(idx)
                        if bw is not None:
                            eng.wait_ge(semhandles[bw[0]], bw[1])
                        for s, v in ws:
                            eng.wait_ge(semhandles[s], v)
                        instr = fn(eng)
                        if inc is not None:
                            assert instr is not None
                            instr.then_inc(semhandles[inc], incby)

                getattr(blk, engname)(section)


def rr(ap, p=128):
    return ap.rearrange("(c p) t -> p c t", p=p)


def build_program(dbg=False):
    nc = bass.Bass("TRN2", target_bir_lowering=False)
    from contextlib import ExitStack
    ctx = ExitStack()
    nc._ctx = ctx
    E = ctx.enter_context

    x_fm = nc.dram_tensor("x_fm", [HID, NTOT], F32, kind="ExternalInput").ap()
    w_up = nc.dram_tensor("w_up", [3072, HID], F32, kind="ExternalInput").ap()
    w_gate = nc.dram_tensor("w_gate", [3072, HID], F32, kind="ExternalInput").ap()
    w_down = nc.dram_tensor("w_down", [HID, 3072], F32, kind="ExternalInput").ap()
    ones_in = nc.dram_tensor("ones_in", [128, 1], F32, kind="ExternalInput").ap()
    cst_in = nc.dram_tensor("cst_in", [128, 16], F32, kind="ExternalInput").ap()
    onesr_in = nc.dram_tensor("onesr_in", [1, 128], F32, kind="ExternalInput").ap()
    out_fm = nc.dram_tensor("out_fm", [HID, NT], F32, kind="ExternalOutput").ap()

    spk = "ExternalOutput" if dbg else "Internal"
    xn_sp = nc.dram_tensor("xn_sp", [HID, NTOT], F32, kind=spk).ap()
    up_sp = nc.dram_tensor("up_sp", [3072, NT], F32, kind=spk).ap()
    gate_sp = nc.dram_tensor("gate_sp", [3072, NT], F32, kind=spk).ap()
    g2p_sp = nc.dram_tensor("g2p_sp", [3072, NT], F32, kind=spk).ap()
    o2p_sp = nc.dram_tensor("o2p_sp", [3072, NT], F32, kind=spk).ap()
    upq_sp = nc.dram_tensor("upq_sp", [3072, HID], BF16, kind=spk).ap()
    gateq_sp = nc.dram_tensor("gateq_sp", [3072, HID], BF16, kind="Internal").ap()
    downq_sp = nc.dram_tensor("downq_sp", [HID, 3072], BF16, kind="Internal").ap()
    cc_in = nc.dram_tensor("cc_in", [1, 8], F32, kind="Internal").ap()
    cc_out = nc.dram_tensor("cc_out", [1, 8], F32, kind="Internal").ap()

    dbg_sc = None
    if dbg:
        dbg_sc = nc.dram_tensor("dbg_sc", [1, 16], F32, kind="ExternalOutput").ap()

    rowb = E(nc.sbuf_tensor("rowb", [1, 4 * GW], F32))
    ones_col = E(nc.sbuf_tensor("ones_col", [128, 1], F32))
    ones_row = E(nc.sbuf_tensor("ones_row", [1, 128], F32))
    amax_xn = E(nc.sbuf_tensor("amax_xn", [128, 1], F32))
    amax_t = E(nc.sbuf_tensor("amax_t", [128, 8], F32))
    amaxw = E(nc.sbuf_tensor("amaxw", [128, 200], F32))
    tmpcol = E(nc.sbuf_tensor("tmpcol", [128, 1], F32))
    redrow = E(nc.sbuf_tensor("redrow", [1, 128], F32))
    scal = E(nc.sbuf_tensor("scal", [1, 16], F32))
    ccs = E(nc.sbuf_tensor("ccs", [1, 8], F32))
    sbc = E(nc.sbuf_tensor("sbc", [128, 12], F32))
    
    ev = [E(nc.sbuf_tensor(f"ev{i}", [128, GW], F32)) for i in range(2)]
    tq = [E(nc.sbuf_tensor(f"tq{i}", [128, GW], F32)) for i in range(6)]

    cstb = E(nc.sbuf_tensor("cstb", [128, 16], F32))
    ffn_es = ExitStack()
    wsc = E(nc.sbuf_tensor("wsc", [128, 5 * 32], F32))
    xg = [ffn_es.enter_context(nc.sbuf_tensor("xg0", [128, 8, GW], F32))]
    x2g = ffn_es.enter_context(nc.sbuf_tensor("x2g", [128, 8, GW], F32))
    xng = [ffn_es.enter_context(nc.sbuf_tensor("xng0", [128, 8, GW], F32))]
    wraw = ffn_es.enter_context(nc.sbuf_tensor("wraw", [128, HID], F32))
    wqt = ffn_es.enter_context(nc.sbuf_tensor("wqt", [128, HID], BF16))
    x1sb = ffn_es.enter_context(nc.sbuf_tensor("x1sb", [128, 8, NT], BF16))
    wsb = ffn_es.enter_context(nc.sbuf_tensor("wsb", [128, 8, 3072], BF16))
    o2q = ffn_es.enter_context(nc.sbuf_tensor("o2q", [128, 12, GW], BF16))
    ps = [E(nc.psum_tensor(f"ps{i}", [128, GW], F32)) for i in range(8)]

    semnames = ["sQ", "gQ", "pe", "ve", "sc", "cc"]
    sh = {}
    s = Sched()
    for n in semnames:
        sh[n] = E(nc.semaphore(n))
        s.sem(n)

    f32r = lambda ap: ap

    def gap(e):
        e.tensor_copy(tmpcol[:], cstb[:, 0:1])

    GRP = [(0, HALO, False)] + [(HALO + i * GW, GW, True) for i in range(NGRP)]

    c_cst = s.dma("gpsimd", cstb[:], cst_in)
    for i, v in enumerate(CONSTS):
        nc.const_aps.register(F32, float(v), cstb[:, i:i + 1])
    s.preamble = ("gQ", c_cst)
    c_ones = s.dma("gpsimd", ones_col[:], ones_in)
    c_onesr = s.dma("gpsimd", ones_row[:], onesr_in)
    s.emit("vector", lambda e: e.memset(amax_xn[:], 0.0), inc="ve")
    s.emit("vector", lambda e: e.memset(amaxw[:], 0.0), inc="ve")
    s.emit("vector", lambda e: e.memset(amax_t[:], 0.0), inc="ve")
    s.emit("vector", lambda e: e.memset(scal[:], 0.0), inc="ve")
    ve_init = s.cnt["ve"]
    lp_units = []
    for t in range(24):
        lp_units.append((rr(w_up)[:, t, :], rr(upq_sp)[:, t, :]))
    for t in range(24):
        lp_units.append((rr(w_gate)[:, t, :], rr(gateq_sp)[:, t, :]))
    for t in range(8):
        for cc3 in range(3):
            lp_units.append((rr(w_down)[:, t, cc3 * HID:(cc3 + 1) * HID],
                             rr(downq_sp)[:, t, cc3 * HID:(cc3 + 1) * HID]))
    lp_prev_v = 0
    lp_prev_sp = 0
    lp_prev_sc = 0
    for ui, (src_ap, dst_ap) in enumerate(lp_units):
        ldw = s.dma("sync", wraw[:], src_ap,
                    waits=[("ve", lp_prev_v)], inc="sQ")
        def lpq_a(e):
            e.tensor_reduce(wsc[:, 0:32], wraw[:].rearrange("p (b k) -> p b k", k=32),
                            axis=AX.X, op=ALU.max, apply_absolute_value=True)
            e.tensor_copy(tmpcol[:], cstb[:, 0:1])
            return e.tensor_scalar(wsc[:, 32:64], wsc[:, 0:32], 1.0 / 7.0, 1e-12,
                                   op0=ALU.mult, op1=ALU.max)
        lv_a = s.emit("vector", lpq_a, waits=[("sQ", ldw), ("gQ", lp_prev_sp),
                                              ("sc", lp_prev_sc)], inc="ve")
        lv_s = s.emit("scalar", lambda e: (
            e.activation(wsc[:, 96:128], wsc[:, 32:64], ACT.Ln),
            e.activation(wsc[:, 64:96], wsc[:, 96:128], ACT.Exp, scale=-1.0))[-1],
            waits=[("ve", lv_a)], inc="sc")
        lp_prev_sc = lv_s
        def lpq_b(e):
            e.tensor_mul(wsc[:, 96:128], wsc[:, 32:64], wsc[:, 64:96])
            gap(e)
            e.tensor_scalar(wsc[:, 96:128], wsc[:, 96:128], -1.0, 2.0,
                            op0=ALU.mult, op1=ALU.add)
            gap(e)
            e.tensor_mul(wsc[:, 128:160], wsc[:, 64:96], wsc[:, 96:128])
            gap(e)
            e.scalar_tensor_tensor(wraw[:].rearrange("p (b k) -> p b k", k=32),
                                   wraw[:].rearrange("p (b k) -> p b k", k=32), 1.0,
                                   wsc[:, 128:160].unsqueeze(2).broadcast_to([128, 32, 32]),
                                   op0=ALU.mult, op1=ALU.mult)
            e.tensor_scalar(wraw[:], wraw[:], MAGIC, None, op0=ALU.add)
            return e.scalar_tensor_tensor(
                wqt[:].rearrange("p (b k) -> p b k", k=32),
                wraw[:].rearrange("p (b k) -> p b k", k=32), -MAGIC,
                wsc[:, 32:64].unsqueeze(2).broadcast_to([128, 32, 32]),
                op0=ALU.add, op1=ALU.mult)
        lp_prev_v = s.emit("vector", lpq_b, waits=[("sc", lv_s)], inc="ve")
        lp_prev_sp = s.dma("gpsimd", dst_ap, wqt[:], waits=[("ve", lp_prev_v)], inc="gQ")
    lpq_done = lp_prev_sp

    g_upw = 0
    for kc in range(8):
        g_upw = s.xdma("sync", lambda e, kc=kc: e.dma_start_transpose(
            wsb[:, kc, :], upq_sp[:, kc * 128:(kc + 1) * 128]),
            waits=[("gQ", lpq_done)], inc="sQ")
    _dummy = s.dma("gpsimd", wsb[:, 0:1, 0:1], upq_sp[0:1, 0:1]) if False else None


    ln_xn = [0] * len(GRP)
    ln_spill = [0] * len(GRP)
    stats_pe = [0] * len(GRP)
    bc_pe = [0] * len(GRP)
    rows_ve = [0] * len(GRP)

    for g, (t0, tw, own) in enumerate(GRP):
        buf = 0
        w = [] if g < 1 else [("ve", ln_xn[g - 1])]
        ld = s.dma("sync", xg[buf][:, :, 0:tw], rr(x_fm)[:, :, t0:t0 + tw],
                   waits=w, inc="sQ")
        sqw = [("sQ", ld)] + ([("ve", ln_xn[g - 1])] if g >= 1 else [])
        sq = s.emit("scalar", lambda e, b=buf, tw=tw: e.activation(
            x2g[:, :, 0:tw], xg[b][:, :, 0:tw], ACT.Square), waits=sqw, inc="sc")
        def statmm(e, b=buf, tw=tw):
            for c in range(8):
                e.matmul(ps[4][0:1, 0:tw], f32r(ones_col[:]), f32r(xg[b][:, c, 0:tw]),
                         start=(c == 0), stop=(c == 7))
            for c in range(8):
                i = e.matmul(ps[5][0:1, 0:tw], f32r(ones_col[:]), f32r(x2g[:, c, 0:tw]),
                             start=(c == 0), stop=(c == 7))
            return i
        mmw = [("sc", sq)] + ([("ve", rows_ve[g - 1])] if g >= 1 else [])
        stats_pe[g] = s.emit("tensor", statmm, waits=mmw, inc="pe")
        def rows1(e, tw=tw):
            e.tensor_scalar(rowb[0:1, 0:tw], ps[4][0:1, 0:tw], 1.0 / HID, None, op0=ALU.mult)
            e.tensor_scalar(rowb[0:1, GW:GW + tw], ps[5][0:1, 0:tw], 1.0 / HID, None, op0=ALU.mult)
            gap(e)
            e.tensor_mul(rowb[0:1, 2 * GW:2 * GW + tw], rowb[0:1, 0:tw], rowb[0:1, 0:tw])
            gap(e)
            return e.tensor_sub(rowb[0:1, GW:GW + tw], rowb[0:1, GW:GW + tw],
                                rowb[0:1, 2 * GW:2 * GW + tw])
        v1 = s.emit("vector", rows1, waits=[("pe", stats_pe[g])], inc="ve")
        def rows2(e, tw=tw):
            e.activation(rowb[0:1, 2 * GW:2 * GW + tw], rowb[0:1, GW:GW + tw], ACT.Ln, bias=1e-5)
            return e.activation(rowb[0:1, 3 * GW:3 * GW + tw], rowb[0:1, 2 * GW:2 * GW + tw],
                                ACT.Exp, scale=-0.5)
        s1 = s.emit("scalar", rows2, waits=[("ve", v1)], inc="sc")
        rows_ve[g] = s.emit("vector", lambda e, tw=tw: e.scalar_tensor_tensor(
            rowb[0:1, 2 * GW:2 * GW + tw], rowb[0:1, 0:tw], -1.0,
            rowb[0:1, 3 * GW:3 * GW + tw], op0=ALU.mult, op1=ALU.mult),
            waits=[("sc", s1)], inc="ve")
        def bc(e, tw=tw):
            e.matmul(ps[6][:, 0:tw], f32r(ones_row[:]), f32r(rowb[0:1, 3 * GW:3 * GW + tw]),
                     start=True, stop=True)
            return e.matmul(ps[7][:, 0:tw], f32r(ones_row[:]), f32r(rowb[0:1, 2 * GW:2 * GW + tw]),
                            start=True, stop=True)
        bcw = [("ve", rows_ve[g])] + ([("ve", ln_xn[g - 1])] if g >= 1 else [])
        bc_pe[g] = s.emit("tensor", bc, waits=bcw, inc="pe")
        gi = g - 1
        def xnmk(e, b=buf, tw=tw, own=own, gi=gi):
            for c in range(8):
                e.tensor_mul(x2g[:, c, 0:tw], xg[b][:, c, 0:tw], ps[6][:, 0:tw])
                e.tensor_add(xng[b][:, c, 0:tw], x2g[:, c, 0:tw], ps[7][:, 0:tw])
            last = None
            if own:
                for c in range(8):
                    last = e.tensor_reduce(amaxw[:, gi * 8 + c:gi * 8 + c + 1],
                                           xng[b][:, c, 0:tw], axis=AX.X,
                                           op=ALU.max, apply_absolute_value=True)
            if last is None:
                last = e.tensor_copy(tmpcol[:], cstb[:, 0:1])
            return last
        xnw = [("pe", bc_pe[g]), ("ve", ve_init)]
        if g >= 1:
            xnw.append(("gQ", ln_spill[g - 1]))
        ln_xn[g] = s.emit("vector", xnmk, waits=xnw, inc="ve")
        ln_spill[g] = s.dma("gpsimd", rr(xn_sp)[:, :, t0:t0 + tw], xng[buf][:, :, 0:tw],
                            waits=[("ve", ln_xn[g])], inc="gQ")

    def ar_round(slot_vals, tag):
        vw = [s.cnt["ve"]]
        for slot, col in slot_vals:
            g1 = s.xdma("sync", lambda e, c=col: e.dma_start(out=redrow[:], in_=c),
                        waits=[("ve", max(vw))], inc="sQ")
            vv = s.emit("vector", lambda e, sl=slot: e.tensor_reduce(
                scal[0:1, sl:sl + 1], redrow[:], axis=AX.X, op=ALU.max),
                waits=[("sQ", g1)], inc="ve")
            vw.append(vv)
        gin = s.dma("gpsimd", cc_in, scal[0:1, 0:8], waits=[("ve", max(vw))])
        ccv = s.emit("gpsimd", lambda e: e.collective_compute(
            "AllReduce", ALU.max, replica_groups=[list(range(NCORE))],
            ins=[cc_in], outs=[cc_out]), waits=[("gQ", gin)], inc="cc")
        gout = s.dma("gpsimd", ccs[:], cc_out, waits=[("cc", ccv)])
        return gout

    fin_xn = s.emit("vector", lambda e: (
        e.tensor_copy(tmpcol[:], cstb[:, 0:1]),
        e.tensor_reduce(amax_xn[:], amaxw[:, 0:64], axis=AX.X, op=ALU.max),
        e.tensor_copy(tmpcol[:], cstb[:, 0:1]))[-1],
        waits=[("ve", ln_xn[-1])], inc="ve")
    g_ar1 = ar_round([(0, amax_xn[:, 0:1])], "xn")
    def mkscale(slot, bc0, bc1, wait_gq):
        sv = s.emit("scalar", lambda e, sl=slot: (
            e.activation(scal[0:1, 11:12], ccs[0:1, sl:sl + 1], ACT.Ln),
            e.activation(scal[0:1, 8:9], scal[0:1, 11:12], ACT.Exp, scale=-1.0))[-1],
            waits=[("gQ", wait_gq), ("pe", s.cnt["pe"]), ("ve", s.cnt["ve"])], inc="sc")
        def nrfix(e, sl=slot):
            e.tensor_mul(scal[0:1, 12:13], ccs[0:1, sl:sl + 1], scal[0:1, 8:9])
            gap(e)
            e.tensor_scalar(scal[0:1, 12:13], scal[0:1, 12:13], -1.0, 2.0,
                            op0=ALU.mult, op1=ALU.add)
            gap(e)
            e.tensor_mul(scal[0:1, 13:14], scal[0:1, 8:9], scal[0:1, 12:13])
            gap(e)
            return e.tensor_copy(scal[0:1, 8:9], scal[0:1, 13:14])
        vv = s.emit("vector", nrfix, waits=[("sc", sv)], inc="ve")
        def sc2(e, sl=slot):
            gap(e)
            e.tensor_scalar(scal[0:1, 9:10], scal[0:1, 8:9], QMAX, None, op0=ALU.mult)
            e.tensor_scalar(scal[0:1, 10:11], ccs[0:1, sl:sl + 1], 1.0 / QMAX, None,
                            op0=ALU.mult)
            return gap(e) or e.tensor_copy(tmpcol[:], cstb[:, 0:1])
        vv = s.emit("vector", sc2, waits=[("ve", vv)], inc="ve")
        mm = s.emit("tensor", lambda e: (
            e.matmul(ps[6][:, 0:1], f32r(ones_row[:]), f32r(scal[0:1, 9:10]), start=True, stop=True),
            e.matmul(ps[6][:, 1:2], f32r(ones_row[:]), f32r(scal[0:1, 10:11]), start=True, stop=True))[-1],
            waits=[("ve", vv)], inc="pe")
        return s.emit("vector", lambda e, b0=bc0, b1=bc1: (
            e.tensor_copy(sbc[:, b0:b0 + 1], ps[6][:, 0:1]),
            e.tensor_copy(sbc[:, b1:b1 + 1], ps[6][:, 1:2]))[-1],
            waits=[("pe", mm)], inc="ve")

    sx_ve = mkscale(0, 0, 1, g_ar1)

    x1_ve = [0] * NGRP
    for g in range(NGRP):
        buf = 0
        t0 = HALO + g * GW
        ld = s.dma("sync", xng[buf][:], rr(xn_sp)[:, :, t0:t0 + GW],
                   waits=[("ve", sx_ve if g < 1 else x1_ve[g - 1]),
                          ("gQ", ln_spill[-1])], inc="sQ")
        def x1mk(e, b=buf, g=g):
            for c in range(8):
                e.tensor_scalar(tq[0][:, :], xng[b][:, c, :], sbc[:, 0:1], MAGIC,
                                op0=ALU.mult, op1=ALU.add)
                i = e.tensor_scalar(x1sb[:, c, g * GW:(g + 1) * GW], tq[0][:, :],
                                    -MAGIC, sbc[:, 1:2], op0=ALU.add, op1=ALU.mult)
            return i
        x1_ve[g] = s.emit("vector", x1mk, waits=[("sQ", ld), ("ve", sx_ve)], inc="ve")

    def mm_3072(spill, amax_slot, wready, tag):
        tile_ev = {}
        evacof = {}
        pe_last = 0
        sp_last = 0
        idx = 0
        for oc in range(24):
            for g in range(NGRP):
                bank = idx % 4
                def mmfn(e, oc=oc, g=g, bank=bank):
                    for kc in range(8):
                        i = e.matmul(ps[bank][:, :], wsb[:, kc, oc * 128:(oc + 1) * 128],
                                     x1sb[:, kc, g * GW:(g + 1) * GW],
                                     start=(kc == 0), stop=(kc == 7))
                    return i
                mw = [("ve", x1_ve[g])] + wready
                if idx >= 2:
                    mw.append(tile_ev[idx - 2])
                if idx >= 4:
                    mw.append(evacof[idx - 4])
                pe_last = s.emit("tensor", mmfn, waits=mw, inc="pe")
                eng = "scalar" if (idx % 2 == 0) else "vector"
                eb = ev[idx % 2]
                if eng == "scalar":
                    ecnt = s.emit("scalar", lambda e, b=bank, eb=eb: e.copy(eb[:], ps[b][:, :]),
                                  waits=[("pe", pe_last)], inc="sc")
                    evacof[idx] = ("sc", ecnt)
                else:
                    ecnt = s.emit("vector", lambda e, b=bank, eb=eb: e.tensor_copy(eb[:], ps[b][:, :]),
                                  waits=[("pe", pe_last)], inc="ve")
                    evacof[idx] = ("ve", ecnt)
                am = s.emit("vector", lambda e, eb=eb, ix=idx: e.tensor_reduce(
                    amaxw[:, ix:ix + 1], eb[:], axis=AX.X, op=ALU.max,
                    apply_absolute_value=True), waits=[evacof[idx]], inc="ve")
                sp_last = s.dma("gpsimd", rr(spill)[:, oc, g * GW:(g + 1) * GW], eb[:],
                                waits=[("ve", am)], inc="gQ")
                tile_ev[idx] = ("gQ", sp_last)
                idx += 1
        fin = s.emit("vector", lambda e, sl=amax_slot: (
            e.tensor_copy(tmpcol[:], cstb[:, 0:1]),
            e.tensor_reduce(amax_t[:, sl:sl + 1], amaxw[:, 0:192], axis=AX.X, op=ALU.max),
            e.tensor_copy(tmpcol[:], cstb[:, 0:1]),
            e.memset(amaxw[:, 0:192], 0.0))[-1], inc="ve")
        return pe_last, sp_last

    up_pe, up_sp_done = mm_3072(up_sp, 0, [("sQ", g_upw)], "up")
    g_gw = 0
    for kc in range(8):
        g_gw = s.xdma("sync", lambda e, kc=kc: e.dma_start_transpose(
            wsb[:, kc, :], gateq_sp[:, kc * 128:(kc + 1) * 128]),
            waits=[("pe", up_pe), ("gQ", lpq_done)], inc="sQ")
    gate_pe, gate_sp_done = mm_3072(gate_sp, 1, [("sQ", g_gw)], "gate")
    wd = wsb[:].rearrange("p c o -> p (c o)").rearrange(
        "p (c o) -> p c o", o=HID)
    g_dnw = 0
    for kc in range(24):
        g_dnw = s.xdma("sync", lambda e, kc=kc: e.dma_start_transpose(
            wd[:, kc, :], downq_sp[:, kc * 128:(kc + 1) * 128]),
            waits=[("pe", gate_pe)], inc="sQ")

    g_ar2 = ar_round([(1, amax_t[:, 0:1]), (2, amax_t[:, 1:2])], "upgate")
    sup_ve = mkscale(1, 2, 3, g_ar2)
    sgate_ve = mkscale(2, 4, 5, g_ar2)

    pa_done = 0
    pa_v1_prev = 0
    for oc in range(24):
        for g in range(NGRP):
            ld = s.dma("sync", tq[0][:], rr(gate_sp)[:, oc, g * GW:(g + 1) * GW],
                       waits=[("ve", max(sup_ve, sgate_ve, pa_v1_prev)),
                              ("gQ", gate_sp_done)], inc="sQ")
            def pa_v1(e):
                e.tensor_scalar(tq[1][:], tq[0][:], sbc[:, 4:5], MAGIC, op0=ALU.mult, op1=ALU.add)
                return e.tensor_scalar(tq[2][:], tq[1][:], -MAGIC, sbc[:, 5:6],
                                       op0=ALU.add, op1=ALU.mult)
            v1 = s.emit("vector", pa_v1, waits=[("sQ", ld)], inc="ve")
            pa_v1_prev = v1
            sg = s.emit("scalar", lambda e: e.activation(tq[3][:], tq[2][:], ACT.Sigmoid),
                        waits=[("ve", v1)], inc="sc")
            def pa_v2(e, ix=oc * NGRP + g):
                e.tensor_scalar(tq[4][:], tq[3][:], 65536.0, MAGIC, op0=ALU.mult, op1=ALU.add)
                e.tensor_scalar(tq[1][:], tq[4][:], -MAGIC, 65535.0, op0=ALU.add, op1=ALU.min)
                e.scalar_tensor_tensor(tq[5][:], tq[1][:], 1.0 / 65536.0, tq[2][:],
                                       op0=ALU.mult, op1=ALU.mult)
                return e.tensor_reduce(amaxw[:, ix:ix + 1], tq[5][:], axis=AX.X,
                                       op=ALU.max, apply_absolute_value=True)
            v2 = s.emit("vector", pa_v2, waits=[("sc", sg), ("gQ", pa_done)], inc="ve")
            pa_done = s.dma("gpsimd", rr(g2p_sp)[:, oc, g * GW:(g + 1) * GW], tq[5][:],
                            waits=[("ve", v2)], inc="gQ")

    s.emit("vector", lambda e: (
        e.tensor_copy(tmpcol[:], cstb[:, 0:1]),
        e.tensor_reduce(amax_t[:, 2:3], amaxw[:, 0:192], axis=AX.X, op=ALU.max),
        e.tensor_copy(tmpcol[:], cstb[:, 0:1]),
        e.memset(amaxw[:, 0:192], 0.0))[-1], inc="ve")
    g_ar3 = ar_round([(3, amax_t[:, 2:3])], "g2")
    sg2_ve = mkscale(3, 6, 7, g_ar3)

    pb_done = 0
    pb_v_prev = 0
    for oc in range(24):
        for g in range(NGRP):
            ld1 = s.dma("sync", tq[0][:], rr(g2p_sp)[:, oc, g * GW:(g + 1) * GW],
                        waits=[("ve", max(sg2_ve, pb_v_prev)), ("gQ", pa_done)], inc="sQ")
            ld2 = s.dma("sync", tq[1][:], rr(up_sp)[:, oc, g * GW:(g + 1) * GW],
                        waits=[("gQ", pb_done)], inc="sQ")
            def pb_v(e, ix=oc * NGRP + g):
                e.tensor_scalar(tq[2][:], tq[0][:], sbc[:, 6:7], MAGIC, op0=ALU.mult, op1=ALU.add)
                e.tensor_scalar(tq[3][:], tq[2][:], -MAGIC, sbc[:, 7:8], op0=ALU.add, op1=ALU.mult)
                e.tensor_scalar(tq[4][:], tq[1][:], sbc[:, 2:3], MAGIC, op0=ALU.mult, op1=ALU.add)
                e.tensor_scalar(tq[5][:], tq[4][:], -MAGIC, sbc[:, 3:4], op0=ALU.add, op1=ALU.mult)
                e.tensor_mul(tq[1][:], tq[3][:], tq[5][:])
                return e.tensor_reduce(amaxw[:, ix:ix + 1], tq[1][:], axis=AX.X,
                                       op=ALU.max, apply_absolute_value=True)
            vb = s.emit("vector", pb_v, waits=[("sQ", ld2), ("gQ", pb_done)], inc="ve")
            pb_v_prev = vb
            pb_done = s.dma("gpsimd", rr(o2p_sp)[:, oc, g * GW:(g + 1) * GW], tq[1][:],
                            waits=[("ve", vb)], inc="gQ")

    s.emit("vector", lambda e: (
        e.tensor_copy(tmpcol[:], cstb[:, 0:1]),
        e.tensor_reduce(amax_t[:, 3:4], amaxw[:, 0:192], axis=AX.X, op=ALU.max),
        e.tensor_copy(tmpcol[:], cstb[:, 0:1]))[-1], inc="ve")
    g_ar4 = ar_round([(4, amax_t[:, 3:4])], "o2")
    so2_ve = mkscale(4, 8, 9, g_ar4)

    dn_last = 0
    out_done = 0
    for g in range(NGRP):
        half_pe = []
        for half in range(2):
            qv = 0
            for kk in range(12):
                kc = half * 12 + kk
                ld = s.dma("sync", tq[kc % 2][:], rr(o2p_sp)[:, kc, g * GW:(g + 1) * GW],
                           waits=[("ve", so2_ve), ("gQ", pb_done)] +
                                 ([("ve", qv)] if kk >= 2 else []) +
                                 ([("pe", dn_last)] if kk < 2 else []), inc="sQ")
                def qc(e, kc=kc, kk=kk):
                    e.tensor_scalar(tq[2 + kc % 2][:], tq[kc % 2][:], sbc[:, 8:9], MAGIC,
                                    op0=ALU.mult, op1=ALU.add)
                    return e.tensor_scalar(o2q[:, kk, :], tq[2 + kc % 2][:], -MAGIC,
                                           sbc[:, 9:10], op0=ALU.add, op1=ALU.mult)
                qv = s.emit("vector", qc, waits=[("sQ", ld)], inc="ve")
            for oc in range(8):
                bank = oc
                def dnmm(e, oc=oc, bank=bank, half=half):
                    for kk in range(12):
                        kc = half * 12 + kk
                        i = e.matmul(ps[bank][:, :], wd[:, kc, oc * 128:(oc + 1) * 128],
                                     o2q[:, kk, :], start=(kc == 0), stop=(kc == 23))
                    return i
                mw = [("ve", qv), ("sQ", g_dnw)]
                if half == 1 and oc >= 4:
                    mw.append(("gQ", out_done))
                dn_pe = s.emit("tensor", dnmm, waits=mw, inc="pe")
                dn_last = dn_pe
                half_pe.append(dn_pe)
        xb = xg[0]
        t0 = HALO + g * GW
        ldx = s.dma("sync", xb[:], rr(x_fm)[:, :, t0:t0 + GW], waits=[("ve", qv)], inc="sQ")
        for oc in range(8):
            bank = oc
            rv = s.emit("vector", lambda e, oc=oc, bank=bank, xb=xb: e.tensor_add(
                ev[oc % 2][:], ps[bank][:, :], xb[:, oc, :]),
                waits=[("pe", half_pe[8 + oc]), ("sQ", ldx)] +
                      ([("gQ", out_done)] if oc >= 2 else []), inc="ve")
            out_done = s.dma("gpsimd", rr(out_fm)[:, oc, g * GW:(g + 1) * GW], ev[oc % 2][:],
                             waits=[("ve", rv)], inc="gQ")

    if dbg:
        s.dma("gpsimd", dbg_sc[0:1, 0:12], sbc[0:1, 0:12],
              waits=[("ve", s.cnt["ve"])])

    ATTN(locals())

    s.build(nc, sh)
    return nc




def ATTN(L):
    nc = L["nc"]; s = L["s"]; E = L["E"]; ps = L["ps"]
    gap = L["gap"]; ones_col = L["ones_col"]; ones_row = L["ones_row"]
    xn_sp = L["xn_sp"]; out_fm = L["out_fm"]; x_fm = L["x_fm"]
    cstb = L["cstb"]; tmpcol = L["tmpcol"]; tq = L["tq"]; ev = L["ev"]
    ffn_es = L["ffn_es"]; dbg = L["dbg"]
    from contextlib import ExitStack

    DI = lambda n, sh_, dt_: nc.dram_tensor(n, sh_, dt_, kind="ExternalInput").ap()
    DN = lambda n, sh_, dt_: nc.dram_tensor(n, sh_, dt_, kind="Internal").ap()
    wqkvbT = DI("wqkvbT", [HID, 1540], F32)
    bands_in = DI("bands_in", [6, 512, 256, 128], F32)
    o_wT_in = DI("o_wT_in", [HID, HID], F32)
    g1xT_in = DI("g1xT_in", [HID, 2048], F32)
    g1sT_in = DI("g1sT_in", [64, 2048], F32)
    g1b_in = DI("g1b_in", [128, 16], F32)
    g2T_in = DI("g2T_in", [2048, 32], F32)
    g2b_in = DI("g2b_in", [32, 1], F32)
    vb_in = DI("vb_in", [32, 1], F32)
    ltmask_in = DI("ltmask_in", [4, 32], F32)
    fmask_in = DI("fmask_in", [4, 32], F32)
    ltvec_in = DI("ltvec_in", [4, 1], F32)
    flvec_in = DI("flvec_in", [4, 1], F32)
    onormc_in = DI("onormc_in", [128, 8], F32)
    m_bdtril_in = DI("m_bdtril_in", [128, 512], F32)
    m_ipack_in = DI("m_ipack_in", [128, 512], F32)
    m_sbu_in = DI("m_sbu_in", [128, 128], F32)
    m_triu_in = DI("m_triu_in", [128, 128], F32)
    ident_in = DI("ident_in", [128, 128], F32)
    mh_in = DI("mh_in", [128, 2], F32)

    xnb_sp = DN("xnb_sp", [HID, NT], BF16)
    xnb_ag = DN("xnb_ag", [2, HID, NT], BF16)
    qkv_sp = DN("qkv_sp", [1536, NTP], BF16)
    beta_sp = DN("beta_sp", [2, NTP], F32)
    beta16_sp = DN("beta16_sp", [16, NTP], BF16)
    qc_sp = DN("qc_sp", [512, NTP], BF16)
    kc_sp = DN("kc_sp", [512, NTP], BF16)
    fir4_sp = DN("fir4_sp", [4, 512, NTP], BF16)
    fir4_ag = DN("fir4_ag", [2, 4, 512, NTP], BF16)
    vc_sp = fir4_sp[3]
    khat_sp = DN("khat_sp", [2, 256, GW], BF16)
    vb_sp = DN("vb_sp", [2, 256, GW], BF16)
    do_sp = DN("do_sp", [2, 2, NT, 256], F32)
    do_ag = DN("do_ag", [2, 2, 2, NT, 256], F32)

    fence = [("gQ", s.cnt["gQ"]), ("sQ", s.cnt["sQ"]), ("ve", s.cnt["ve"]),
             ("sc", s.cnt["sc"]), ("pe", s.cnt["pe"])]
    for eng in ENGS:
        s.emit(eng, lambda e: None, waits=fence)

    xcast = s.dma("gpsimd", rr(xnb_sp), rr(xn_sp)[:, :, HALO:], waits=fence)
    agx = s.emit("gpsimd", lambda e: e.collective_compute(
        "AllGather", ALU.bypass, replica_groups=PAIRS,
        ins=[xnb_sp], outs=[xnb_ag]), waits=[("gQ", xcast)], inc="cc")

    ffn_es.close()
    ident = E(nc.sbuf_tensor("ident", [128, 128], BF16))
    identf = E(nc.sbuf_tensor("identf", [128, 128], F32))
    p12_es = ExitStack()
    E12 = p12_es.enter_context
    wp = E12(nc.sbuf_tensor("wp", [128, 8, 1540], BF16))
    xnb = E12(nc.sbuf_tensor("xnb", [128, 8, GW], BF16))
    evb = [E12(nc.sbuf_tensor(f"evb{i}", [128, GW], BF16)) for i in range(4)]
    pdat = [E12(nc.sbuf_tensor(f"pdat{i}", [128, 64 * 66], BF16)) for i in range(2)]
    zt16 = E12(nc.sbuf_tensor("zt16", [128, 256], BF16))
    zt32 = E12(nc.sbuf_tensor("zt32", [128, 256], F32))
    bvt = E12(nc.sbuf_tensor("bvt", [2, GW], F32))
    m_bdtril = E12(nc.sbuf_tensor("m_bdtril", [128, 512], F32))
    m_ipack = E12(nc.sbuf_tensor("m_ipack", [128, 512], BF16))
    m_sbu = E12(nc.sbuf_tensor("m_sbu", [128, 128], F32))
    m_triu = E12(nc.sbuf_tensor("m_triu", [128, 128], F32))

    bndj = [[E12(nc.sbuf_tensor(f"bnd{j}_{i}", [128, 2, 8, 128], BF16)) for i in range(2)]
            for j in range(3)]
    p3_es = ExitStack()
    E3 = p3_es.enter_context
    brow = L["rowb"]
    qf = E3(nc.sbuf_tensor("qf", [128, 2, GW], BF16))
    kf = E3(nc.sbuf_tensor("kf", [128, 2, GW], BF16))
    vf = E3(nc.sbuf_tensor("vf", [128, 2, GW], BF16))
    qh_ = E3(nc.sbuf_tensor("qh_", [128, 2, GW], BF16))
    kh_ = E3(nc.sbuf_tensor("kh_", [128, 2, GW], BF16))
    kbh = E3(nc.sbuf_tensor("kbh", [128, 2, GW], BF16))
    vbf = E3(nc.sbuf_tensor("vbf", [128, 2, GW], BF16))
    ktm = E3(nc.sbuf_tensor("ktm", [128, 4, 256], BF16))
    kbtm = E3(nc.sbuf_tensor("kbtm", [128, 4, 256], BF16))
    vbtm = E3(nc.sbuf_tensor("vbtm", [128, 4, 256], BF16))
    bcol = E3(nc.sbuf_tensor("bcol", [128, 4, 16], BF16))
    bcolf = E3(nc.sbuf_tensor("bcolf", [128, 4, 16], F32))
    sq2 = E3(nc.sbuf_tensor("sq2", [128, 2, GW], F32))
    apack = E3(nc.sbuf_tensor("apack", [128, 512], BF16))
    tpk = [E3(nc.sbuf_tensor(f"tpk{i}", [128, 512], BF16)) for i in range(4)]
    ppk = [E3(nc.sbuf_tensor(f"ppk{i}", [128, 512], BF16)) for i in range(5)]
    prt = [E3(nc.sbuf_tensor(f"prt{i}", [128, 512], BF16)) for i in range(2)]
    utm = E3(nc.sbuf_tensor("utm", [128, 4, 256], BF16))
    wtm = E3(nc.sbuf_tensor("wtm", [128, 4, 256], BF16))
    wnf = E3(nc.sbuf_tensor("wnf", [128, 2, GW], BF16))
    s0b = [E3(nc.sbuf_tensor(f"s0b{i}", [128, 512], BF16)) for i in range(2)]
    y0t = E3(nc.sbuf_tensor("y0t", [128, 256], BF16))
    t1t = E3(nc.sbuf_tensor("t1t", [128, 256], BF16))
    t2t = E3(nc.sbuf_tensor("t2t", [128, 256], BF16))
    uadj = E3(nc.sbuf_tensor("uadj", [128, 256], BF16))
    nnt = E3(nc.sbuf_tensor("nnt", [128, 128], BF16))
    attt = E3(nc.sbuf_tensor("attt", [128, 128], BF16))
    oev = E3(nc.sbuf_tensor("oev", [128, 256], F32))


    zz = s.emit("vector", lambda e: (e.memset(zt16[:], 0.0),
                                     e.memset(zt32[:], 0.0))[-1], inc="ve")
    cl = 0
    for ap_s, ap_d in [(m_bdtril_in, m_bdtril[:]), (m_ipack_in, m_ipack[:]),
                       (m_sbu_in, m_sbu[:]), (m_triu_in, m_triu[:]),
                       (ident_in, ident[:]), (ident_in, identf[:])]:
        cl = s.dma("gpsimd", ap_d, ap_s, waits=fence)
    wpl = s.dma("gpsimd", wp[:], rr(wqkvbT), waits=fence)

    zp = 0
    for oc in range(12):
        zp = s.dma("gpsimd", rr(qkv_sp)[:, oc, 0:PADL], zt16[:, 0:PADL], waits=[("ve", zz)])
        zp = s.dma("gpsimd", rr(qkv_sp)[:, oc, NTP - PADR:NTP], zt16[:, 0:PADR],
                   waits=[("ve", zz)])
    zp = s.dma("gpsimd", beta_sp[0:2, 0:PADL], zt32[0:2, 0:PADL],
               waits=[("ve", zz)])
    zp = s.dma("gpsimd", beta_sp[0:2, NTP - PADR:NTP], zt32[0:2, 0:PADR],
               waits=[("ve", zz)])
    for c0 in range(NTP // 256):
        zp = s.dma("gpsimd", beta16_sp[:, c0 * 256:c0 * 256 + 256],
                   zt16[0:16, 0:256], waits=[("ve", zz)])

    p1sp = zp
    p1ev = {}
    for g in range(16):
        sl, off = g // 8, (g % 8) * GW
        ldr = s.dma("sync", xnb[:], rr(xnb_ag[sl])[:, :, off:off + GW],
                    waits=[("cc", agx), ("ve", p1ev.get(g - 1, 0))], inc="sQ")
        last = 0
        for oc in range(13):
            bank = oc % 2
            mwid = 128 if oc < 12 else 2
            def p1mm(e, oc=oc, bank=bank, mwid=mwid):
                for kc in range(8):
                    i = e.matmul(ps[bank][0:mwid, :], wp[:, kc, oc * 128:oc * 128 + mwid],
                                 xnb[:, kc, :], start=(kc == 0), stop=(kc == 7))
                return i
            pw = [("sQ", ldr), ("gQ", wpl), ("gQ", p1sp)]
            mm = s.emit("tensor", p1mm, waits=pw, inc="pe")
            tcol = PADL + g * GW
            if oc < 12:
                vv = s.emit("vector", lambda e, b=bank, oc=oc: e.tensor_copy(
                    evb[oc % 4][:], ps[b][:, :]), waits=[("pe", mm)], inc="ve")
                last = vv
                p1sp = s.dma("gpsimd", rr(qkv_sp)[:, oc, tcol:tcol + GW],
                             evb[oc % 4][:], waits=[("ve", vv)], inc="gQ")
            else:
                sg = s.emit("scalar", lambda e, b=bank: e.activation(
                    bvt[0:2, :], ps[b][0:2, :], ACT.Sigmoid),
                    waits=[("pe", mm)], inc="sc")
                p1sp = s.dma("gpsimd", beta_sp[0:2, tcol:tcol + GW], bvt[0:2, :],
                             waits=[("sc", sg)], inc="gQ")
                p1sp = s.dma("gpsimd", beta16_sp[0:2, tcol:tcol + GW], bvt[0:2, :],
                             waits=[("sc", sg)], inc="gQ")
        p1ev[g] = last
    p1_done = p1sp

    NW2 = NTP // 128

    def conv_stream(src_sp, jobs, wait_gq, tag):
        done = 0
        pdat_guard = {}
        bank_guard = {}
        spill_last = 0
        evb_guard = {}
        for cg in range(8):
            buf = cg % 2
            trw = [("gQ", wait_gq), ("pe", pdat_guard.get(cg - 2, 0)),
                   ("gQ", spill_last)]
            tr = s.xdma("sync", lambda e, cg=cg, buf=buf, src=src_sp: e.dma_start_transpose(
                pdat[buf][:], src[cg * 64:cg * 64 + 64, :].rearrange(
                    "c (w x) -> (c w) x", x=128)), waits=trw, inc="sQ")
            for c8 in range(8):
                bpar = (cg * 8 + c8) % 2
                bls = []
                for j, (fid, dst, sil) in enumerate(jobs):
                    for hh2 in range(2):
                        bl = s.dma("gpsimd", bndj[j][bpar][:, hh2],
                                   bands_in[fid, cg * 64 + c8 * 8:cg * 64 + c8 * 8 + 8,
                                            hh2 * 128:hh2 * 128 + 128, :]
                                   .rearrange("c p t -> p c t"),
                                   waits=[("pe", bank_guard.get(("b", cg, c8 - 2, j), 0))],
                                   inc="gQ")
                    bank_guard[("b", cg, c8, j)] = 0
                    bls.append(bl)
                mmlast = 0
                for ci in range(8):
                    c = c8 * 8 + ci
                    slot = ci % 4
                    for j, (fid, dst, sil) in enumerate(jobs):
                        bank = 2 + j
                        def cmm(e, buf=buf, c=c, bank=bank, slot=slot, j=j, bpar=bpar,
                                ci=ci):
                            base = c * NW2
                            e.matmul(ps[bank][0:64, slot * 128:slot * 128 + 128],
                                     pdat[buf][:, base:base + 64],
                                     bndj[j][bpar][:, 0, ci, :], start=True, stop=False)
                            return e.matmul(ps[bank][0:64, slot * 128:slot * 128 + 128],
                                            pdat[buf][:, base + 1:base + 65],
                                            bndj[j][bpar][:, 1, ci, :],
                                            start=False, stop=True)
                        mw = [("sQ", tr), ("gQ", bls[j]),
                              bank_guard.get((bank, slot), ("pe", 0))]
                        mm = s.emit("tensor", cmm, waits=mw, inc="pe")
                        mmlast = mm
                        if ci == 7:
                            ew = [("pe", mm), evb_guard.get(j, ("pe", 0))]
                            if sil:
                                ecnt = s.emit("scalar", lambda e, bank=bank, j=j: e.activation(
                                    evb[j][0:64, :], ps[bank][0:64, :], ACT.Silu),
                                    waits=ew, inc="sc")
                                ekey = ("sc", ecnt)
                            else:
                                ecnt = s.emit("vector", lambda e, bank=bank, j=j: e.tensor_copy(
                                    evb[j][0:64, :], ps[bank][0:64, :]),
                                    waits=ew, inc="ve")
                                ekey = ("ve", ecnt)
                            c0 = cg * 64 + c8 * 8 + 4
                            spill_last = s.dma(
                                "gpsimd",
                                dst[c0:c0 + 4, PADL:PADL + 8192].rearrange(
                                    "c (w t) -> w c t", t=128),
                                evb[j][0:64, :].rearrange("w (c t) -> w c t", t=128),
                                waits=[ekey], inc="gQ")
                            for sl_ in range(4):
                                bank_guard[(bank, sl_)] = ("gQ", spill_last)
                            evb_guard[j] = ("gQ", spill_last)
                        elif ci == 3:
                            ew = [("pe", mm), evb_guard.get(j, ("pe", 0))]
                            if sil:
                                ecnt = s.emit("scalar", lambda e, bank=bank, j=j: e.activation(
                                    evb[j][0:64, :], ps[bank][0:64, :], ACT.Silu),
                                    waits=ew, inc="sc")
                                ekey = ("sc", ecnt)
                            else:
                                ecnt = s.emit("vector", lambda e, bank=bank, j=j: e.tensor_copy(
                                    evb[j][0:64, :], ps[bank][0:64, :]),
                                    waits=ew, inc="ve")
                                ekey = ("ve", ecnt)
                            c0 = cg * 64 + c8 * 8
                            spill_last = s.dma(
                                "gpsimd",
                                dst[c0:c0 + 4, PADL:PADL + 8192].rearrange(
                                    "c (w t) -> w c t", t=128),
                                evb[j][0:64, :].rearrange("w (c t) -> w c t", t=128),
                                waits=[ekey], inc="gQ")
                            for sl_ in range(4):
                                bank_guard[(bank, sl_)] = ("gQ", spill_last)
                            evb_guard[j] = ("gQ", spill_last)
                pdat_guard[cg] = mmlast
        return spill_last

    zp2 = 0
    for dst in (qc_sp, kc_sp, fir4_sp[0], fir4_sp[1], fir4_sp[2], fir4_sp[3]):
        for col0, wdt in ((0, PADL), (NTP - PADR, PADR)):
            for cc4 in range(4):
                zp2 = s.dma("gpsimd", rr(dst)[:, cc4, col0:col0 + wdt],
                            zt16[:, 0:wdt], waits=[("ve", zz)])

    qcd = conv_stream(qkv_sp[0:512], [(0, qc_sp, True)], p1_done, "q")
    kcd = conv_stream(qkv_sp[512:1024], [(1, kc_sp, True)], p1_done, "k")
    vcd = conv_stream(qkv_sp[1024:1536], [(2, vc_sp, True)], p1_done, "v")
    fird = conv_stream(vc_sp, [(3, fir4_sp[0], False), (4, fir4_sp[1], False),
                               (5, fir4_sp[2], False)], max(vcd, zp2), "fir")

    agf = s.emit("gpsimd", lambda e: e.collective_compute(
        "AllGather", ALU.bypass, replica_groups=PAIRS,
        ins=[fir4_sp], outs=[fir4_ag]), waits=[("gQ", fird)], inc="cc")

    NSC = 4
    vsent = {}
    dospill = 0
    for h in range(2):
        sinit = s.emit("vector", lambda e, h=h: e.memset(s0b[h][:], 0.0), inc="ve")
        for g in range(16):
            ts0 = PADL + g * GW
            gpar = g % 2
            w_lds = [("gQ", fird), ("pe", s.cnt["pe"]), ("ve", s.cnt["ve"])]
            l1 = s.dma("sync", qf[:], rr(qc_sp[256 * h:256 * h + 256])[:, :, ts0:ts0 + GW],
                       waits=w_lds, inc="sQ")
            l2 = s.dma("sync", kf[:], rr(kc_sp[256 * h:256 * h + 256])[:, :, ts0:ts0 + GW],
                       inc="sQ")
            l3 = s.dma("sync", vf[:], rr(vc_sp[256 * h:256 * h + 256])[:, :, ts0:ts0 + GW],
                       inc="sQ")
            l4 = s.dma("sync", brow[0:1, 0:GW], beta_sp[h:h + 1, ts0:ts0 + GW], inc="sQ")
            sqs = s.emit("scalar", lambda e: e.activation(sq2[:], qf[:], ACT.Square),
                         waits=[("sQ", l1)], inc="sc")
            def ssmm(e):
                for c2 in range(2):
                    i = e.matmul(ps[6][0:1, 0:GW], ones_col[:], sq2[:, c2, :],
                                 start=(c2 == 0), stop=(c2 == 1))
                return i
            mq = s.emit("tensor", ssmm, waits=[("sc", sqs)], inc="pe")
            rq = s.emit("scalar", lambda e: (
                e.activation(brow[0:1, GW:2 * GW], ps[6][0:1, 0:GW], ACT.Ln, bias=1e-6),
                e.activation(brow[0:1, 2 * GW:3 * GW], brow[0:1, GW:2 * GW], ACT.Exp,
                             scale=-0.5))[-1], waits=[("pe", mq)], inc="sc")
            sqs2 = s.emit("scalar", lambda e: e.activation(sq2[:], kf[:], ACT.Square),
                          waits=[("sQ", l2), ("sc", rq)], inc="sc")
            mk = s.emit("tensor", ssmm, waits=[("sc", sqs2), ("sc", rq)], inc="pe")
            rk = s.emit("scalar", lambda e: (
                e.activation(brow[0:1, GW:2 * GW], ps[6][0:1, 0:GW], ACT.Ln, bias=1e-6),
                e.activation(brow[0:1, 3 * GW:4 * GW], brow[0:1, GW:2 * GW], ACT.Exp,
                             scale=-0.5))[-1], waits=[("pe", mk)], inc="sc")
            def bcmm(e):
                e.matmul(ps[6][:, 0:GW], ones_row[:].bitcast(F32), brow[0:1, 2 * GW:3 * GW],
                         start=True, stop=True)
                e.matmul(ps[7][:, 0:GW], ones_row[:].bitcast(F32), brow[0:1, 3 * GW:4 * GW],
                         start=True, stop=True)
                return e.matmul(ps[5][:, 0:GW], ones_row[:].bitcast(F32), brow[0:1, 0:GW],
                                start=True, stop=True)
            bc = s.emit("tensor", bcmm, waits=[("sc", rk)], inc="pe")
            def scl(e):
                for c2 in range(2):
                    e.tensor_mul(qh_[:, c2, :], qf[:, c2, :], ps[6][:, :])
                    e.tensor_mul(kh_[:, c2, :], kf[:, c2, :], ps[7][:, :])
                for c2 in range(2):
                    e.tensor_mul(kbh[:, c2, :], kh_[:, c2, :], ps[5][:, :])
                    i = e.tensor_mul(vbf[:, c2, :], vf[:, c2, :], ps[5][:, :])
                return i
            sc1 = s.emit("vector", scl, waits=[("pe", bc), ("sQ", l3), ("sQ", l4)], inc="ve")
            sp1 = s.dma("gpsimd", rr(khat_sp[gpar]), kh_[:], waits=[("ve", sc1)], inc="gQ")
            sp2 = s.dma("gpsimd", rr(vb_sp[gpar]), vbf[:], waits=[("ve", sc1)], inc="gQ")
            trs = []
            for tt in range(4):
                trs.append(s.xdma("sync", lambda e, tt=tt, gpar=gpar: e.dma_start_transpose(
                    ktm[:, tt, :], khat_sp[gpar][:, tt * 128:tt * 128 + 128]),
                    waits=[("gQ", sp2)], inc="sQ"))
                trs.append(s.xdma("sync", lambda e, tt=tt, gpar=gpar: e.dma_start_transpose(
                    vbtm[:, tt, :], vb_sp[gpar][:, tt * 128:tt * 128 + 128]),
                    waits=[("gQ", sp2)], inc="sQ"))
                trs.append(s.xdma("sync", lambda e, tt=tt, ts0=ts0: e.dma_start_transpose(
                    bcol[:, tt, :], beta16_sp[:, ts0 + tt * 128:ts0 + tt * 128 + 128]),
                    waits=[("gQ", p1_done)], inc="sQ"))
            trd = trs[-1]
            def kbtf(e, h=h):
                e.tensor_copy(bcolf[:].rearrange("p a c -> p (a c)"),
                              bcol[:].rearrange("p a c -> p (a c)"))
                gap(e)
                for tt in range(4):
                    r = e.tensor_scalar(kbtm[:, tt, :], ktm[:, tt, :],
                                        bcolf[:, tt, h:h + 1], None, op0=ALU.mult)
                return r
            kbt = s.emit("vector", kbtf, waits=[("sQ", trd)], inc="ve")
            def amm(e):
                for i in range(16):
                    pk, gb = i // 4, i % 4
                    for c2 in range(2):
                        r = e.matmul(ps[2][32 * gb:32 * gb + 32,
                                           pk * 128 + 32 * gb:pk * 128 + 32 * gb + 32],
                                     kbh[:, c2, 32 * i:32 * i + 32],
                                     kh_[:, c2, 32 * i:32 * i + 32],
                                     start=(c2 == 0), stop=(c2 == 1),
                                     tile_position=(0, 32 * gb))
                return r
            am = s.emit("tensor", amm, waits=[("ve", kbt), ("ve", sc1)], inc="pe")
            av = s.emit("vector", lambda e: e.tensor_mul(apack[:], ps[2][:, :], m_bdtril[:]),
                        waits=[("pe", am)], inc="ve")
            tv = s.emit("vector", lambda e: e.transpose(tpk[0][:], apack[:]),
                        waits=[("ve", av)], inc="ve")
            chain_v = tv
            for lvl in range(1, 5):
                pprev = apack if lvl == 1 else ppk[lvl - 1]
                tprev = tpk[lvl - 1]
                def sqmm(e, pprev=pprev, tprev=tprev, lvl=lvl):
                    r = None
                    for pk in range(4):
                        sl = slice(pk * 128, pk * 128 + 128)
                        r = e.matmul(ps[2][:, sl], tprev[:, sl], pprev[:, sl],
                                     start=True, stop=True)
                        if lvl < 4:
                            r = e.matmul(ps[3][:, sl], pprev[:, sl], tprev[:, sl],
                                         start=True, stop=True)
                    return r
                sq_ = s.emit("tensor", sqmm, waits=[("ve", chain_v)], inc="pe")
                def sqev(e, lvl=lvl):
                    e.tensor_copy(ppk[lvl][:], ps[2][:, :])
                    if lvl < 4:
                        return e.tensor_copy(tpk[lvl][:], ps[3][:, :])
                    return gap(e) or e.tensor_copy(tmpcol[:], cstb[:, 0:1])
                chain_v = s.emit("vector", sqev, waits=[("pe", sq_)], inc="ve")
            pv = s.emit("vector", lambda e: e.tensor_sub(prt[0][:], m_ipack[:], tpk[0][:]),
                        waits=[("ve", chain_v)], inc="ve")
            cur = 0
            for lvl in range(1, 5):
                def prmm(e, lvl=lvl, cur=cur):
                    for pk in range(4):
                        sl = slice(pk * 128, pk * 128 + 128)
                        e.matmul(ps[2][:, sl], ppk[lvl][:, sl], prt[cur][:, sl],
                                 start=True, stop=False)
                        r = e.matmul(ps[2][:, sl], ident[:], prt[cur][:, sl],
                                     start=False, stop=True)
                    return r
                pm = s.emit("tensor", prmm, waits=[("ve", pv)], inc="pe")
                pv = s.emit("vector", lambda e, cur=cur: e.tensor_copy(
                    prt[1 - cur][:], ps[2][:, :]), waits=[("pe", pm)], inc="ve")
                cur = 1 - cur
            def uwmm(e, cur=cur):
                for i in range(16):
                    pk, gb = i // 4, i % 4
                    lh = prt[cur][32 * gb:32 * gb + 32,
                                  pk * 128 + 32 * gb:pk * 128 + 32 * gb + 32]
                    e.matmul(ps[2][32 * gb:32 * gb + 32, (pk % 2) * 256:(pk % 2) * 256 + 256],
                             lh, vbtm[32 * gb:32 * gb + 32, pk, :],
                             start=True, stop=True, tile_position=(32 * gb, 32 * gb))
                    r = e.matmul(ps[3][32 * gb:32 * gb + 32, (pk % 2) * 256:(pk % 2) * 256 + 256],
                                 lh, kbtm[32 * gb:32 * gb + 32, pk, :],
                                 start=True, stop=True, tile_position=(32 * gb, 32 * gb))
                    if pk == 1 or pk == 3:
                        pass
                return r
            uws = []
            for half in range(2):
                def uwmm2(e, cur=cur, half=half):
                    for i in range(8 * half, 8 * half + 8):
                        pk, gb = i // 4, i % 4
                        lh = prt[cur][32 * gb:32 * gb + 32,
                                      pk * 128 + 32 * gb:pk * 128 + 32 * gb + 32]
                        e.matmul(ps[2][32 * gb:32 * gb + 32,
                                       (pk % 2) * 256:(pk % 2) * 256 + 256],
                                 lh, vbtm[32 * gb:32 * gb + 32, pk, :],
                                 start=True, stop=True, tile_position=(32 * gb, 32 * gb))
                        r = e.matmul(ps[3][32 * gb:32 * gb + 32,
                                           (pk % 2) * 256:(pk % 2) * 256 + 256],
                                 lh, kbtm[32 * gb:32 * gb + 32, pk, :],
                                 start=True, stop=True, tile_position=(32 * gb, 32 * gb))
                    return r
                um = s.emit("tensor", uwmm2, waits=[("ve", pv), ("ve", kbt)] +
                            ([("ve", uws[-1])] if uws else []), inc="pe")
                uv = s.emit("vector", lambda e, half=half: (
                    e.tensor_copy(utm[:].rearrange("p a d -> p (a d)")
                                  [:, half * 512:half * 512 + 512], ps[2][:, :]),
                    e.tensor_copy(wtm[:].rearrange("p a d -> p (a d)")
                                  [:, half * 512:half * 512 + 512], ps[3][:, :]))[-1],
                    waits=[("pe", um)], inc="ve")
                uws.append(uv)
            def wtr(e):
                for dkc in range(2):
                    for tt in range(4):
                        r = e.matmul(ps[4][:, tt * 128:tt * 128 + 128],
                                     wtm[:, tt, dkc * 128:dkc * 128 + 128], ident[:],
                                     start=True, stop=True)
                return r
            wv = uws[-1]
            for dkc in range(2):
                def wtr2(e, dkc=dkc):
                    for tt in range(4):
                        r = e.matmul(ps[4][:, tt * 128:tt * 128 + 128],
                                     wtm[:, tt, dkc * 128:dkc * 128 + 128], ident[:],
                                     start=True, stop=True)
                    return r
                wm = s.emit("tensor", wtr2, waits=[("ve", wv)], inc="pe")
                wv = s.emit("vector", lambda e, dkc=dkc: e.tensor_scalar(
                    wnf[:, dkc, :], ps[4][:, :], -1.0, None, op0=ALU.mult),
                    waits=[("pe", wm)], inc="ve")
            prep_done = wv
            for cc in range(4):
                csl = slice(128 * cc, 128 * cc + 128)
                def y0mm(e, cc=cc, h=h, csl=csl):
                    e.matmul(ps[2][:, 0:256], ident[:], utm[:, cc, :], start=True, stop=False)
                    for dkc in range(2):
                        r = e.matmul(ps[2][:, 0:256], wnf[:, dkc, csl],
                                     s0b[h][:, dkc * 256:dkc * 256 + 256],
                                     start=False, stop=(dkc == 1))
                    return r
                m1 = s.emit("tensor", y0mm, waits=[("ve", prep_done),
                                                   ("ve", s.cnt["ve"])], inc="pe")
                v1 = s.emit("vector", lambda e: e.tensor_copy(y0t[:], ps[2][:, 0:256]),
                            waits=[("pe", m1)], inc="ve")
                def ntmm(e, csl=csl):
                    for c2 in range(2):
                        r = e.matmul(ps[3][:, 0:128], kh_[:, c2, csl], wnf[:, c2, csl],
                                     start=(c2 == 0), stop=(c2 == 1))
                    return r
                m2 = s.emit("tensor", ntmm, waits=[("ve", v1)], inc="pe")
                v2 = s.emit("vector", lambda e: e.tensor_mul(nnt[:], ps[3][:, 0:128], m_sbu[:]),
                            waits=[("pe", m2)], inc="ve")
                def neu(e, src, dst):
                    def f(e2):
                        e2.matmul(ps[2][:, 0:256], ident[:], y0t[:], start=True, stop=False)
                        return e2.matmul(ps[2][:, 0:256], nnt[:], src[:], start=False, stop=True)
                    return f
                v_ = v2
                for (srcb, dstb) in ((y0t, t1t), (t1t, t2t), (t2t, uadj)):
                    mn = s.emit("tensor", lambda e, srcb=srcb: (
                        e.matmul(ps[2][:, 0:256], ident[:], y0t[:], start=True, stop=False),
                        e.matmul(ps[2][:, 0:256], nnt[:], srcb[:], start=False, stop=True))[-1],
                        waits=[("ve", v_)], inc="pe")
                    v_ = s.emit("vector", lambda e, dstb=dstb: e.tensor_copy(
                        dstb[:], ps[2][:, 0:256]), waits=[("pe", mn)], inc="ve")
                def atmm(e, csl=csl):
                    for c2 in range(2):
                        r = e.matmul(ps[3][:, 128:256], kh_[:, c2, csl], qh_[:, c2, csl],
                                     start=(c2 == 0), stop=(c2 == 1))
                    return r
                m3 = s.emit("tensor", atmm, waits=[("ve", v_)], inc="pe")
                v3 = s.emit("vector", lambda e: e.tensor_mul(attt[:], ps[3][:, 128:256],
                                                             m_triu[:]),
                            waits=[("pe", m3)], inc="ve")
                def omm(e, h=h, csl=csl):
                    for dkc in range(2):
                        e.matmul(ps[4][:, 0:256], qh_[:, dkc, csl],
                                 s0b[h][:, dkc * 256:dkc * 256 + 256],
                                 start=(dkc == 0), stop=False)
                    return e.matmul(ps[4][:, 0:256], attt[:], uadj[:], start=False, stop=True)
                m4 = s.emit("tensor", omm, waits=[("ve", v3)], inc="pe")
                v4 = s.emit("vector", lambda e: e.tensor_copy(oev[:], ps[4][:, 0:256]),
                            waits=[("pe", m4)], inc="ve")
                t_in_half = (g % 8) * GW + cc * 128
                dospill = s.dma("gpsimd", do_sp[g // 8, h, t_in_half:t_in_half + 128, :],
                                oev[:], waits=[("ve", v4)], inc="gQ")
                def smm(e, cc=cc, h=h, g=g):
                    for dkc in range(2):
                        r = e.matmul(ps[h][:, dkc * 256:dkc * 256 + 256],
                                     ktm[:, cc, dkc * 128:dkc * 128 + 128], uadj[:],
                                     start=(g == 0 and cc == 0 and dkc == 0),
                                     stop=(g == 15 and cc == 3 and dkc == 1),
                                     skip_group_check=True)
                    return r
                m5 = s.emit("tensor", smm, waits=[("ve", v4), ("ve", sinit)], inc="pe")
                sref = s.emit("vector", lambda e, h=h: e.tensor_copy(s0b[h][:], ps[h][:, :]),
                              waits=[("pe", m5)], inc="ve")

    a2f = agf
    a2d = s.emit("gpsimd", lambda e: e.collective_compute(
        "AllGather", ALU.bypass, replica_groups=PAIRS,
        ins=[do_sp], outs=[do_ag]), waits=[("gQ", dospill)], inc="cc")

    p3_es.close()
    p12_es.close()
    g1xw = E(nc.sbuf_tensor("g1xw", [128, 8, 2048], BF16))
    g1sw = E(nc.sbuf_tensor("g1sw", [64, 16, 128], BF16))
    g2w = E(nc.sbuf_tensor("g2w", [128, 16, 32], BF16))
    g1bc = E(nc.sbuf_tensor("g1bc", [128, 16], F32))
    owb = E(nc.sbuf_tensor("owb", [128, 8, HID], BF16))
    owf = E(nc.sbuf_tensor("owf", [128, HID], F32))
    onc = E(nc.sbuf_tensor("onc", [128, 8], F32))
    btile = [E(nc.sbuf_tensor(f"btile{i}", [128, 4, HID], BF16)) for i in range(4)]
    dtile = E(nc.sbuf_tensor("dtile", [128, 4, HID], F32))
    oac2 = E(nc.sbuf_tensor("oac2", [128, HID], F32))
    bt_a = E(nc.sbuf_tensor("bt_a", [128, HID], BF16))
    bt_b = E(nc.sbuf_tensor("bt_b", [128, HID], BF16))
    dt_a = E(nc.sbuf_tensor("dt_a", [128, HID], F32))
    mhc = E(nc.sbuf_tensor("mhc", [128, 2], F32))
    x2t = E(nc.sbuf_tensor("x2t", [128, HID], F32))
    statc = E(nc.sbuf_tensor("statc", [128, 64], F32))
    ss20 = E(nc.sbuf_tensor("ss20", [128, 24], F32))
    statf = E(nc.sbuf_tensor("statf", [64, GW], BF16))
    hdnb = E(nc.sbuf_tensor("hdnb", [128, 16, GW], BF16))
    lgf = E(nc.sbuf_tensor("lgf", [32, GW], F32))
    mrow = E(nc.sbuf_tensor("mrow", [4, 64], F32))
    ltm = E(nc.sbuf_tensor("ltm", [128, 4, 32], F32))
    w20 = E(nc.sbuf_tensor("w20", [128, 4, 32], F32))
    gsc = E(nc.sbuf_tensor("gsc", [32, 8], F32))
    fexp = E(nc.sbuf_tensor("fexp", [128, 32], F32))
    sexp = E(nc.sbuf_tensor("sexp", [128, 32], F32))
    oacc = E(nc.sbuf_tensor("oacc", [128, HID], F32))
    ofm = E(nc.sbuf_tensor("ofm", [128, 8, GW], BF16))
    xnbb = E(nc.sbuf_tensor("xnbb", [128, 8, GW], BF16))
    brw = E(nc.sbuf_tensor("brw", [1, 64], F32))

    stz0 = s.emit("vector", lambda e: e.memset(statc[:], 0.0),
                  waits=[("sc", s.cnt["sc"]), ("pe", s.cnt["pe"]),
                         ("gQ", s.cnt["gQ"]), ("sQ", s.cnt["sQ"])], inc="ve")

    gl0 = s.dma("gpsimd", mhc[:], mh_in)
    gl = s.dma("gpsimd", g1xw[:], rr(g1xT_in))
    gl = s.dma("gpsimd", g1sw[:], g1sT_in.rearrange("r (c t) -> r c t", t=128))
    gl = s.dma("gpsimd", g2w[:], rr(g2T_in))
    gl = s.dma("gpsimd", g1bc[:], g1b_in)
    gl = s.dma("gpsimd", onc[:], onormc_in)
    mk1 = s.dma("sync", mrow[0:4, 0:32], ltmask_in, inc="sQ")
    mk1 = s.dma("sync", mrow[0:4, 32:64], fmask_in, inc="sQ")
    lt1 = s.dma("sync", gsc[0:4, 0:1], ltvec_in, inc="sQ")
    lt1 = s.dma("sync", gsc[0:4, 1:2], flvec_in, inc="sQ")
    lt1 = s.dma("sync", gsc[0:32, 2:3], g2b_in, inc="sQ")
    lt1 = s.dma("sync", gsc[0:32, 7:8], vb_in, inc="sQ")
    owd = 0
    for kc in range(8):
        lw = s.dma("sync", owf[:], rr(o_wT_in)[:, kc, :],
                   waits=[("gQ", gl), ("ve", owd)], inc="sQ")
        owd = s.emit("vector", lambda e, kc=kc: e.tensor_scalar(
            owb[:, kc, :], owf[:], onc[:, kc:kc + 1], None, op0=ALU.mult),
            waits=[("sQ", lw)], inc="ve")
    gc1 = s.emit("scalar", lambda e: (
        e.activation(gsc[0:4, 4:5], gsc[0:4, 0:1], ACT.Exp, scale=-1.0),
        e.activation(gsc[0:4, 5:6], gsc[0:4, 1:2], ACT.Sigmoid))[-1],
        waits=[("sQ", lt1), ("sQ", mk1)], inc="sc")
    def gc2f(e):
        gap(e)
        e.tensor_scalar(gsc[0:4, 6:7], gsc[0:4, 5:6], -4.0, 1.0,
                        op0=ALU.mult, op1=ALU.add)
        gap(e)
        e.tensor_add(gsc[0:32, 3:4], gsc[0:32, 2:3], gsc[0:32, 7:8])
        gap(e)
        e.tensor_copy(gsc[0:32, 2:3], gsc[0:32, 3:4])
        gap(e)
        return e.tensor_copy(tmpcol[:], cstb[:, 0:1])
    gc2 = s.emit("vector", gc2f, waits=[("sc", gc1)], inc="ve")
    def gmms(e):
        e.matmul(ps[6][0:32, 0:1], mrow[0:4, 0:32], gsc[0:4, 4:5], start=True, stop=True)
        e.matmul(ps[6][0:32, 1:2], mrow[0:4, 32:64], gsc[0:4, 5:6], start=True, stop=True)
        return e.matmul(ps[6][0:32, 2:3], mrow[0:4, 0:32], gsc[0:4, 6:7],
                        start=True, stop=True)
    gm1 = s.emit("tensor", gmms, waits=[("ve", gc2)], inc="pe")
    def gev(e):
        e.tensor_copy(gsc[0:32, 4:7], ps[6][0:32, 0:3])
        gap(e)
        e.tensor_mul(gsc[0:32, 3:4], gsc[0:32, 2:3], gsc[0:32, 4:5])
        gap(e)
        return e.tensor_copy(tmpcol[:], cstb[:, 0:1])
    gv1 = s.emit("vector", gev, waits=[("pe", gm1)], inc="ve")
    def growmm(e):
        e.matmul(ps[6][0:1, 0:32], gsc[0:32, 5:6], identf[0:32, 0:32], start=True, stop=True)
        return e.matmul(ps[6][0:1, 32:64], gsc[0:32, 6:7], identf[0:32, 0:32],
                        start=True, stop=True)
    gm2 = s.emit("tensor", growmm, waits=[("ve", gv1)], inc="pe")
    gv2 = s.emit("vector", lambda e: e.tensor_copy(brw[0:1, 0:64], ps[6][0:1, 0:64]),
                 waits=[("pe", gm2)], inc="ve")
    def gbcmm(e):
        e.matmul(ps[6][:, 0:32], ones_row[:].bitcast(F32), brw[0:1, 0:32],
                 start=True, stop=True)
        return e.matmul(ps[6][:, 32:64], ones_row[:].bitcast(F32), brw[0:1, 32:64],
                        start=True, stop=True)
    gm3 = s.emit("tensor", gbcmm, waits=[("ve", gv2)], inc="pe")
    gv3 = s.emit("vector", lambda e: (e.tensor_copy(fexp[:], ps[6][:, 0:32]),
                                      e.tensor_copy(sexp[:], ps[6][:, 32:64]))[-1],
                 waits=[("pe", gm3)], inc="ve")

    STT = [("short", 0), ("long", 1), ("wide", 2), ("delta", 3), ("v", 4)]
    prevtile = {"ve": 0, "sp": 0, "pe": 0}
    for gb in range(8):
        xl = s.dma("gpsimd", xnbb[:], rr(xn_sp)[:, :, HALO + gb * GW:HALO + gb * GW + GW],
                   inc="gQ")
        st_done = []
        for tt in range(4):
            til = gb * 4 + tt
            tw = [("cc", a2f), ("cc", a2d), ("ve", prevtile["ve"]),
                  ("pe", prevtile["pe"])]
            lds = 0
            vcomb = 0
            for f in range(4):
                ca = PADL + til * 128
                cb = PADL + NT + til * 128
                for sl in range(2):
                    lds = s.xdma("sync", lambda e, f=f, sl=sl, ca=ca:
                        e.dma_start_transpose(
                            bt_a[:, sl * 512:sl * 512 + 512],
                            fir4_ag[sl, f, :, ca:ca + 128]),
                        waits=tw + [("ve", vcomb)], inc="sQ")
                    lds = s.xdma("sync", lambda e, f=f, sl=sl, cb=cb:
                        e.dma_start_transpose(
                            bt_b[:, sl * 512:sl * 512 + 512],
                            fir4_ag[sl, f, :, cb:cb + 128]),
                        waits=tw, inc="sQ")
                def fcomb(e, f=f, tt=tt):
                    e.tensor_scalar(oac2[:], bt_a[:], mhc[:, 0:1], None, op0=ALU.mult)
                    return e.scalar_tensor_tensor(btile[f][:, tt, :], bt_b[:],
                                                  mhc[:, 1:2], oac2[:],
                                                  op0=ALU.mult, op1=ALU.add)
                vcomb = s.emit("vector", fcomb, waits=[("sQ", lds), ("gQ", gl0)],
                               inc="ve")
            ldd = 0
            for sl in range(2):
                for lh in range(2):
                    hd = sl * 2 + lh
                    ldd = s.dma("sync", dt_a[:, hd * 256:hd * 256 + 256],
                                do_ag[sl, 0, lh, til * 128:til * 128 + 128, :],
                                waits=tw + [("ve", vcomb)], inc="sQ")
                    ldd = s.dma("sync", oacc[:, hd * 256:hd * 256 + 256],
                                do_ag[sl, 1, lh, til * 128:til * 128 + 128, :],
                                waits=tw, inc="sQ")
            def dcomb(e, tt=tt):
                e.tensor_scalar(dt_a[:], dt_a[:], mhc[:, 0:1], None, op0=ALU.mult)
                gap(e)
                return e.scalar_tensor_tensor(dtile[:, tt, :], oacc[:],
                                              mhc[:, 1:2], dt_a[:],
                                              op0=ALU.mult, op1=ALU.add)
            vcomb = s.emit("vector", dcomb, waits=[("sQ", ldd)], inc="ve")
            ldd_v = ("ve", vcomb)
            lds = lds
            vops = 0
            for T, (nm, ti) in enumerate(STT):
                if ti == 3:
                    seg = dtile[:, tt, :].rearrange("p (a d) -> p a d", d=256)
                    wkey = [("ve", vcomb)]
                else:
                    fi = [0, 1, 2, None, 3][ti]
                    seg = btile[fi][:, tt, :].rearrange("p (a d) -> p a d", d=256)
                    wkey = [("ve", vcomb)]
                def stat1(e, seg=seg, T=T):
                    e.tensor_reduce(statc[:, T * 4:T * 4 + 4], seg, axis=AX.X, op=ALU.add)
                    return e.tensor_reduce(statc[:, 40 + T * 4:40 + T * 4 + 4], seg,
                                           axis=AX.X, op=ALU.max,
                                           apply_absolute_value=True)
                vops = s.emit("vector", stat1, waits=wkey + [("pe", prevtile["pe"])],
                              inc="ve")
                sqs = s.emit("scalar", lambda e, seg=seg: e.activation(
                    x2t[:].rearrange("p (a d) -> p a d", d=256), seg, ACT.Square),
                    waits=wkey + [("ve", vops), ("ve", s.cnt["ve"])], inc="sc")
                vops = s.emit("vector", lambda e, T=T: e.tensor_reduce(
                    ss20[:, T * 4:T * 4 + 4], x2t[:].rearrange("p (a d) -> p a d", d=256),
                    axis=AX.X, op=ALU.add), waits=[("sc", sqs)], inc="ve")
            def stfin(e):
                gap(e)
                e.tensor_scalar(statc[:, 0:20], statc[:, 0:20], 1.0 / 256.0, None,
                                op0=ALU.mult)
                e.tensor_scalar(ss20[:, 0:20], ss20[:, 0:20], 1.0 / 256.0, 1e-8,
                                op0=ALU.mult, op1=ALU.max)
                gap(e)
                return e.tensor_copy(tmpcol[:], cstb[:, 0:1])
            vf1 = s.emit("vector", stfin, waits=[("ve", vops)], inc="ve")
            sf1 = s.emit("scalar", lambda e: (
                e.activation(ss20[:, 0:20], ss20[:, 0:20], ACT.Ln),
                e.activation(statc[:, 20:40], ss20[:, 0:20], ACT.Exp, scale=0.5))[-1],
                waits=[("ve", vf1)], inc="sc")
            stm = s.emit("tensor", lambda e: e.matmul(
                ps[7][0:64, 0:128], statc[:, 0:64], identf[:], start=True, stop=True),
                waits=[("sc", sf1)], inc="pe")
            stv = s.emit("vector", lambda e, tt=tt: e.tensor_copy(
                statf[:, tt * 128:tt * 128 + 128], ps[7][0:64, 0:128]),
                waits=[("pe", stm)], inc="ve")
            st_done.append(stv)
            prevtile["pe"] = stm
            prevtile["ve"] = stv
        g1ev = 0
        for oc in range(16):
            bank = oc % 2
            def g1mm(e, oc=oc, bank=bank):
                for kc in range(8):
                    e.matmul(ps[bank][:, :], g1xw[:, kc, oc * 128:oc * 128 + 128],
                             xnbb[:, kc, :], start=(kc == 0), stop=False)
                return e.matmul(ps[bank][:, :], g1sw[:, oc, :], statf[:],
                                start=False, stop=True)
            gw_ = [("gQ", xl), ("ve", st_done[-1])]
            g1m = s.emit("tensor", g1mm, waits=gw_, inc="pe")
            g1ev = s.emit("scalar", lambda e, oc=oc, bank=bank: e.activation(
                hdnb[:, oc, :], ps[bank][:, :], ACT.Gelu, bias=g1bc[:, oc:oc + 1]),
                waits=[("pe", g1m)], inc="sc")
        def g2mm(e):
            for kc in range(16):
                i = e.matmul(ps[2][0:32, :], g2w[:, kc, :], hdnb[:, kc, :],
                             start=(kc == 0), stop=(kc == 15))
            return i
        g2m = s.emit("tensor", g2mm, waits=[("sc", g1ev)], inc="pe")
        g2e = s.emit("scalar", lambda e: e.activation(
            lgf[:, :], ps[2][0:32, :], ACT.Identity, bias=gsc[0:32, 3:4],
            scale=gsc[0:32, 4:5]), waits=[("pe", g2m), ("ve", gv3)], inc="sc")
        def ltr(e):
            for tt in range(4):
                r = e.matmul(ps[3][:, tt * 32:tt * 32 + 32],
                             lgf[0:32, tt * 128:tt * 128 + 128], identf[0:32, 0:32],
                             start=True, stop=True)
            return r
        lm = s.emit("tensor", ltr, waits=[("sc", g2e)], inc="pe")
        lv = s.emit("vector", lambda e: e.tensor_copy(
            ltm[:].rearrange("p a c -> p (a c)"), ps[3][:, 0:128]),
            waits=[("pe", lm)], inc="ve")
        def smax(e):
            for tt in range(4):
                sl = ltm[:, tt, 0:20].rearrange("p (h j) -> p h j", j=5)
                e.tensor_reduce(w20[:, tt, 20:24], sl, axis=AX.X, op=ALU.max)
            gap(e)
            for tt in range(4):
                for h4 in range(4):
                    e.tensor_scalar(ltm[:, tt, h4 * 5:h4 * 5 + 5],
                                    ltm[:, tt, h4 * 5:h4 * 5 + 5],
                                    w20[:, tt, 20 + h4:21 + h4], None, op0=ALU.subtract)
            return e.tensor_copy(tmpcol[:], cstb[:, 0:1])
        sv1 = s.emit("vector", smax, waits=[("ve", lv)], inc="ve")
        se1 = s.emit("scalar", lambda e: e.activation(
            ltm[:].rearrange("p a c -> p (a c)"), ltm[:].rearrange("p a c -> p (a c)"),
            ACT.Exp), waits=[("ve", sv1)], inc="sc")
        def ssum(e):
            for tt in range(4):
                sl = ltm[:, tt, 0:20].rearrange("p (h j) -> p h j", j=5)
                e.tensor_reduce(w20[:, tt, 20:24], sl, axis=AX.X, op=ALU.add)
            gap(e)
            return e.tensor_copy(tmpcol[:], cstb[:, 0:1])
        sv2 = s.emit("vector", ssum, waits=[("sc", se1)], inc="ve")
        se2 = s.emit("scalar", lambda e: (
            e.activation(w20[:, 0, 24:28], w20[:, 0, 20:24], ACT.Ln),
            e.activation(w20[:, 1, 24:28], w20[:, 1, 20:24], ACT.Ln),
            e.activation(w20[:, 2, 24:28], w20[:, 2, 20:24], ACT.Ln),
            e.activation(w20[:, 3, 24:28], w20[:, 3, 20:24], ACT.Ln),
            e.activation(w20[:, 0, 28:32], w20[:, 0, 24:28], ACT.Exp, scale=-1.0),
            e.activation(w20[:, 1, 28:32], w20[:, 1, 24:28], ACT.Exp, scale=-1.0),
            e.activation(w20[:, 2, 28:32], w20[:, 2, 24:28], ACT.Exp, scale=-1.0),
            e.activation(w20[:, 3, 28:32], w20[:, 3, 24:28], ACT.Exp, scale=-1.0))[-1],
            waits=[("ve", sv2)], inc="sc")
        def wmk(e):
            for tt in range(4):
                for h4 in range(4):
                    e.tensor_scalar(ltm[:, tt, h4 * 5:h4 * 5 + 5],
                                    ltm[:, tt, h4 * 5:h4 * 5 + 5],
                                    w20[:, tt, 28 + h4:29 + h4], None, op0=ALU.mult)
            gap(e)
            for tt in range(4):
                e.scalar_tensor_tensor(ltm[:, tt, 0:20], ltm[:, tt, 0:20], 1.0,
                                       sexp[:, 0:20], op0=ALU.mult, op1=ALU.mult)
            gap(e)
            for tt in range(4):
                e.tensor_add(w20[:, tt, 0:20], ltm[:, tt, 0:20], fexp[:, 0:20])
            gap(e)
            return e.tensor_copy(tmpcol[:], cstb[:, 0:1])
        sv3 = s.emit("vector", wmk, waits=[("sc", se2), ("ve", gv3)], inc="ve")
        for tt in range(4):
            til = gb * 4 + tt
            def ocomb(e, tt=tt):
                for h4 in range(4):
                    e.tensor_scalar(oac2[:, h4 * 256:h4 * 256 + 256],
                                    btile[0][:, tt, h4 * 256:h4 * 256 + 256],
                                    w20[:, tt, h4 * 5:h4 * 5 + 1], None, op0=ALU.mult)
                srcs = [(btile[1][:, tt, :], 1), (btile[2][:, tt, :], 2),
                        (dtile[:, tt, :], 3), (btile[3][:, tt, :], 4)]
                bufs = [oac2, oacc, oac2, oacc, oac2]
                for step, (bsl, j) in enumerate(srcs):
                    src_acc = bufs[step]
                    dst_acc = bufs[step + 1]
                    for h4 in range(4):
                        e.scalar_tensor_tensor(dst_acc[:, h4 * 256:h4 * 256 + 256],
                                               bsl[:, h4 * 256:h4 * 256 + 256],
                                               w20[:, tt, h4 * 5 + j:h4 * 5 + j + 1],
                                               src_acc[:, h4 * 256:h4 * 256 + 256],
                                               op0=ALU.mult, op1=ALU.add)
                return e.tensor_copy(tmpcol[:], cstb[:, 0:1])
            ov = s.emit("vector", ocomb, waits=[("ve", sv3)], inc="ve")
            sq_o = s.emit("scalar", lambda e: e.activation(
                x2t[:].rearrange("p (a d) -> p a d", d=256),
                oac2[:].rearrange("p (a d) -> p a d", d=256), ACT.Square),
                waits=[("ve", ov)], inc="sc")
            rv1 = s.emit("vector", lambda e: e.tensor_reduce(
                w20[:, 0, 20:24] if False else ss20[:, 20:24],
                x2t[:].rearrange("p (a d) -> p a d", d=256), axis=AX.X, op=ALU.add),
                waits=[("sc", sq_o)], inc="ve")
            rs1 = s.emit("scalar", lambda e: (
                e.activation(ss20[:, 20:24], ss20[:, 20:24], ACT.Ln,
                             scale=1.0 / 256.0, bias=1e-5),
                e.activation(ss20[:, 20:24], ss20[:, 20:24], ACT.Exp, scale=-0.5))[-1],
                waits=[("ve", rv1)], inc="sc")
            rv2 = s.emit("vector", lambda e: [
                e.tensor_scalar(oacc[:, h4 * 256:h4 * 256 + 256],
                                oac2[:, h4 * 256:h4 * 256 + 256],
                                ss20[:, 20 + h4:21 + h4], None, op0=ALU.mult)
                for h4 in range(4)][-1], waits=[("sc", rs1)], inc="ve")
            def otr(e):
                for c8 in range(8):
                    r = e.matmul(ps[4 + c8 // 4][:, (c8 % 4) * 128:(c8 % 4) * 128 + 128],
                                 oacc[:, c8 * 128:c8 * 128 + 128], identf[:],
                                 start=True, stop=True)
                return r
            om = s.emit("tensor", otr, waits=[("ve", rv2)], inc="pe")
            ovv = s.emit("vector", lambda e, tt=tt: (
                e.tensor_copy(ofm[:, 0:4, tt * 128:tt * 128 + 128],
                              ps[4][:, :].rearrange("p (c t) -> p c t", t=128)),
                e.tensor_copy(ofm[:, 4:8, tt * 128:tt * 128 + 128],
                              ps[5][:, :].rearrange("p (c t) -> p c t", t=128)))[-1],
                waits=[("pe", om)], inc="ve")
            prevtile["ve"] = max(prevtile["ve"], ovv)
        fvk = {0: 0, 1: 0}
        evg = {0: 0, 1: 0}
        for oc in range(8):
            bank = 2 + oc % 2
            def owmm(e, oc=oc, bank=bank):
                for kc in range(8):
                    i = e.matmul(ps[bank][:, :], owb[:, kc, oc * 128:oc * 128 + 128],
                                 ofm[:, kc, :], start=(kc == 0), stop=(kc == 7))
                return i
            owm = s.emit("tensor", owmm, waits=[("ve", prevtile["ve"]),
                                                ("ve", owd), ("ve", fvk[oc % 2])],
                         inc="pe")
            lo = s.dma("sync", tq[oc % 2][:], rr(out_fm)[:, oc, gb * GW:gb * GW + GW],
                       waits=[("pe", owm), ("ve", fvk[oc % 2])], inc="sQ")
            fv = s.emit("vector", lambda e, oc=oc, bank=bank: e.tensor_add(
                ev[oc % 2][:], ps[bank][:, :], tq[oc % 2][:]),
                waits=[("sQ", lo), ("pe", owm), ("gQ", evg[oc % 2])], inc="ve")
            p3b_sp = s.dma("gpsimd", rr(out_fm)[:, oc, gb * GW:gb * GW + GW],
                           ev[oc % 2][:], waits=[("ve", fv)], inc="gQ")
            fvk[oc % 2] = fv
            evg[oc % 2] = p3b_sp
            prevtile["pe"] = owm

    s.emit("gpsimd", lambda e: None,
           waits=[("gQ", s.cnt["gQ"]), ("sQ", s.cnt["sQ"]), ("ve", s.cnt["ve"]),
                  ("sc", s.cnt["sc"]), ("pe", s.cnt["pe"]), ("cc", s.cnt["cc"])])


_cache = {}


def kernel(**inputs):
    x = np.ascontiguousarray(inputs["L0_attn_hidden_states"], dtype=np.float32)
    B, L, _ = x.shape
    w_up = np.ascontiguousarray(inputs["L0_ffn_w_up"], dtype=np.float32)
    w_gate = np.ascontiguousarray(inputs["L0_ffn_w_gate"], dtype=np.float32)
    w_down = np.ascontiguousarray(inputs["L0_ffn_w_down"], dtype=np.float32)

    qT = np.ascontiguousarray(inputs["L0_attn_q_w"].astype(np.float32).T)
    kT = np.ascontiguousarray(inputs["L0_attn_k_w"].astype(np.float32).T)
    vT = np.ascontiguousarray(inputs["L0_attn_v_w"].astype(np.float32).T)
    bT = np.ascontiguousarray(inputs["L0_attn_b_w"].astype(np.float32).T)
    convs = [np.asarray(inputs[k], np.float32) for k in
             ("L0_attn_convq_w", "L0_attn_convk_w", "L0_attn_convv_w")]
    firs = [np.asarray(inputs[k], np.float32).reshape(HID, -1) for k in
            ("L0_attn_fir_s", "L0_attn_fir_l", "L0_attn_fir_wd")]
    g1w = np.asarray(inputs["L0_attn_g1_w"], np.float32)
    g1b = np.asarray(inputs["L0_attn_g1_b"], np.float32)
    g2w = np.asarray(inputs["L0_attn_g2_w"], np.float32)
    g2b = np.asarray(inputs["L0_attn_g2_b"], np.float32)
    vb = np.asarray(inputs["L0_attn_value_bias"], np.float32)
    ltv = np.asarray(inputs["L0_attn_log_temp"], np.float32)
    flv = np.asarray(inputs["L0_attn_floor_logit"], np.float32)
    onw = np.asarray(inputs["L0_attn_onorm_w"], np.float32)
    owT = np.ascontiguousarray(inputs["L0_attn_o_w"].astype(np.float32).T)

    filt_all = convs + firs
    bands_full = np.zeros((6, HID, 256, 128), np.float32)
    s2 = np.arange(256)[:, None]
    tt_ = np.arange(128)[None, :]
    for f, wf in enumerate(filt_all):
        K = wf.shape[1]
        tau = s2 - 128 - tt_ + K - 1
        valid = (tau >= 0) & (tau < K)
        tcl = np.clip(tau, 0, K - 1)
        bands_full[f] = np.where(valid[None, :, :], wf[:, tcl], 0.0)

    g1xT = np.ascontiguousarray(g1w[:, 0:HID].T)
    g1sT = np.zeros((64, 2048), np.float32)
    for stat in range(3):
        for T in range(5):
            for h in range(4):
                g1sT[stat * 20 + T * 4 + h] = g1w[:, HID + T * 12 + stat * 4 + h]
    g2T = np.zeros((2048, 32), np.float32)
    g2T[:, 0:20] = g2w.T
    g2bc = np.zeros((32, 1), np.float32); g2bc[0:20, 0] = g2b
    vbc = np.zeros((32, 1), np.float32)
    for h in range(4):
        vbc[h * 5 + 4, 0] = vb[h]
    ltmask = np.zeros((4, 32), np.float32)
    fmask = np.zeros((4, 32), np.float32)
    for h in range(4):
        for j in range(5):
            ltmask[h, h * 5 + j] = 1.0
            if j < 4:
                fmask[h, h * 5 + j] = 1.0
    onormc = np.zeros((128, 8), np.float32)
    for kc in range(8):
        onormc[:, kc] = onw[(np.arange(128) + 128 * kc) % 256]
    bd32 = np.tril(np.ones((32, 32), np.float32), -1)
    bd128 = np.zeros((128, 128), np.float32)
    for gblk in range(4):
        bd128[gblk * 32:(gblk + 1) * 32, gblk * 32:(gblk + 1) * 32] = bd32
    m_bdtril = np.tile(bd128, (1, 4))
    m_ipack = np.tile(np.eye(128, dtype=np.float32), (1, 4))
    m_sbu = np.zeros((128, 128), np.float32)
    for a in range(4):
        for bb2 in range(4):
            if a < bb2:
                m_sbu[a * 32:a * 32 + 32, bb2 * 32:bb2 * 32 + 32] = 1.0
    m_triu = np.triu(np.ones((128, 128), np.float32), 0)
    ident = np.eye(128, dtype=np.float32)

    nc = build_program(dbg=DEBUG)

    cstv = np.zeros((128, 16), np.float32)
    for i, v in enumerate(CONSTS):
        cstv[:, i] = np.float32(v)

    in_maps = []
    for c in range(NCORE):
        b, hh = c // 2, c % 2
        t0 = hh * NT
        xs = np.zeros((HID, NTOT), np.float32)
        xs[:, HALO:] = x[b, t0:t0 + NT, :].T
        wq = np.zeros((HID, 1540), np.float32)
        wq[:, 0:512] = qT[:, 512 * hh:512 * hh + 512]
        wq[:, 512:1024] = kT[:, 512 * hh:512 * hh + 512]
        wq[:, 1024:1536] = vT[:, 512 * hh:512 * hh + 512]
        wq[:, 1536:1538] = bT[:, 2 * hh:2 * hh + 2]
        in_maps.append({
            "x_fm": xs, "w_up": w_up, "w_gate": w_gate, "w_down": w_down,
            "ones_in": np.ones((128, 1), np.float32),
            "onesr_in": np.ones((1, 128), np.float32),
            "cst_in": cstv,
            "wqkvbT": wq,
            "bands_in": np.ascontiguousarray(bands_full[:, 512 * hh:512 * hh + 512]),
            "o_wT_in": owT, "g1xT_in": g1xT, "g1sT_in": g1sT,
            "g1b_in": np.ascontiguousarray(g1b.reshape(16, 128).T), "g2T_in": g2T,
            "g2b_in": g2bc, "vb_in": vbc,
            "ltmask_in": ltmask, "fmask_in": fmask,
            "ltvec_in": ltv.reshape(4, 1), "flvec_in": flv.reshape(4, 1),
            "onormc_in": onormc, "m_bdtril_in": m_bdtril, "m_ipack_in": m_ipack,
            "m_sbu_in": m_sbu, "m_triu_in": m_triu, "ident_in": ident,
            "mh_in": np.ascontiguousarray(
                np.tile(np.array([1.0 - hh, float(hh)], np.float32), (128, 1))),
        })

    res = run_bass_kernel_spmd(nc, in_maps, core_ids=list(range(NCORE)))
    out = np.empty((B, L, HID), np.float32)
    for c in range(NCORE):
        b, hh = c // 2, c % 2
        out[b, hh * NT:(hh + 1) * NT, :] = res.results[c]["out_fm"].T
    if DEBUG:
        kernel.dbg = [{k: v for k, v in r.items() if k != "out_fm"} for r in res.results]
    return out


# version 05 (final)
# speedup vs baseline: 1.0613x; 1.0114x over previous
import numpy as np
import ml_dtypes
import concourse.bass as bass
import concourse.mybir as mybir
from concourse.bass_utils import run_bass_kernel_spmd

F32 = mybir.dt.float32
F32R = mybir.dt.float32r
BF16 = mybir.dt.bfloat16
ALU = mybir.AluOpType
ACT = mybir.ActivationFunctionType
AX = mybir.AxisListType

DEBUG = False

NCORE = 8
HID = 1024
NT = 4096
HALO = 128
NTOT = NT + HALO
NGRP = 8
GW = 512
MAGIC = float(np.float32(1.5 * 2.0 ** 23))
QMAX = 32767.0
CONSTS = [1e-5, 1e-6, 1e-8, -0.5, 0.5, 1.0]
PADL = 128
PADR = 128
NTP = PADL + 2 * NT + PADR
PAIRS = [[0, 1], [2, 3], [4, 5], [6, 7]]

ENGS = ("sync", "gpsimd", "tensor", "vector", "scalar")


class Sched:

    def __init__(self):
        self.ops = {e: [] for e in ENGS}
        self.cnt = {}
        self.lastw = {e: {} for e in ENGS}

    def sem(self, name):
        self.cnt[name] = 0
        return name

    def emit(self, eng, fn, waits=(), inc=None, incby=1):
        ws = []
        for s, v in waits:
            if v > self.lastw[eng].get(s, -1):
                ws.append((s, v))
                self.lastw[eng][s] = v
        self.ops[eng].append((ws, fn, inc, incby))
        if inc is not None:
            self.cnt[inc] += incby
            return self.cnt[inc]
        return None

    last_xfer = None

    def dma(self, eng, out, in_, waits=(), inc="gQ"):
        return self.emit(eng, lambda e, o=out, i=in_: e.dma_start(out=o, in_=i),
                         waits=waits, inc=inc, incby=16)

    def xdma(self, eng, fn, waits=(), inc="gQ"):
        assert eng == "sync"
        return self.emit(eng, fn, waits=waits, inc=inc, incby=16)

    preamble = None

    def build(self, nc, semhandles):
        issuer = {"gQ": "gpsimd", "sQ": "sync"}
        waited = {S: set() for S in issuer}
        if self.preamble and self.preamble[0] in waited:
            waited[self.preamble[0]].add(self.preamble[1] // 16)
        for e2 in ENGS:
            for ws, _fn, _inc, _incby in self.ops[e2]:
                for S, v in ws:
                    if S in waited:
                        waited[S].add(v // 16)
        wsorted = {S: sorted(waited[S]) for S in issuer}

        import bisect
        selfwaits = {e2: {} for e2 in ENGS}
        nbound = 0
        pcs = {e2: 0 for e2 in ENGS}
        cnt_sim = {sem: 0 for sem in self.cnt}
        issued = {S: 0 for S in issuer}
        P = {e2: dict.fromkeys(issuer, 0) for e2 in ENGS}
        Pinc = {sem: {} for sem in self.cnt}
        okP = {S: [] for S in issuer}
        sem_of = {engname: S for S, engname in issuer.items()}
        live = [e2 for e2 in ENGS if self.ops[e2]]
        while live:
            progress = False
            for e2 in list(live):
                while pcs[e2] < len(self.ops[e2]):
                    ws, _fn, inc, incby = self.ops[e2][pcs[e2]]
                    ok = True
                    for S, v in ws:
                        if S in issuer:
                            if issued[S] < v:
                                ok = False
                                break
                        elif cnt_sim[S] < v:
                            ok = False
                            break
                    if not ok:
                        break
                    for S, v in ws:
                        if S in issuer:
                            m = v // 16
                            if m > P[e2][S]:
                                P[e2][S] = m
                            if m > 0:
                                snap = okP[S][m - 1]
                                for S2 in issuer:
                                    if snap[S2] > P[e2][S2]:
                                        P[e2][S2] = snap[S2]
                        else:
                            snap = Pinc[S].get(v)
                            if snap:
                                for S2 in issuer:
                                    if snap[S2] > P[e2][S2]:
                                        P[e2][S2] = snap[S2]
                    if inc is not None and inc in issuer:
                        S = inc
                        j = issued[S] // 16 + 1
                        wl = wsorted[S]
                        i_hi = bisect.bisect_left(wl, j)
                        if i_hi > 0 and wl[i_hi - 1] > P[e2][S]:
                            mstar = wl[i_hi - 1]
                            selfwaits[e2][pcs[e2]] = (S, mstar * 16)
                            nbound += 1
                            P[e2][S] = mstar
                            snap = okP[S][mstar - 1]
                            for S2 in issuer:
                                if snap[S2] > P[e2][S2]:
                                    P[e2][S2] = snap[S2]
                        okP[S].append(dict(P[e2]))
                        issued[S] += 16
                    elif inc is not None:
                        cnt_sim[inc] += incby
                        Pinc[inc][cnt_sim[inc]] = dict(P[e2])
                    pcs[e2] += 1
                    progress = True
                if pcs[e2] >= len(self.ops[e2]):
                    live.remove(e2)
            if not progress and live:
                raise RuntimeError(f"boundary pass: schedule deadlock at {pcs}")
        import sys as _sys
        print(f"Sched: inserted {nbound} DMA-boundary self-waits "
              f"(gQ waited={len(waited['gQ'])}, sQ waited={len(waited['sQ'])})",
              file=_sys.stderr)

        with nc.Block() as blk:
            for engname in ENGS:
                ops = self.ops[engname]
                if not ops:
                    continue

                def section(eng, _ops=ops, _eng=engname):
                    if self.preamble and _eng != "gpsimd":
                        eng.wait_ge(semhandles[self.preamble[0]], self.preamble[1])
                    for idx, (ws, fn, inc, incby) in enumerate(_ops):
                        bw = selfwaits[_eng].get(idx)
                        if bw is not None:
                            eng.wait_ge(semhandles[bw[0]], bw[1])
                        for s, v in ws:
                            eng.wait_ge(semhandles[s], v)
                        instr = fn(eng)
                        if inc is not None:
                            assert instr is not None
                            instr.then_inc(semhandles[inc], incby)

                getattr(blk, engname)(section)


def rr(ap, p=128):
    return ap.rearrange("(c p) t -> p c t", p=p)


def build_program(dbg=False):
    nc = bass.Bass("TRN2", target_bir_lowering=False)
    from contextlib import ExitStack
    ctx = ExitStack()
    nc._ctx = ctx
    E = ctx.enter_context

    x_fm = nc.dram_tensor("x_fm", [HID, NTOT], F32, kind="ExternalInput").ap()
    w_up = nc.dram_tensor("w_up", [3072, HID], F32, kind="ExternalInput").ap()
    w_gate = nc.dram_tensor("w_gate", [3072, HID], F32, kind="ExternalInput").ap()
    w_down = nc.dram_tensor("w_down", [HID, 3072], F32, kind="ExternalInput").ap()
    ones_in = nc.dram_tensor("ones_in", [128, 1], F32, kind="ExternalInput").ap()
    cst_in = nc.dram_tensor("cst_in", [128, 16], F32, kind="ExternalInput").ap()
    onesr_in = nc.dram_tensor("onesr_in", [1, 128], F32, kind="ExternalInput").ap()
    out_fm = nc.dram_tensor("out_fm", [HID, NT], F32, kind="ExternalOutput").ap()

    spk = "ExternalOutput" if dbg else "Internal"
    xn_sp = nc.dram_tensor("xn_sp", [HID, NTOT], F32, kind=spk).ap()
    up_sp = nc.dram_tensor("up_sp", [3072, NT], F32, kind=spk).ap()
    gate_sp = nc.dram_tensor("gate_sp", [3072, NT], F32, kind=spk).ap()
    g2p_sp = nc.dram_tensor("g2p_sp", [3072, NT], F32, kind=spk).ap()
    o2p_sp = nc.dram_tensor("o2p_sp", [3072, NT], F32, kind=spk).ap()
    upq_sp = nc.dram_tensor("upq_sp", [3072, HID], BF16, kind=spk).ap()
    gateq_sp = nc.dram_tensor("gateq_sp", [3072, HID], BF16, kind="Internal").ap()
    downq_sp = nc.dram_tensor("downq_sp", [HID, 3072], BF16, kind="Internal").ap()
    cc_in = nc.dram_tensor("cc_in", [1, 8], F32, kind="Internal").ap()
    cc_out = nc.dram_tensor("cc_out", [1, 8], F32, kind="Internal").ap()

    dbg_sc = None
    if dbg:
        dbg_sc = nc.dram_tensor("dbg_sc", [1, 16], F32, kind="ExternalOutput").ap()

    rowb = E(nc.sbuf_tensor("rowb", [1, 4 * GW], F32))
    ones_col = E(nc.sbuf_tensor("ones_col", [128, 1], F32))
    ones_row = E(nc.sbuf_tensor("ones_row", [1, 128], F32))
    amax_xn = E(nc.sbuf_tensor("amax_xn", [128, 1], F32))
    amax_t = E(nc.sbuf_tensor("amax_t", [128, 8], F32))
    amaxw = E(nc.sbuf_tensor("amaxw", [128, 200], F32))
    tmpcol = E(nc.sbuf_tensor("tmpcol", [128, 1], F32))
    redrow = E(nc.sbuf_tensor("redrow", [1, 128], F32))
    scal = E(nc.sbuf_tensor("scal", [1, 16], F32))
    ccs = E(nc.sbuf_tensor("ccs", [1, 8], F32))
    sbc = E(nc.sbuf_tensor("sbc", [128, 12], F32))
    
    ev = [E(nc.sbuf_tensor(f"ev{i}", [128, GW], F32)) for i in range(2)]
    tq = [E(nc.sbuf_tensor(f"tq{i}", [128, GW], F32)) for i in range(6)]

    cstb = E(nc.sbuf_tensor("cstb", [128, 16], F32))
    ffn_es = ExitStack()
    wsc = E(nc.sbuf_tensor("wsc", [128, 5 * 32], F32))
    xg = [ffn_es.enter_context(nc.sbuf_tensor("xg0", [128, 8, GW], F32))]
    x2g = ffn_es.enter_context(nc.sbuf_tensor("x2g", [128, 8, GW], F32))
    xng = [ffn_es.enter_context(nc.sbuf_tensor("xng0", [128, 8, GW], F32))]
    wraw = ffn_es.enter_context(nc.sbuf_tensor("wraw", [128, HID], F32))
    wqt = ffn_es.enter_context(nc.sbuf_tensor("wqt", [128, HID], BF16))
    x1sb = ffn_es.enter_context(nc.sbuf_tensor("x1sb", [128, 8, NT], BF16))
    wsb = ffn_es.enter_context(nc.sbuf_tensor("wsb", [128, 8, 3072], BF16))
    o2q = ffn_es.enter_context(nc.sbuf_tensor("o2q", [128, 12, GW], BF16))
    ps = [E(nc.psum_tensor(f"ps{i}", [128, GW], F32)) for i in range(8)]

    semnames = ["sQ", "gQ", "pe", "ve", "sc", "cc"]
    sh = {}
    s = Sched()
    for n in semnames:
        sh[n] = E(nc.semaphore(n))
        s.sem(n)

    f32r = lambda ap: ap

    def gap(e):
        e.tensor_copy(tmpcol[:], cstb[:, 0:1])

    GRP = [(0, HALO, False)] + [(HALO + i * GW, GW, True) for i in range(NGRP)]

    c_cst = s.dma("gpsimd", cstb[:], cst_in)
    for i, v in enumerate(CONSTS):
        nc.const_aps.register(F32, float(v), cstb[:, i:i + 1])
    s.preamble = ("gQ", c_cst)
    c_ones = s.dma("gpsimd", ones_col[:], ones_in)
    c_onesr = s.dma("gpsimd", ones_row[:], onesr_in)
    s.emit("vector", lambda e: e.memset(amax_xn[:], 0.0), inc="ve")
    s.emit("vector", lambda e: e.memset(amaxw[:], 0.0), inc="ve")
    s.emit("vector", lambda e: e.memset(amax_t[:], 0.0), inc="ve")
    s.emit("vector", lambda e: e.memset(scal[:], 0.0), inc="ve")
    ve_init = s.cnt["ve"]
    lp_units = []
    for t in range(24):
        lp_units.append((rr(w_up)[:, t, :], rr(upq_sp)[:, t, :]))
    for t in range(24):
        lp_units.append((rr(w_gate)[:, t, :], rr(gateq_sp)[:, t, :]))
    for t in range(8):
        for cc3 in range(3):
            lp_units.append((rr(w_down)[:, t, cc3 * HID:(cc3 + 1) * HID],
                             rr(downq_sp)[:, t, cc3 * HID:(cc3 + 1) * HID]))
    lp_prev_v = 0
    lp_prev_sp = 0
    lp_prev_sc = 0
    for ui, (src_ap, dst_ap) in enumerate(lp_units):
        ldw = s.dma("sync", wraw[:], src_ap,
                    waits=[("ve", lp_prev_v)], inc="sQ")
        def lpq_a(e):
            e.tensor_reduce(wsc[:, 0:32], wraw[:].rearrange("p (b k) -> p b k", k=32),
                            axis=AX.X, op=ALU.max, apply_absolute_value=True)
            e.tensor_copy(tmpcol[:], cstb[:, 0:1])
            return e.tensor_scalar(wsc[:, 32:64], wsc[:, 0:32], 1.0 / 7.0, 1e-12,
                                   op0=ALU.mult, op1=ALU.max)
        lv_a = s.emit("vector", lpq_a, waits=[("sQ", ldw), ("gQ", lp_prev_sp),
                                              ("sc", lp_prev_sc)], inc="ve")
        lv_s = s.emit("scalar", lambda e: (
            e.activation(wsc[:, 96:128], wsc[:, 32:64], ACT.Ln),
            e.activation(wsc[:, 64:96], wsc[:, 96:128], ACT.Exp, scale=-1.0))[-1],
            waits=[("ve", lv_a)], inc="sc")
        lp_prev_sc = lv_s
        def lpq_b(e):
            e.tensor_mul(wsc[:, 96:128], wsc[:, 32:64], wsc[:, 64:96])
            gap(e)
            e.tensor_scalar(wsc[:, 96:128], wsc[:, 96:128], -1.0, 2.0,
                            op0=ALU.mult, op1=ALU.add)
            gap(e)
            e.tensor_mul(wsc[:, 128:160], wsc[:, 64:96], wsc[:, 96:128])
            gap(e)
            e.scalar_tensor_tensor(wraw[:].rearrange("p (b k) -> p b k", k=32),
                                   wraw[:].rearrange("p (b k) -> p b k", k=32), 1.0,
                                   wsc[:, 128:160].unsqueeze(2).broadcast_to([128, 32, 32]),
                                   op0=ALU.mult, op1=ALU.mult)
            e.tensor_scalar(wraw[:], wraw[:], MAGIC, None, op0=ALU.add)
            return e.scalar_tensor_tensor(
                wqt[:].rearrange("p (b k) -> p b k", k=32),
                wraw[:].rearrange("p (b k) -> p b k", k=32), -MAGIC,
                wsc[:, 32:64].unsqueeze(2).broadcast_to([128, 32, 32]),
                op0=ALU.add, op1=ALU.mult)
        lp_prev_v = s.emit("vector", lpq_b, waits=[("sc", lv_s)], inc="ve")
        lp_prev_sp = s.dma("gpsimd", dst_ap, wqt[:], waits=[("ve", lp_prev_v)], inc="gQ")
    lpq_done = lp_prev_sp

    g_upw = 0
    for kc in range(8):
        g_upw = s.xdma("sync", lambda e, kc=kc: e.dma_start_transpose(
            wsb[:, kc, :], upq_sp[:, kc * 128:(kc + 1) * 128]),
            waits=[("gQ", lpq_done)], inc="sQ")
    _dummy = s.dma("gpsimd", wsb[:, 0:1, 0:1], upq_sp[0:1, 0:1]) if False else None


    ln_xn = [0] * len(GRP)
    ln_spill = [0] * len(GRP)
    stats_pe = [0] * len(GRP)
    bc_pe = [0] * len(GRP)
    rows_ve = [0] * len(GRP)

    for g, (t0, tw, own) in enumerate(GRP):
        buf = 0
        w = [] if g < 1 else [("ve", ln_xn[g - 1])]
        ld = s.dma("sync", xg[buf][:, :, 0:tw], rr(x_fm)[:, :, t0:t0 + tw],
                   waits=w, inc="sQ")
        sqw = [("sQ", ld)] + ([("ve", ln_xn[g - 1])] if g >= 1 else [])
        sq = s.emit("scalar", lambda e, b=buf, tw=tw: e.activation(
            x2g[:, :, 0:tw], xg[b][:, :, 0:tw], ACT.Square), waits=sqw, inc="sc")
        def statmm(e, b=buf, tw=tw):
            for c in range(8):
                e.matmul(ps[4][0:1, 0:tw], f32r(ones_col[:]), f32r(xg[b][:, c, 0:tw]),
                         start=(c == 0), stop=(c == 7))
            for c in range(8):
                i = e.matmul(ps[5][0:1, 0:tw], f32r(ones_col[:]), f32r(x2g[:, c, 0:tw]),
                             start=(c == 0), stop=(c == 7))
            return i
        mmw = [("sc", sq)] + ([("ve", rows_ve[g - 1])] if g >= 1 else [])
        stats_pe[g] = s.emit("tensor", statmm, waits=mmw, inc="pe")
        def rows1(e, tw=tw):
            e.tensor_scalar(rowb[0:1, 0:tw], ps[4][0:1, 0:tw], 1.0 / HID, None, op0=ALU.mult)
            e.tensor_scalar(rowb[0:1, GW:GW + tw], ps[5][0:1, 0:tw], 1.0 / HID, None, op0=ALU.mult)
            gap(e)
            e.tensor_mul(rowb[0:1, 2 * GW:2 * GW + tw], rowb[0:1, 0:tw], rowb[0:1, 0:tw])
            gap(e)
            return e.tensor_sub(rowb[0:1, GW:GW + tw], rowb[0:1, GW:GW + tw],
                                rowb[0:1, 2 * GW:2 * GW + tw])
        v1 = s.emit("vector", rows1, waits=[("pe", stats_pe[g])], inc="ve")
        def rows2(e, tw=tw):
            e.activation(rowb[0:1, 2 * GW:2 * GW + tw], rowb[0:1, GW:GW + tw], ACT.Ln, bias=1e-5)
            return e.activation(rowb[0:1, 3 * GW:3 * GW + tw], rowb[0:1, 2 * GW:2 * GW + tw],
                                ACT.Exp, scale=-0.5)
        s1 = s.emit("scalar", rows2, waits=[("ve", v1)], inc="sc")
        rows_ve[g] = s.emit("vector", lambda e, tw=tw: e.scalar_tensor_tensor(
            rowb[0:1, 2 * GW:2 * GW + tw], rowb[0:1, 0:tw], -1.0,
            rowb[0:1, 3 * GW:3 * GW + tw], op0=ALU.mult, op1=ALU.mult),
            waits=[("sc", s1)], inc="ve")
        def bc(e, tw=tw):
            e.matmul(ps[6][:, 0:tw], f32r(ones_row[:]), f32r(rowb[0:1, 3 * GW:3 * GW + tw]),
                     start=True, stop=True)
            return e.matmul(ps[7][:, 0:tw], f32r(ones_row[:]), f32r(rowb[0:1, 2 * GW:2 * GW + tw]),
                            start=True, stop=True)
        bcw = [("ve", rows_ve[g])] + ([("ve", ln_xn[g - 1])] if g >= 1 else [])
        bc_pe[g] = s.emit("tensor", bc, waits=bcw, inc="pe")
        gi = g - 1
        def xnmk(e, b=buf, tw=tw, own=own, gi=gi):
            for c in range(8):
                e.tensor_mul(x2g[:, c, 0:tw], xg[b][:, c, 0:tw], ps[6][:, 0:tw])
                e.tensor_add(xng[b][:, c, 0:tw], x2g[:, c, 0:tw], ps[7][:, 0:tw])
            last = None
            if own:
                for c in range(8):
                    last = e.tensor_reduce(amaxw[:, gi * 8 + c:gi * 8 + c + 1],
                                           xng[b][:, c, 0:tw], axis=AX.X,
                                           op=ALU.max, apply_absolute_value=True)
            if last is None:
                last = e.tensor_copy(tmpcol[:], cstb[:, 0:1])
            return last
        xnw = [("pe", bc_pe[g]), ("ve", ve_init)]
        if g >= 1:
            xnw.append(("gQ", ln_spill[g - 1]))
        ln_xn[g] = s.emit("vector", xnmk, waits=xnw, inc="ve")
        ln_spill[g] = s.dma("gpsimd", rr(xn_sp)[:, :, t0:t0 + tw], xng[buf][:, :, 0:tw],
                            waits=[("ve", ln_xn[g])], inc="gQ")

    def ar_round(slot_vals, tag):
        vw = [s.cnt["ve"]]
        for slot, col in slot_vals:
            g1 = s.xdma("sync", lambda e, c=col: e.dma_start(out=redrow[:], in_=c),
                        waits=[("ve", max(vw))], inc="sQ")
            vv = s.emit("vector", lambda e, sl=slot: e.tensor_reduce(
                scal[0:1, sl:sl + 1], redrow[:], axis=AX.X, op=ALU.max),
                waits=[("sQ", g1)], inc="ve")
            vw.append(vv)
        gin = s.dma("gpsimd", cc_in, scal[0:1, 0:8], waits=[("ve", max(vw))])
        ccv = s.emit("gpsimd", lambda e: e.collective_compute(
            "AllReduce", ALU.max, replica_groups=[list(range(NCORE))],
            ins=[cc_in], outs=[cc_out]), waits=[("gQ", gin)], inc="cc")
        gout = s.dma("gpsimd", ccs[:], cc_out, waits=[("cc", ccv)])
        return gout

    fin_xn = s.emit("vector", lambda e: (
        e.tensor_copy(tmpcol[:], cstb[:, 0:1]),
        e.tensor_reduce(amax_xn[:], amaxw[:, 0:64], axis=AX.X, op=ALU.max),
        e.tensor_copy(tmpcol[:], cstb[:, 0:1]))[-1],
        waits=[("ve", ln_xn[-1])], inc="ve")
    g_ar1 = ar_round([(0, amax_xn[:, 0:1])], "xn")
    def mkscale(slot, bc0, bc1, wait_gq):
        sv = s.emit("scalar", lambda e, sl=slot: (
            e.activation(scal[0:1, 11:12], ccs[0:1, sl:sl + 1], ACT.Ln),
            e.activation(scal[0:1, 8:9], scal[0:1, 11:12], ACT.Exp, scale=-1.0))[-1],
            waits=[("gQ", wait_gq), ("pe", s.cnt["pe"]), ("ve", s.cnt["ve"])], inc="sc")
        def nrfix(e, sl=slot):
            e.tensor_mul(scal[0:1, 12:13], ccs[0:1, sl:sl + 1], scal[0:1, 8:9])
            gap(e)
            e.tensor_scalar(scal[0:1, 12:13], scal[0:1, 12:13], -1.0, 2.0,
                            op0=ALU.mult, op1=ALU.add)
            gap(e)
            e.tensor_mul(scal[0:1, 13:14], scal[0:1, 8:9], scal[0:1, 12:13])
            gap(e)
            return e.tensor_copy(scal[0:1, 8:9], scal[0:1, 13:14])
        vv = s.emit("vector", nrfix, waits=[("sc", sv)], inc="ve")
        def sc2(e, sl=slot):
            gap(e)
            e.tensor_scalar(scal[0:1, 9:10], scal[0:1, 8:9], QMAX, None, op0=ALU.mult)
            e.tensor_scalar(scal[0:1, 10:11], ccs[0:1, sl:sl + 1], 1.0 / QMAX, None,
                            op0=ALU.mult)
            return gap(e) or e.tensor_copy(tmpcol[:], cstb[:, 0:1])
        vv = s.emit("vector", sc2, waits=[("ve", vv)], inc="ve")
        mm = s.emit("tensor", lambda e: (
            e.matmul(ps[6][:, 0:1], f32r(ones_row[:]), f32r(scal[0:1, 9:10]), start=True, stop=True),
            e.matmul(ps[6][:, 1:2], f32r(ones_row[:]), f32r(scal[0:1, 10:11]), start=True, stop=True))[-1],
            waits=[("ve", vv)], inc="pe")
        return s.emit("vector", lambda e, b0=bc0, b1=bc1: (
            e.tensor_copy(sbc[:, b0:b0 + 1], ps[6][:, 0:1]),
            e.tensor_copy(sbc[:, b1:b1 + 1], ps[6][:, 1:2]))[-1],
            waits=[("pe", mm)], inc="ve")

    sx_ve = mkscale(0, 0, 1, g_ar1)

    x1_ve = [0] * NGRP
    for g in range(NGRP):
        buf = 0
        t0 = HALO + g * GW
        ld = s.dma("sync", xng[buf][:], rr(xn_sp)[:, :, t0:t0 + GW],
                   waits=[("ve", sx_ve if g < 1 else x1_ve[g - 1]),
                          ("gQ", ln_spill[-1])], inc="sQ")
        def x1mk(e, b=buf, g=g):
            for c in range(8):
                e.tensor_scalar(tq[0][:, :], xng[b][:, c, :], sbc[:, 0:1], MAGIC,
                                op0=ALU.mult, op1=ALU.add)
                i = e.tensor_scalar(x1sb[:, c, g * GW:(g + 1) * GW], tq[0][:, :],
                                    -MAGIC, sbc[:, 1:2], op0=ALU.add, op1=ALU.mult)
            return i
        x1_ve[g] = s.emit("vector", x1mk, waits=[("sQ", ld), ("ve", sx_ve)], inc="ve")

    def mm_3072(spill, amax_slot, wready, tag):
        tile_ev = {}
        evacof = {}
        pe_last = 0
        sp_last = 0
        idx = 0
        for oc in range(24):
            for g in range(NGRP):
                bank = idx % 4
                def mmfn(e, oc=oc, g=g, bank=bank):
                    for kc in range(8):
                        i = e.matmul(ps[bank][:, :], wsb[:, kc, oc * 128:(oc + 1) * 128],
                                     x1sb[:, kc, g * GW:(g + 1) * GW],
                                     start=(kc == 0), stop=(kc == 7))
                    return i
                mw = [("ve", x1_ve[g])] + wready
                if idx >= 2:
                    mw.append(tile_ev[idx - 2])
                if idx >= 4:
                    mw.append(evacof[idx - 4])
                pe_last = s.emit("tensor", mmfn, waits=mw, inc="pe")
                eng = "scalar" if (idx % 2 == 0) else "vector"
                eb = ev[idx % 2]
                if eng == "scalar":
                    ecnt = s.emit("scalar", lambda e, b=bank, eb=eb: e.copy(eb[:], ps[b][:, :]),
                                  waits=[("pe", pe_last)], inc="sc")
                    evacof[idx] = ("sc", ecnt)
                else:
                    ecnt = s.emit("vector", lambda e, b=bank, eb=eb: e.tensor_copy(eb[:], ps[b][:, :]),
                                  waits=[("pe", pe_last)], inc="ve")
                    evacof[idx] = ("ve", ecnt)
                am = s.emit("vector", lambda e, eb=eb, ix=idx: e.tensor_reduce(
                    amaxw[:, ix:ix + 1], eb[:], axis=AX.X, op=ALU.max,
                    apply_absolute_value=True), waits=[evacof[idx]], inc="ve")
                sp_last = s.dma("gpsimd", rr(spill)[:, oc, g * GW:(g + 1) * GW], eb[:],
                                waits=[("ve", am)], inc="gQ")
                tile_ev[idx] = ("gQ", sp_last)
                idx += 1
        fin = s.emit("vector", lambda e, sl=amax_slot: (
            e.tensor_copy(tmpcol[:], cstb[:, 0:1]),
            e.tensor_reduce(amax_t[:, sl:sl + 1], amaxw[:, 0:192], axis=AX.X, op=ALU.max),
            e.tensor_copy(tmpcol[:], cstb[:, 0:1]),
            e.memset(amaxw[:, 0:192], 0.0))[-1], inc="ve")
        return pe_last, sp_last

    up_pe, up_sp_done = mm_3072(up_sp, 0, [("sQ", g_upw)], "up")
    g_gw = 0
    for kc in range(8):
        g_gw = s.xdma("sync", lambda e, kc=kc: e.dma_start_transpose(
            wsb[:, kc, :], gateq_sp[:, kc * 128:(kc + 1) * 128]),
            waits=[("pe", up_pe), ("gQ", lpq_done)], inc="sQ")
    gate_pe, gate_sp_done = mm_3072(gate_sp, 1, [("sQ", g_gw)], "gate")
    wd = wsb[:].rearrange("p c o -> p (c o)").rearrange(
        "p (c o) -> p c o", o=HID)
    g_dnw = 0
    for kc in range(24):
        g_dnw = s.xdma("sync", lambda e, kc=kc: e.dma_start_transpose(
            wd[:, kc, :], downq_sp[:, kc * 128:(kc + 1) * 128]),
            waits=[("pe", gate_pe)], inc="sQ")

    g_ar2 = ar_round([(1, amax_t[:, 0:1]), (2, amax_t[:, 1:2])], "upgate")
    sup_ve = mkscale(1, 2, 3, g_ar2)
    sgate_ve = mkscale(2, 4, 5, g_ar2)

    pa_done = 0
    pa_v1_prev = 0
    for oc in range(24):
        for g in range(NGRP):
            ld = s.dma("sync", tq[0][:], rr(gate_sp)[:, oc, g * GW:(g + 1) * GW],
                       waits=[("ve", max(sup_ve, sgate_ve, pa_v1_prev)),
                              ("gQ", gate_sp_done)], inc="sQ")
            def pa_v1(e):
                e.tensor_scalar(tq[1][:], tq[0][:], sbc[:, 4:5], MAGIC, op0=ALU.mult, op1=ALU.add)
                return e.tensor_scalar(tq[2][:], tq[1][:], -MAGIC, sbc[:, 5:6],
                                       op0=ALU.add, op1=ALU.mult)
            v1 = s.emit("vector", pa_v1, waits=[("sQ", ld)], inc="ve")
            pa_v1_prev = v1
            sg = s.emit("scalar", lambda e: e.activation(tq[3][:], tq[2][:], ACT.Sigmoid),
                        waits=[("ve", v1)], inc="sc")
            def pa_v2(e, ix=oc * NGRP + g):
                e.tensor_scalar(tq[4][:], tq[3][:], 65536.0, MAGIC, op0=ALU.mult, op1=ALU.add)
                e.tensor_scalar(tq[1][:], tq[4][:], -MAGIC, 65535.0, op0=ALU.add, op1=ALU.min)
                e.scalar_tensor_tensor(tq[5][:], tq[1][:], 1.0 / 65536.0, tq[2][:],
                                       op0=ALU.mult, op1=ALU.mult)
                return e.tensor_reduce(amaxw[:, ix:ix + 1], tq[5][:], axis=AX.X,
                                       op=ALU.max, apply_absolute_value=True)
            v2 = s.emit("vector", pa_v2, waits=[("sc", sg), ("gQ", pa_done)], inc="ve")
            pa_done = s.dma("gpsimd", rr(g2p_sp)[:, oc, g * GW:(g + 1) * GW], tq[5][:],
                            waits=[("ve", v2)], inc="gQ")

    s.emit("vector", lambda e: (
        e.tensor_copy(tmpcol[:], cstb[:, 0:1]),
        e.tensor_reduce(amax_t[:, 2:3], amaxw[:, 0:192], axis=AX.X, op=ALU.max),
        e.tensor_copy(tmpcol[:], cstb[:, 0:1]),
        e.memset(amaxw[:, 0:192], 0.0))[-1], inc="ve")
    g_ar3 = ar_round([(3, amax_t[:, 2:3])], "g2")
    sg2_ve = mkscale(3, 6, 7, g_ar3)

    pb_done = 0
    pb_v_prev = 0
    for oc in range(24):
        for g in range(NGRP):
            ld1 = s.dma("sync", tq[0][:], rr(g2p_sp)[:, oc, g * GW:(g + 1) * GW],
                        waits=[("ve", max(sg2_ve, pb_v_prev)), ("gQ", pa_done)], inc="sQ")
            ld2 = s.dma("sync", tq[1][:], rr(up_sp)[:, oc, g * GW:(g + 1) * GW],
                        waits=[("gQ", pb_done)], inc="sQ")
            def pb_v(e, ix=oc * NGRP + g):
                e.tensor_scalar(tq[2][:], tq[0][:], sbc[:, 6:7], MAGIC, op0=ALU.mult, op1=ALU.add)
                e.tensor_scalar(tq[3][:], tq[2][:], -MAGIC, sbc[:, 7:8], op0=ALU.add, op1=ALU.mult)
                e.tensor_scalar(tq[4][:], tq[1][:], sbc[:, 2:3], MAGIC, op0=ALU.mult, op1=ALU.add)
                e.tensor_scalar(tq[5][:], tq[4][:], -MAGIC, sbc[:, 3:4], op0=ALU.add, op1=ALU.mult)
                e.tensor_mul(tq[1][:], tq[3][:], tq[5][:])
                return e.tensor_reduce(amaxw[:, ix:ix + 1], tq[1][:], axis=AX.X,
                                       op=ALU.max, apply_absolute_value=True)
            vb = s.emit("vector", pb_v, waits=[("sQ", ld2), ("gQ", pb_done)], inc="ve")
            pb_v_prev = vb
            pb_done = s.dma("gpsimd", rr(o2p_sp)[:, oc, g * GW:(g + 1) * GW], tq[1][:],
                            waits=[("ve", vb)], inc="gQ")

    s.emit("vector", lambda e: (
        e.tensor_copy(tmpcol[:], cstb[:, 0:1]),
        e.tensor_reduce(amax_t[:, 3:4], amaxw[:, 0:192], axis=AX.X, op=ALU.max),
        e.tensor_copy(tmpcol[:], cstb[:, 0:1]))[-1], inc="ve")
    g_ar4 = ar_round([(4, amax_t[:, 3:4])], "o2")
    so2_ve = mkscale(4, 8, 9, g_ar4)

    dn_last = 0
    out_done = 0
    for g in range(NGRP):
        half_pe = []
        for half in range(2):
            qv = 0
            for kk in range(12):
                kc = half * 12 + kk
                ld = s.dma("sync", tq[kc % 2][:], rr(o2p_sp)[:, kc, g * GW:(g + 1) * GW],
                           waits=[("ve", so2_ve), ("gQ", pb_done)] +
                                 ([("ve", qv)] if kk >= 2 else []) +
                                 ([("pe", dn_last)] if kk < 2 else []), inc="sQ")
                def qc(e, kc=kc, kk=kk):
                    e.tensor_scalar(tq[2 + kc % 2][:], tq[kc % 2][:], sbc[:, 8:9], MAGIC,
                                    op0=ALU.mult, op1=ALU.add)
                    return e.tensor_scalar(o2q[:, kk, :], tq[2 + kc % 2][:], -MAGIC,
                                           sbc[:, 9:10], op0=ALU.add, op1=ALU.mult)
                qv = s.emit("vector", qc, waits=[("sQ", ld)], inc="ve")
            for oc in range(8):
                bank = oc
                def dnmm(e, oc=oc, bank=bank, half=half):
                    for kk in range(12):
                        kc = half * 12 + kk
                        i = e.matmul(ps[bank][:, :], wd[:, kc, oc * 128:(oc + 1) * 128],
                                     o2q[:, kk, :], start=(kc == 0), stop=(kc == 23))
                    return i
                mw = [("ve", qv), ("sQ", g_dnw)]
                if half == 1 and oc >= 4:
                    mw.append(("gQ", out_done))
                dn_pe = s.emit("tensor", dnmm, waits=mw, inc="pe")
                dn_last = dn_pe
                half_pe.append(dn_pe)
        xb = xg[0]
        t0 = HALO + g * GW
        ldx = s.dma("sync", xb[:], rr(x_fm)[:, :, t0:t0 + GW], waits=[("ve", qv)], inc="sQ")
        for oc in range(8):
            bank = oc
            rv = s.emit("vector", lambda e, oc=oc, bank=bank, xb=xb: e.tensor_add(
                ev[oc % 2][:], ps[bank][:, :], xb[:, oc, :]),
                waits=[("pe", half_pe[8 + oc]), ("sQ", ldx)] +
                      ([("gQ", out_done)] if oc >= 2 else []), inc="ve")
            out_done = s.dma("gpsimd", rr(out_fm)[:, oc, g * GW:(g + 1) * GW], ev[oc % 2][:],
                             waits=[("ve", rv)], inc="gQ")

    if dbg:
        s.dma("gpsimd", dbg_sc[0:1, 0:12], sbc[0:1, 0:12],
              waits=[("ve", s.cnt["ve"])])

    ATTN(locals())

    s.build(nc, sh)
    return nc




def ATTN(L):
    nc = L["nc"]; s = L["s"]; E = L["E"]; ps = L["ps"]
    gap = L["gap"]; ones_col = L["ones_col"]; ones_row = L["ones_row"]
    xn_sp = L["xn_sp"]; out_fm = L["out_fm"]; x_fm = L["x_fm"]
    cstb = L["cstb"]; tmpcol = L["tmpcol"]; tq = L["tq"]; ev = L["ev"]
    ffn_es = L["ffn_es"]; dbg = L["dbg"]
    from contextlib import ExitStack

    DI = lambda n, sh_, dt_: nc.dram_tensor(n, sh_, dt_, kind="ExternalInput").ap()
    DN = lambda n, sh_, dt_: nc.dram_tensor(n, sh_, dt_, kind="Internal").ap()
    wqkvbT = DI("wqkvbT", [HID, 1540], F32)
    bands_in = DI("bands_in", [6, 512, 256, 128], F32)
    o_wT_in = DI("o_wT_in", [HID, HID], F32)
    g1xT_in = DI("g1xT_in", [HID, 2048], F32)
    g1sT_in = DI("g1sT_in", [64, 2048], F32)
    g1b_in = DI("g1b_in", [128, 16], F32)
    g2T_in = DI("g2T_in", [2048, 32], F32)
    g2b_in = DI("g2b_in", [32, 1], F32)
    vb_in = DI("vb_in", [32, 1], F32)
    ltmask_in = DI("ltmask_in", [4, 32], F32)
    fmask_in = DI("fmask_in", [4, 32], F32)
    ltvec_in = DI("ltvec_in", [4, 1], F32)
    flvec_in = DI("flvec_in", [4, 1], F32)
    onormc_in = DI("onormc_in", [128, 8], F32)
    m_bdtril_in = DI("m_bdtril_in", [128, 512], F32)
    m_ipack_in = DI("m_ipack_in", [128, 512], F32)
    m_sbu_in = DI("m_sbu_in", [128, 128], F32)
    m_triu_in = DI("m_triu_in", [128, 128], F32)
    ident_in = DI("ident_in", [128, 128], F32)
    mh_in = DI("mh_in", [128, 2], F32)

    xnb_sp = DN("xnb_sp", [HID, NT], BF16)
    xnb_ag = DN("xnb_ag", [2, HID, NT], BF16)
    qkv_sp = DN("qkv_sp", [1536, NTP], BF16)
    beta_sp = DN("beta_sp", [2, NTP], F32)
    beta16_sp = DN("beta16_sp", [16, NTP], BF16)
    qc_sp = DN("qc_sp", [512, NTP], BF16)
    kc_sp = DN("kc_sp", [512, NTP], BF16)
    fir4_sp = DN("fir4_sp", [4, 512, NTP], BF16)
    fir4_ag = DN("fir4_ag", [2, 4, 512, NTP], BF16)
    vc_sp = fir4_sp[3]
    khat_sp = DN("khat_sp", [2, 256, GW], BF16)
    vb_sp = DN("vb_sp", [2, 256, GW], BF16)
    do_sp = DN("do_sp", [2, 2, NT, 256], F32)
    do_ag = DN("do_ag", [2, 2, 2, NT, 256], F32)

    fence = [("gQ", s.cnt["gQ"]), ("sQ", s.cnt["sQ"]), ("ve", s.cnt["ve"]),
             ("sc", s.cnt["sc"]), ("pe", s.cnt["pe"])]
    for eng in ENGS:
        s.emit(eng, lambda e: None, waits=fence)

    xcast = s.dma("gpsimd", rr(xnb_sp), rr(xn_sp)[:, :, HALO:], waits=fence)
    agx = s.emit("gpsimd", lambda e: e.collective_compute(
        "AllGather", ALU.bypass, replica_groups=PAIRS,
        ins=[xnb_sp], outs=[xnb_ag]), waits=[("gQ", xcast)], inc="cc")

    ffn_es.close()
    ident = E(nc.sbuf_tensor("ident", [128, 128], BF16))
    identf = E(nc.sbuf_tensor("identf", [128, 128], F32))
    p12_es = ExitStack()
    E12 = p12_es.enter_context
    wp = E12(nc.sbuf_tensor("wp", [128, 8, 1540], BF16))
    xnb = E12(nc.sbuf_tensor("xnb", [128, 8, GW], BF16))
    evb = [E12(nc.sbuf_tensor(f"evb{i}", [128, GW], BF16)) for i in range(4)]
    pdat = [E12(nc.sbuf_tensor(f"pdat{i}", [128, 64 * 66], BF16)) for i in range(2)]
    zt16 = E12(nc.sbuf_tensor("zt16", [128, 256], BF16))
    zt32 = E12(nc.sbuf_tensor("zt32", [128, 256], F32))
    bvt = E12(nc.sbuf_tensor("bvt", [2, GW], F32))
    m_bdtril = E12(nc.sbuf_tensor("m_bdtril", [128, 512], F32))
    m_ipack = E12(nc.sbuf_tensor("m_ipack", [128, 512], BF16))
    m_sbu = E12(nc.sbuf_tensor("m_sbu", [128, 128], F32))
    m_triu = E12(nc.sbuf_tensor("m_triu", [128, 128], F32))

    bndj = [[E12(nc.sbuf_tensor(f"bnd{j}_{i}", [128, 2, 8, 128], BF16)) for i in range(2)]
            for j in range(3)]
    p3_es = ExitStack()
    E3 = p3_es.enter_context
    brow = L["rowb"]
    qf = E3(nc.sbuf_tensor("qf", [128, 2, GW], BF16))
    kf = E3(nc.sbuf_tensor("kf", [128, 2, GW], BF16))
    vf = E3(nc.sbuf_tensor("vf", [128, 2, GW], BF16))
    qh_ = E3(nc.sbuf_tensor("qh_", [128, 2, GW], BF16))
    kh_ = E3(nc.sbuf_tensor("kh_", [128, 2, GW], BF16))
    kbh = E3(nc.sbuf_tensor("kbh", [128, 2, GW], BF16))
    vbf = E3(nc.sbuf_tensor("vbf", [128, 2, GW], BF16))
    ktm = E3(nc.sbuf_tensor("ktm", [128, 4, 256], BF16))
    kbtm = E3(nc.sbuf_tensor("kbtm", [128, 4, 256], BF16))
    vbtm = E3(nc.sbuf_tensor("vbtm", [128, 4, 256], BF16))
    bcol = E3(nc.sbuf_tensor("bcol", [128, 4, 16], BF16))
    bcolf = E3(nc.sbuf_tensor("bcolf", [128, 4, 16], F32))
    sq2 = E3(nc.sbuf_tensor("sq2", [128, 2, GW], F32))
    apack = E3(nc.sbuf_tensor("apack", [128, 512], BF16))
    tpk = [E3(nc.sbuf_tensor(f"tpk{i}", [128, 512], BF16)) for i in range(4)]
    ppk = [E3(nc.sbuf_tensor(f"ppk{i}", [128, 512], BF16)) for i in range(5)]
    prt = [E3(nc.sbuf_tensor(f"prt{i}", [128, 512], BF16)) for i in range(2)]
    utm = E3(nc.sbuf_tensor("utm", [128, 4, 256], BF16))
    wtm = E3(nc.sbuf_tensor("wtm", [128, 4, 256], BF16))
    wnf = E3(nc.sbuf_tensor("wnf", [128, 2, GW], BF16))
    s0b = [E3(nc.sbuf_tensor(f"s0b{i}", [128, 512], BF16)) for i in range(2)]
    y0t = E3(nc.sbuf_tensor("y0t", [128, 256], BF16))
    t1t = E3(nc.sbuf_tensor("t1t", [128, 256], BF16))
    t2t = E3(nc.sbuf_tensor("t2t", [128, 256], BF16))
    uadj = E3(nc.sbuf_tensor("uadj", [128, 256], BF16))
    nnt = E3(nc.sbuf_tensor("nnt", [128, 128], BF16))
    attt = E3(nc.sbuf_tensor("attt", [128, 128], BF16))
    oev = E3(nc.sbuf_tensor("oev", [128, 256], F32))


    zz = s.emit("vector", lambda e: (e.memset(zt16[:], 0.0),
                                     e.memset(zt32[:], 0.0))[-1], inc="ve")
    cl = 0
    for ap_s, ap_d in [(m_bdtril_in, m_bdtril[:]), (m_ipack_in, m_ipack[:]),
                       (m_sbu_in, m_sbu[:]), (m_triu_in, m_triu[:]),
                       (ident_in, ident[:]), (ident_in, identf[:])]:
        cl = s.dma("gpsimd", ap_d, ap_s, waits=fence)
    wpl = s.dma("gpsimd", wp[:], rr(wqkvbT), waits=fence)

    zp = 0
    for oc in range(12):
        zp = s.dma("gpsimd", rr(qkv_sp)[:, oc, 0:PADL], zt16[:, 0:PADL], waits=[("ve", zz)])
        zp = s.dma("gpsimd", rr(qkv_sp)[:, oc, NTP - PADR:NTP], zt16[:, 0:PADR],
                   waits=[("ve", zz)])
    zp = s.dma("gpsimd", beta_sp[0:2, 0:PADL], zt32[0:2, 0:PADL],
               waits=[("ve", zz)])
    zp = s.dma("gpsimd", beta_sp[0:2, NTP - PADR:NTP], zt32[0:2, 0:PADR],
               waits=[("ve", zz)])
    for c0 in range(NTP // 256):
        zp = s.dma("gpsimd", beta16_sp[:, c0 * 256:c0 * 256 + 256],
                   zt16[0:16, 0:256], waits=[("ve", zz)])

    p1sp = zp
    p1ev = {}
    for g in range(16):
        sl, off = g // 8, (g % 8) * GW
        ldr = s.dma("sync", xnb[:], rr(xnb_ag[sl])[:, :, off:off + GW],
                    waits=[("cc", agx), ("ve", p1ev.get(g - 1, 0))], inc="sQ")
        last = 0
        for oc in range(13):
            bank = oc % 2
            mwid = 128 if oc < 12 else 2
            def p1mm(e, oc=oc, bank=bank, mwid=mwid):
                for kc in range(8):
                    i = e.matmul(ps[bank][0:mwid, :], wp[:, kc, oc * 128:oc * 128 + mwid],
                                 xnb[:, kc, :], start=(kc == 0), stop=(kc == 7))
                return i
            pw = [("sQ", ldr), ("gQ", wpl), ("gQ", p1sp)]
            mm = s.emit("tensor", p1mm, waits=pw, inc="pe")
            tcol = PADL + g * GW
            if oc < 12:
                vv = s.emit("vector", lambda e, b=bank, oc=oc: e.tensor_copy(
                    evb[oc % 4][:], ps[b][:, :]), waits=[("pe", mm)], inc="ve")
                last = vv
                p1sp = s.dma("gpsimd", rr(qkv_sp)[:, oc, tcol:tcol + GW],
                             evb[oc % 4][:], waits=[("ve", vv)], inc="gQ")
            else:
                sg = s.emit("scalar", lambda e, b=bank: e.activation(
                    bvt[0:2, :], ps[b][0:2, :], ACT.Sigmoid),
                    waits=[("pe", mm)], inc="sc")
                p1sp = s.dma("gpsimd", beta_sp[0:2, tcol:tcol + GW], bvt[0:2, :],
                             waits=[("sc", sg)], inc="gQ")
                p1sp = s.dma("gpsimd", beta16_sp[0:2, tcol:tcol + GW], bvt[0:2, :],
                             waits=[("sc", sg)], inc="gQ")
        p1ev[g] = last
    p1_done = p1sp

    NW2 = NTP // 128

    def conv_stream(src_sp, jobs, wait_gq, tag):
        done = 0
        pdat_guard = {}
        bank_guard = {}
        spill_last = 0
        evb_guard = {}
        for cg in range(8):
            buf = cg % 2
            trw = [("gQ", wait_gq), ("pe", pdat_guard.get(cg - 2, 0)),
                   ("gQ", spill_last)]
            tr = s.xdma("sync", lambda e, cg=cg, buf=buf, src=src_sp: e.dma_start_transpose(
                pdat[buf][:], src[cg * 64:cg * 64 + 64, :].rearrange(
                    "c (w x) -> (c w) x", x=128)), waits=trw, inc="sQ")
            for c8 in range(8):
                bpar = (cg * 8 + c8) % 2
                bls = []
                for j, (fid, dst, sil) in enumerate(jobs):
                    for hh2 in range(2):
                        bl = s.dma("gpsimd", bndj[j][bpar][:, hh2],
                                   bands_in[fid, cg * 64 + c8 * 8:cg * 64 + c8 * 8 + 8,
                                            hh2 * 128:hh2 * 128 + 128, :]
                                   .rearrange("c p t -> p c t"),
                                   waits=[("pe", bank_guard.get(("b", cg, c8 - 2, j), 0))],
                                   inc="gQ")
                    bank_guard[("b", cg, c8, j)] = 0
                    bls.append(bl)
                mmlast = 0
                for ci in range(8):
                    c = c8 * 8 + ci
                    slot = ci % 4
                    for j, (fid, dst, sil) in enumerate(jobs):
                        bank = 2 + j
                        def cmm(e, buf=buf, c=c, bank=bank, slot=slot, j=j, bpar=bpar,
                                ci=ci):
                            base = c * NW2
                            e.matmul(ps[bank][0:64, slot * 128:slot * 128 + 128],
                                     pdat[buf][:, base:base + 64],
                                     bndj[j][bpar][:, 0, ci, :], start=True, stop=False)
                            return e.matmul(ps[bank][0:64, slot * 128:slot * 128 + 128],
                                            pdat[buf][:, base + 1:base + 65],
                                            bndj[j][bpar][:, 1, ci, :],
                                            start=False, stop=True)
                        mw = [("sQ", tr), ("gQ", bls[j]),
                              bank_guard.get((bank, slot), ("pe", 0))]
                        mm = s.emit("tensor", cmm, waits=mw, inc="pe")
                        mmlast = mm
                        if ci == 7:
                            ew = [("pe", mm), evb_guard.get(j, ("pe", 0))]
                            if sil:
                                ecnt = s.emit("scalar", lambda e, bank=bank, j=j: e.activation(
                                    evb[j][0:64, :], ps[bank][0:64, :], ACT.Silu),
                                    waits=ew, inc="sc")
                                ekey = ("sc", ecnt)
                            else:
                                ecnt = s.emit("vector", lambda e, bank=bank, j=j: e.tensor_copy(
                                    evb[j][0:64, :], ps[bank][0:64, :]),
                                    waits=ew, inc="ve")
                                ekey = ("ve", ecnt)
                            c0 = cg * 64 + c8 * 8 + 4
                            spill_last = s.dma(
                                "gpsimd",
                                dst[c0:c0 + 4, PADL:PADL + 8192].rearrange(
                                    "c (w t) -> w c t", t=128),
                                evb[j][0:64, :].rearrange("w (c t) -> w c t", t=128),
                                waits=[ekey], inc="gQ")
                            for sl_ in range(4):
                                bank_guard[(bank, sl_)] = ("gQ", spill_last)
                            evb_guard[j] = ("gQ", spill_last)
                        elif ci == 3:
                            ew = [("pe", mm), evb_guard.get(j, ("pe", 0))]
                            if sil:
                                ecnt = s.emit("scalar", lambda e, bank=bank, j=j: e.activation(
                                    evb[j][0:64, :], ps[bank][0:64, :], ACT.Silu),
                                    waits=ew, inc="sc")
                                ekey = ("sc", ecnt)
                            else:
                                ecnt = s.emit("vector", lambda e, bank=bank, j=j: e.tensor_copy(
                                    evb[j][0:64, :], ps[bank][0:64, :]),
                                    waits=ew, inc="ve")
                                ekey = ("ve", ecnt)
                            c0 = cg * 64 + c8 * 8
                            spill_last = s.dma(
                                "gpsimd",
                                dst[c0:c0 + 4, PADL:PADL + 8192].rearrange(
                                    "c (w t) -> w c t", t=128),
                                evb[j][0:64, :].rearrange("w (c t) -> w c t", t=128),
                                waits=[ekey], inc="gQ")
                            for sl_ in range(4):
                                bank_guard[(bank, sl_)] = ("gQ", spill_last)
                            evb_guard[j] = ("gQ", spill_last)
                pdat_guard[cg] = mmlast
        return spill_last

    zp2 = 0
    for dst in (qc_sp, kc_sp, fir4_sp[0], fir4_sp[1], fir4_sp[2], fir4_sp[3]):
        for col0, wdt in ((0, PADL), (NTP - PADR, PADR)):
            for cc4 in range(4):
                zp2 = s.dma("gpsimd", rr(dst)[:, cc4, col0:col0 + wdt],
                            zt16[:, 0:wdt], waits=[("ve", zz)])

    qcd = conv_stream(qkv_sp[0:512], [(0, qc_sp, True)], p1_done, "q")
    kcd = conv_stream(qkv_sp[512:1024], [(1, kc_sp, True)], p1_done, "k")
    vcd = conv_stream(qkv_sp[1024:1536], [(2, vc_sp, True)], p1_done, "v")
    fird = conv_stream(vc_sp, [(3, fir4_sp[0], False), (4, fir4_sp[1], False),
                               (5, fir4_sp[2], False)], max(vcd, zp2), "fir")

    agf = s.emit("gpsimd", lambda e: e.collective_compute(
        "AllGather", ALU.bypass, replica_groups=PAIRS,
        ins=[fir4_sp], outs=[fir4_ag]), waits=[("gQ", fird)], inc="cc")

    NSC = 4
    vsent = {}
    dospill = 0
    for h in range(2):
        sinit = s.emit("vector", lambda e, h=h: e.memset(s0b[h][:], 0.0), inc="ve")
        for g in range(16):
            ts0 = PADL + g * GW
            gpar = g % 2
            w_lds = [("gQ", fird), ("pe", s.cnt["pe"]), ("ve", s.cnt["ve"])]
            l1 = s.dma("sync", qf[:], rr(qc_sp[256 * h:256 * h + 256])[:, :, ts0:ts0 + GW],
                       waits=w_lds, inc="sQ")
            l2 = s.dma("sync", kf[:], rr(kc_sp[256 * h:256 * h + 256])[:, :, ts0:ts0 + GW],
                       inc="sQ")
            l3 = s.dma("sync", vf[:], rr(vc_sp[256 * h:256 * h + 256])[:, :, ts0:ts0 + GW],
                       inc="sQ")
            l4 = s.dma("sync", brow[0:1, 0:GW], beta_sp[h:h + 1, ts0:ts0 + GW], inc="sQ")
            sqs = s.emit("scalar", lambda e: e.activation(sq2[:], qf[:], ACT.Square),
                         waits=[("sQ", l1)], inc="sc")
            def ssmm(e):
                for c2 in range(2):
                    i = e.matmul(ps[6][0:1, 0:GW], ones_col[:], sq2[:, c2, :],
                                 start=(c2 == 0), stop=(c2 == 1))
                return i
            mq = s.emit("tensor", ssmm, waits=[("sc", sqs)], inc="pe")
            rq = s.emit("scalar", lambda e: (
                e.activation(brow[0:1, GW:2 * GW], ps[6][0:1, 0:GW], ACT.Ln, bias=1e-6),
                e.activation(brow[0:1, 2 * GW:3 * GW], brow[0:1, GW:2 * GW], ACT.Exp,
                             scale=-0.5))[-1], waits=[("pe", mq)], inc="sc")
            sqs2 = s.emit("scalar", lambda e: e.activation(sq2[:], kf[:], ACT.Square),
                          waits=[("sQ", l2), ("sc", rq)], inc="sc")
            mk = s.emit("tensor", ssmm, waits=[("sc", sqs2), ("sc", rq)], inc="pe")
            rk = s.emit("scalar", lambda e: (
                e.activation(brow[0:1, GW:2 * GW], ps[6][0:1, 0:GW], ACT.Ln, bias=1e-6),
                e.activation(brow[0:1, 3 * GW:4 * GW], brow[0:1, GW:2 * GW], ACT.Exp,
                             scale=-0.5))[-1], waits=[("pe", mk)], inc="sc")
            def bcmm(e):
                e.matmul(ps[6][:, 0:GW], ones_row[:].bitcast(F32), brow[0:1, 2 * GW:3 * GW],
                         start=True, stop=True)
                e.matmul(ps[7][:, 0:GW], ones_row[:].bitcast(F32), brow[0:1, 3 * GW:4 * GW],
                         start=True, stop=True)
                return e.matmul(ps[5][:, 0:GW], ones_row[:].bitcast(F32), brow[0:1, 0:GW],
                                start=True, stop=True)
            bc = s.emit("tensor", bcmm, waits=[("sc", rk)], inc="pe")
            def scl(e):
                for c2 in range(2):
                    e.tensor_mul(qh_[:, c2, :], qf[:, c2, :], ps[6][:, :])
                    e.tensor_mul(kh_[:, c2, :], kf[:, c2, :], ps[7][:, :])
                for c2 in range(2):
                    e.tensor_mul(kbh[:, c2, :], kh_[:, c2, :], ps[5][:, :])
                    i = e.tensor_mul(vbf[:, c2, :], vf[:, c2, :], ps[5][:, :])
                return i
            sc1 = s.emit("vector", scl, waits=[("pe", bc), ("sQ", l3), ("sQ", l4)], inc="ve")
            sp1 = s.dma("gpsimd", rr(khat_sp[gpar]), kh_[:], waits=[("ve", sc1)], inc="gQ")
            sp2 = s.dma("gpsimd", rr(vb_sp[gpar]), vbf[:], waits=[("ve", sc1)], inc="gQ")
            trs = []
            for tt in range(4):
                trs.append(s.xdma("sync", lambda e, tt=tt, gpar=gpar: e.dma_start_transpose(
                    ktm[:, tt, :], khat_sp[gpar][:, tt * 128:tt * 128 + 128]),
                    waits=[("gQ", sp2)], inc="sQ"))
                trs.append(s.xdma("sync", lambda e, tt=tt, gpar=gpar: e.dma_start_transpose(
                    vbtm[:, tt, :], vb_sp[gpar][:, tt * 128:tt * 128 + 128]),
                    waits=[("gQ", sp2)], inc="sQ"))
                trs.append(s.xdma("sync", lambda e, tt=tt, ts0=ts0: e.dma_start_transpose(
                    bcol[:, tt, :], beta16_sp[:, ts0 + tt * 128:ts0 + tt * 128 + 128]),
                    waits=[("gQ", p1_done)], inc="sQ"))
            trd = trs[-1]
            def kbtf(e, h=h):
                e.tensor_copy(bcolf[:].rearrange("p a c -> p (a c)"),
                              bcol[:].rearrange("p a c -> p (a c)"))
                gap(e)
                for tt in range(4):
                    r = e.tensor_scalar(kbtm[:, tt, :], ktm[:, tt, :],
                                        bcolf[:, tt, h:h + 1], None, op0=ALU.mult)
                return r
            kbt = s.emit("vector", kbtf, waits=[("sQ", trd)], inc="ve")
            def amm(e):
                for i in range(16):
                    pk, gb = i // 4, i % 4
                    for c2 in range(2):
                        r = e.matmul(ps[2][32 * gb:32 * gb + 32,
                                           pk * 128 + 32 * gb:pk * 128 + 32 * gb + 32],
                                     kbh[:, c2, 32 * i:32 * i + 32],
                                     kh_[:, c2, 32 * i:32 * i + 32],
                                     start=(c2 == 0), stop=(c2 == 1),
                                     tile_position=(0, 32 * gb))
                return r
            am = s.emit("tensor", amm, waits=[("ve", kbt), ("ve", sc1)], inc="pe")
            av = s.emit("vector", lambda e: e.tensor_mul(apack[:], ps[2][:, :], m_bdtril[:]),
                        waits=[("pe", am)], inc="ve")
            tv = s.emit("vector", lambda e: e.transpose(tpk[0][:], apack[:]),
                        waits=[("ve", av)], inc="ve")
            chain_v = tv
            for lvl in range(1, 5):
                pprev = apack if lvl == 1 else ppk[lvl - 1]
                tprev = tpk[lvl - 1]
                def sqmm(e, pprev=pprev, tprev=tprev, lvl=lvl):
                    r = None
                    for pk in range(4):
                        sl = slice(pk * 128, pk * 128 + 128)
                        r = e.matmul(ps[2][:, sl], tprev[:, sl], pprev[:, sl],
                                     start=True, stop=True)
                        if lvl < 4:
                            r = e.matmul(ps[3][:, sl], pprev[:, sl], tprev[:, sl],
                                         start=True, stop=True)
                    return r
                sq_ = s.emit("tensor", sqmm, waits=[("ve", chain_v)], inc="pe")
                def sqev(e, lvl=lvl):
                    e.tensor_copy(ppk[lvl][:], ps[2][:, :])
                    if lvl < 4:
                        return e.tensor_copy(tpk[lvl][:], ps[3][:, :])
                    return gap(e) or e.tensor_copy(tmpcol[:], cstb[:, 0:1])
                chain_v = s.emit("vector", sqev, waits=[("pe", sq_)], inc="ve")
            pv = s.emit("vector", lambda e: e.tensor_sub(prt[0][:], m_ipack[:], tpk[0][:]),
                        waits=[("ve", chain_v)], inc="ve")
            cur = 0
            for lvl in range(1, 5):
                def prmm(e, lvl=lvl, cur=cur):
                    for pk in range(4):
                        sl = slice(pk * 128, pk * 128 + 128)
                        e.matmul(ps[2][:, sl], ppk[lvl][:, sl], prt[cur][:, sl],
                                 start=True, stop=False)
                        r = e.matmul(ps[2][:, sl], ident[:], prt[cur][:, sl],
                                     start=False, stop=True)
                    return r
                pm = s.emit("tensor", prmm, waits=[("ve", pv)], inc="pe")
                pv = s.emit("vector", lambda e, cur=cur: e.tensor_copy(
                    prt[1 - cur][:], ps[2][:, :]), waits=[("pe", pm)], inc="ve")
                cur = 1 - cur
            def uwmm(e, cur=cur):
                for i in range(16):
                    pk, gb = i // 4, i % 4
                    lh = prt[cur][32 * gb:32 * gb + 32,
                                  pk * 128 + 32 * gb:pk * 128 + 32 * gb + 32]
                    e.matmul(ps[2][32 * gb:32 * gb + 32, (pk % 2) * 256:(pk % 2) * 256 + 256],
                             lh, vbtm[32 * gb:32 * gb + 32, pk, :],
                             start=True, stop=True, tile_position=(32 * gb, 32 * gb))
                    r = e.matmul(ps[3][32 * gb:32 * gb + 32, (pk % 2) * 256:(pk % 2) * 256 + 256],
                                 lh, kbtm[32 * gb:32 * gb + 32, pk, :],
                                 start=True, stop=True, tile_position=(32 * gb, 32 * gb))
                    if pk == 1 or pk == 3:
                        pass
                return r
            uws = []
            for half in range(2):
                def uwmm2(e, cur=cur, half=half):
                    for i in range(8 * half, 8 * half + 8):
                        pk, gb = i // 4, i % 4
                        lh = prt[cur][32 * gb:32 * gb + 32,
                                      pk * 128 + 32 * gb:pk * 128 + 32 * gb + 32]
                        e.matmul(ps[2][32 * gb:32 * gb + 32,
                                       (pk % 2) * 256:(pk % 2) * 256 + 256],
                                 lh, vbtm[32 * gb:32 * gb + 32, pk, :],
                                 start=True, stop=True, tile_position=(32 * gb, 32 * gb))
                        r = e.matmul(ps[3][32 * gb:32 * gb + 32,
                                           (pk % 2) * 256:(pk % 2) * 256 + 256],
                                 lh, kbtm[32 * gb:32 * gb + 32, pk, :],
                                 start=True, stop=True, tile_position=(32 * gb, 32 * gb))
                    return r
                um = s.emit("tensor", uwmm2, waits=[("ve", pv), ("ve", kbt)] +
                            ([("ve", uws[-1])] if uws else []), inc="pe")
                uv = s.emit("vector", lambda e, half=half: (
                    e.tensor_copy(utm[:].rearrange("p a d -> p (a d)")
                                  [:, half * 512:half * 512 + 512], ps[2][:, :]),
                    e.tensor_copy(wtm[:].rearrange("p a d -> p (a d)")
                                  [:, half * 512:half * 512 + 512], ps[3][:, :]))[-1],
                    waits=[("pe", um)], inc="ve")
                uws.append(uv)
            def wtr(e):
                for dkc in range(2):
                    for tt in range(4):
                        r = e.matmul(ps[4][:, tt * 128:tt * 128 + 128],
                                     wtm[:, tt, dkc * 128:dkc * 128 + 128], ident[:],
                                     start=True, stop=True)
                return r
            wv = uws[-1]
            for dkc in range(2):
                def wtr2(e, dkc=dkc):
                    for tt in range(4):
                        r = e.matmul(ps[4][:, tt * 128:tt * 128 + 128],
                                     wtm[:, tt, dkc * 128:dkc * 128 + 128], ident[:],
                                     start=True, stop=True)
                    return r
                wm = s.emit("tensor", wtr2, waits=[("ve", wv)], inc="pe")
                wv = s.emit("vector", lambda e, dkc=dkc: e.tensor_scalar(
                    wnf[:, dkc, :], ps[4][:, :], -1.0, None, op0=ALU.mult),
                    waits=[("pe", wm)], inc="ve")
            prep_done = wv
            for cc in range(4):
                csl = slice(128 * cc, 128 * cc + 128)
                ntt = t2t[:, 0:128]
                def r1(e, cc=cc, h=h, csl=csl):
                    e.matmul(ps[2][:, 0:256], ident[:], utm[:, cc, :], start=True, stop=False)
                    for dkc in range(2):
                        e.matmul(ps[2][:, 0:256], wnf[:, dkc, csl],
                                 s0b[h][:, dkc * 256:dkc * 256 + 256],
                                 start=False, stop=(dkc == 1))
                    for c2 in range(2):
                        e.matmul(ps[3][:, 0:128], kh_[:, c2, csl], wnf[:, c2, csl],
                                 start=(c2 == 0), stop=(c2 == 1))
                    for c2 in range(2):
                        r = e.matmul(ps[3][:, 128:256], kh_[:, c2, csl], qh_[:, c2, csl],
                                     start=(c2 == 0), stop=(c2 == 1))
                    return r
                m1 = s.emit("tensor", r1, waits=[("ve", prep_done),
                                                 ("ve", s.cnt["ve"])], inc="pe")
                v1 = s.emit("vector", lambda e: (
                    e.tensor_copy(y0t[:], ps[2][:, 0:256]),
                    e.tensor_mul(nnt[:], ps[3][:, 0:128], m_sbu[:]),
                    e.tensor_mul(attt[:], ps[3][:, 128:256], m_triu[:]))[-1],
                    waits=[("pe", m1)], inc="ve")
                def r2(e):
                    e.matmul(ps[2][:, 0:256], ident[:], y0t[:], start=True, stop=False)
                    e.matmul(ps[2][:, 0:256], nnt[:], y0t[:], start=False, stop=True)
                    return e.matmul(ps[4][:, 0:128], nnt[:], ident[:],
                                    start=True, stop=True)
                m2 = s.emit("tensor", r2, waits=[("ve", v1)], inc="pe")
                v2 = s.emit("vector", lambda e, ntt=ntt: (
                    e.tensor_copy(t1t[:], ps[2][:, 0:256]),
                    e.tensor_copy(ntt, ps[4][:, 0:128]))[-1],
                    waits=[("pe", m2)], inc="ve")
                def r3(e, h=h, csl=csl, ntt=ntt):
                    e.matmul(ps[4][:, 0:128], ntt, nnt[:], start=True, stop=True)
                    for dkc in range(2):
                        r = e.matmul(ps[5][:, 0:256], qh_[:, dkc, csl],
                                     s0b[h][:, dkc * 256:dkc * 256 + 256],
                                     start=(dkc == 0), stop=False)
                    return r
                m3 = s.emit("tensor", r3, waits=[("ve", v2)], inc="pe")
                v3 = s.emit("vector", lambda e: e.tensor_copy(nnt[:], ps[4][:, 0:128]),
                            waits=[("pe", m3)], inc="ve")
                def r4(e):
                    e.matmul(ps[2][:, 0:256], ident[:], t1t[:], start=True, stop=False)
                    return e.matmul(ps[2][:, 0:256], nnt[:], t1t[:], start=False, stop=True)
                m4 = s.emit("tensor", r4, waits=[("ve", v3)], inc="pe")
                v4 = s.emit("vector", lambda e: e.tensor_copy(uadj[:], ps[2][:, 0:256]),
                            waits=[("pe", m4)], inc="ve")
                def r5(e, cc=cc, h=h, g=g):
                    e.matmul(ps[5][:, 0:256], attt[:], uadj[:], start=False, stop=True)
                    for dkc in range(2):
                        r = e.matmul(ps[h][:, dkc * 256:dkc * 256 + 256],
                                     ktm[:, cc, dkc * 128:dkc * 128 + 128], uadj[:],
                                     start=(g == 0 and cc == 0 and dkc == 0),
                                     stop=(g == 15 and cc == 3 and dkc == 1),
                                     skip_group_check=True)
                    return r
                m5 = s.emit("tensor", r5, waits=[("ve", v4), ("ve", sinit)], inc="pe")
                v5 = s.emit("vector", lambda e, h=h: (
                    e.tensor_copy(oev[:], ps[5][:, 0:256]),
                    e.tensor_copy(s0b[h][:], ps[h][:, :]))[-1],
                    waits=[("pe", m5)], inc="ve")
                t_in_half = (g % 8) * GW + cc * 128
                dospill = s.dma("gpsimd", do_sp[g // 8, h, t_in_half:t_in_half + 128, :],
                                oev[:], waits=[("ve", v5)], inc="gQ")
                sref = v5

    a2f = agf
    a2d = s.emit("gpsimd", lambda e: e.collective_compute(
        "AllGather", ALU.bypass, replica_groups=PAIRS,
        ins=[do_sp], outs=[do_ag]), waits=[("gQ", dospill)], inc="cc")

    p3_es.close()
    p12_es.close()
    g1xw = E(nc.sbuf_tensor("g1xw", [128, 8, 2048], BF16))
    g1sw = E(nc.sbuf_tensor("g1sw", [64, 16, 128], BF16))
    g2w = E(nc.sbuf_tensor("g2w", [128, 16, 32], BF16))
    g1bc = E(nc.sbuf_tensor("g1bc", [128, 16], F32))
    owb = E(nc.sbuf_tensor("owb", [128, 8, HID], BF16))
    owf = E(nc.sbuf_tensor("owf", [128, HID], F32))
    onc = E(nc.sbuf_tensor("onc", [128, 8], F32))
    btile = [E(nc.sbuf_tensor(f"btile{i}", [128, 4, HID], BF16)) for i in range(4)]
    dtile = E(nc.sbuf_tensor("dtile", [128, 4, HID], F32))
    oac2 = E(nc.sbuf_tensor("oac2", [128, HID], F32))
    bt_a = E(nc.sbuf_tensor("bt_a", [128, HID], BF16))
    bt_b = E(nc.sbuf_tensor("bt_b", [128, HID], BF16))
    dt_a = E(nc.sbuf_tensor("dt_a", [128, HID], F32))
    mhc = E(nc.sbuf_tensor("mhc", [128, 2], F32))
    x2t = E(nc.sbuf_tensor("x2t", [128, HID], F32))
    statc = E(nc.sbuf_tensor("statc", [128, 64], F32))
    ss20 = E(nc.sbuf_tensor("ss20", [128, 24], F32))
    statf = E(nc.sbuf_tensor("statf", [64, GW], BF16))
    hdnb = E(nc.sbuf_tensor("hdnb", [128, 16, GW], BF16))
    lgf = E(nc.sbuf_tensor("lgf", [32, GW], F32))
    mrow = E(nc.sbuf_tensor("mrow", [4, 64], F32))
    ltm = E(nc.sbuf_tensor("ltm", [128, 4, 32], F32))
    w20 = E(nc.sbuf_tensor("w20", [128, 4, 32], F32))
    gsc = E(nc.sbuf_tensor("gsc", [32, 8], F32))
    fexp = E(nc.sbuf_tensor("fexp", [128, 32], F32))
    sexp = E(nc.sbuf_tensor("sexp", [128, 32], F32))
    oacc = E(nc.sbuf_tensor("oacc", [128, HID], F32))
    ofm = E(nc.sbuf_tensor("ofm", [128, 8, GW], BF16))
    xnbb = E(nc.sbuf_tensor("xnbb", [128, 8, GW], BF16))
    brw = E(nc.sbuf_tensor("brw", [1, 64], F32))

    stz0 = s.emit("vector", lambda e: e.memset(statc[:], 0.0),
                  waits=[("sc", s.cnt["sc"]), ("pe", s.cnt["pe"]),
                         ("gQ", s.cnt["gQ"]), ("sQ", s.cnt["sQ"])], inc="ve")

    gl0 = s.dma("gpsimd", mhc[:], mh_in)
    gl = s.dma("gpsimd", g1xw[:], rr(g1xT_in))
    gl = s.dma("gpsimd", g1sw[:], g1sT_in.rearrange("r (c t) -> r c t", t=128))
    gl = s.dma("gpsimd", g2w[:], rr(g2T_in))
    gl = s.dma("gpsimd", g1bc[:], g1b_in)
    gl = s.dma("gpsimd", onc[:], onormc_in)
    mk1 = s.dma("sync", mrow[0:4, 0:32], ltmask_in, inc="sQ")
    mk1 = s.dma("sync", mrow[0:4, 32:64], fmask_in, inc="sQ")
    lt1 = s.dma("sync", gsc[0:4, 0:1], ltvec_in, inc="sQ")
    lt1 = s.dma("sync", gsc[0:4, 1:2], flvec_in, inc="sQ")
    lt1 = s.dma("sync", gsc[0:32, 2:3], g2b_in, inc="sQ")
    lt1 = s.dma("sync", gsc[0:32, 7:8], vb_in, inc="sQ")
    owd = 0
    for kc in range(8):
        lw = s.dma("sync", owf[:], rr(o_wT_in)[:, kc, :],
                   waits=[("gQ", gl), ("ve", owd)], inc="sQ")
        owd = s.emit("vector", lambda e, kc=kc: e.tensor_scalar(
            owb[:, kc, :], owf[:], onc[:, kc:kc + 1], None, op0=ALU.mult),
            waits=[("sQ", lw)], inc="ve")
    gc1 = s.emit("scalar", lambda e: (
        e.activation(gsc[0:4, 4:5], gsc[0:4, 0:1], ACT.Exp, scale=-1.0),
        e.activation(gsc[0:4, 5:6], gsc[0:4, 1:2], ACT.Sigmoid))[-1],
        waits=[("sQ", lt1), ("sQ", mk1)], inc="sc")
    def gc2f(e):
        gap(e)
        e.tensor_scalar(gsc[0:4, 6:7], gsc[0:4, 5:6], -4.0, 1.0,
                        op0=ALU.mult, op1=ALU.add)
        gap(e)
        e.tensor_add(gsc[0:32, 3:4], gsc[0:32, 2:3], gsc[0:32, 7:8])
        gap(e)
        e.tensor_copy(gsc[0:32, 2:3], gsc[0:32, 3:4])
        gap(e)
        return e.tensor_copy(tmpcol[:], cstb[:, 0:1])
    gc2 = s.emit("vector", gc2f, waits=[("sc", gc1)], inc="ve")
    def gmms(e):
        e.matmul(ps[6][0:32, 0:1], mrow[0:4, 0:32], gsc[0:4, 4:5], start=True, stop=True)
        e.matmul(ps[6][0:32, 1:2], mrow[0:4, 32:64], gsc[0:4, 5:6], start=True, stop=True)
        return e.matmul(ps[6][0:32, 2:3], mrow[0:4, 0:32], gsc[0:4, 6:7],
                        start=True, stop=True)
    gm1 = s.emit("tensor", gmms, waits=[("ve", gc2)], inc="pe")
    def gev(e):
        e.tensor_copy(gsc[0:32, 4:7], ps[6][0:32, 0:3])
        gap(e)
        e.tensor_mul(gsc[0:32, 3:4], gsc[0:32, 2:3], gsc[0:32, 4:5])
        gap(e)
        return e.tensor_copy(tmpcol[:], cstb[:, 0:1])
    gv1 = s.emit("vector", gev, waits=[("pe", gm1)], inc="ve")
    def growmm(e):
        e.matmul(ps[6][0:1, 0:32], gsc[0:32, 5:6], identf[0:32, 0:32], start=True, stop=True)
        return e.matmul(ps[6][0:1, 32:64], gsc[0:32, 6:7], identf[0:32, 0:32],
                        start=True, stop=True)
    gm2 = s.emit("tensor", growmm, waits=[("ve", gv1)], inc="pe")
    gv2 = s.emit("vector", lambda e: e.tensor_copy(brw[0:1, 0:64], ps[6][0:1, 0:64]),
                 waits=[("pe", gm2)], inc="ve")
    def gbcmm(e):
        e.matmul(ps[6][:, 0:32], ones_row[:].bitcast(F32), brw[0:1, 0:32],
                 start=True, stop=True)
        return e.matmul(ps[6][:, 32:64], ones_row[:].bitcast(F32), brw[0:1, 32:64],
                        start=True, stop=True)
    gm3 = s.emit("tensor", gbcmm, waits=[("ve", gv2)], inc="pe")
    gv3 = s.emit("vector", lambda e: (e.tensor_copy(fexp[:], ps[6][:, 0:32]),
                                      e.tensor_copy(sexp[:], ps[6][:, 32:64]))[-1],
                 waits=[("pe", gm3)], inc="ve")

    STT = [("short", 0), ("long", 1), ("wide", 2), ("delta", 3), ("v", 4)]
    prevtile = {"ve": 0, "sp": 0, "pe": 0}
    for gb in range(8):
        xl = s.dma("gpsimd", xnbb[:], rr(xn_sp)[:, :, HALO + gb * GW:HALO + gb * GW + GW],
                   inc="gQ")
        st_done = []
        for tt in range(4):
            til = gb * 4 + tt
            tw = [("cc", a2f), ("cc", a2d), ("ve", prevtile["ve"]),
                  ("pe", prevtile["pe"])]
            lds = 0
            vcomb = 0
            for f in range(4):
                ca = PADL + til * 128
                cb = PADL + NT + til * 128
                for sl in range(2):
                    lds = s.xdma("sync", lambda e, f=f, sl=sl, ca=ca:
                        e.dma_start_transpose(
                            bt_a[:, sl * 512:sl * 512 + 512],
                            fir4_ag[sl, f, :, ca:ca + 128]),
                        waits=tw + [("ve", vcomb)], inc="sQ")
                    lds = s.xdma("sync", lambda e, f=f, sl=sl, cb=cb:
                        e.dma_start_transpose(
                            bt_b[:, sl * 512:sl * 512 + 512],
                            fir4_ag[sl, f, :, cb:cb + 128]),
                        waits=tw, inc="sQ")
                def fcomb(e, f=f, tt=tt):
                    e.tensor_scalar(oac2[:], bt_a[:], mhc[:, 0:1], None, op0=ALU.mult)
                    return e.scalar_tensor_tensor(btile[f][:, tt, :], bt_b[:],
                                                  mhc[:, 1:2], oac2[:],
                                                  op0=ALU.mult, op1=ALU.add)
                vcomb = s.emit("vector", fcomb, waits=[("sQ", lds), ("gQ", gl0)],
                               inc="ve")
            ldd = 0
            for sl in range(2):
                for lh in range(2):
                    hd = sl * 2 + lh
                    ldd = s.dma("sync", dt_a[:, hd * 256:hd * 256 + 256],
                                do_ag[sl, 0, lh, til * 128:til * 128 + 128, :],
                                waits=tw + [("ve", vcomb)], inc="sQ")
                    ldd = s.dma("sync", oacc[:, hd * 256:hd * 256 + 256],
                                do_ag[sl, 1, lh, til * 128:til * 128 + 128, :],
                                waits=tw, inc="sQ")
            def dcomb(e, tt=tt):
                e.tensor_scalar(dt_a[:], dt_a[:], mhc[:, 0:1], None, op0=ALU.mult)
                gap(e)
                return e.scalar_tensor_tensor(dtile[:, tt, :], oacc[:],
                                              mhc[:, 1:2], dt_a[:],
                                              op0=ALU.mult, op1=ALU.add)
            vcomb = s.emit("vector", dcomb, waits=[("sQ", ldd)], inc="ve")
            ldd_v = ("ve", vcomb)
            lds = lds
            vops = 0
            for T, (nm, ti) in enumerate(STT):
                if ti == 3:
                    seg = dtile[:, tt, :].rearrange("p (a d) -> p a d", d=256)
                    wkey = [("ve", vcomb)]
                else:
                    fi = [0, 1, 2, None, 3][ti]
                    seg = btile[fi][:, tt, :].rearrange("p (a d) -> p a d", d=256)
                    wkey = [("ve", vcomb)]
                def stat1(e, seg=seg, T=T):
                    e.tensor_reduce(statc[:, T * 4:T * 4 + 4], seg, axis=AX.X, op=ALU.add)
                    return e.tensor_reduce(statc[:, 40 + T * 4:40 + T * 4 + 4], seg,
                                           axis=AX.X, op=ALU.max,
                                           apply_absolute_value=True)
                vops = s.emit("vector", stat1, waits=wkey + [("pe", prevtile["pe"])],
                              inc="ve")
                sqs = s.emit("scalar", lambda e, seg=seg: e.activation(
                    x2t[:].rearrange("p (a d) -> p a d", d=256), seg, ACT.Square),
                    waits=wkey + [("ve", vops), ("ve", s.cnt["ve"])], inc="sc")
                vops = s.emit("vector", lambda e, T=T: e.tensor_reduce(
                    ss20[:, T * 4:T * 4 + 4], x2t[:].rearrange("p (a d) -> p a d", d=256),
                    axis=AX.X, op=ALU.add), waits=[("sc", sqs)], inc="ve")
            def stfin(e):
                gap(e)
                e.tensor_scalar(statc[:, 0:20], statc[:, 0:20], 1.0 / 256.0, None,
                                op0=ALU.mult)
                e.tensor_scalar(ss20[:, 0:20], ss20[:, 0:20], 1.0 / 256.0, 1e-8,
                                op0=ALU.mult, op1=ALU.max)
                gap(e)
                return e.tensor_copy(tmpcol[:], cstb[:, 0:1])
            vf1 = s.emit("vector", stfin, waits=[("ve", vops)], inc="ve")
            sf1 = s.emit("scalar", lambda e: (
                e.activation(ss20[:, 0:20], ss20[:, 0:20], ACT.Ln),
                e.activation(statc[:, 20:40], ss20[:, 0:20], ACT.Exp, scale=0.5))[-1],
                waits=[("ve", vf1)], inc="sc")
            stm = s.emit("tensor", lambda e: e.matmul(
                ps[7][0:64, 0:128], statc[:, 0:64], identf[:], start=True, stop=True),
                waits=[("sc", sf1)], inc="pe")
            stv = s.emit("vector", lambda e, tt=tt: e.tensor_copy(
                statf[:, tt * 128:tt * 128 + 128], ps[7][0:64, 0:128]),
                waits=[("pe", stm)], inc="ve")
            st_done.append(stv)
            prevtile["pe"] = stm
            prevtile["ve"] = stv
        g1ev = 0
        for oc in range(16):
            bank = oc % 2
            def g1mm(e, oc=oc, bank=bank):
                for kc in range(8):
                    e.matmul(ps[bank][:, :], g1xw[:, kc, oc * 128:oc * 128 + 128],
                             xnbb[:, kc, :], start=(kc == 0), stop=False)
                return e.matmul(ps[bank][:, :], g1sw[:, oc, :], statf[:],
                                start=False, stop=True)
            gw_ = [("gQ", xl), ("ve", st_done[-1])]
            g1m = s.emit("tensor", g1mm, waits=gw_, inc="pe")
            g1ev = s.emit("scalar", lambda e, oc=oc, bank=bank: e.activation(
                hdnb[:, oc, :], ps[bank][:, :], ACT.Gelu, bias=g1bc[:, oc:oc + 1]),
                waits=[("pe", g1m)], inc="sc")
        def g2mm(e):
            for kc in range(16):
                i = e.matmul(ps[2][0:32, :], g2w[:, kc, :], hdnb[:, kc, :],
                             start=(kc == 0), stop=(kc == 15))
            return i
        g2m = s.emit("tensor", g2mm, waits=[("sc", g1ev)], inc="pe")
        g2e = s.emit("scalar", lambda e: e.activation(
            lgf[:, :], ps[2][0:32, :], ACT.Identity, bias=gsc[0:32, 3:4],
            scale=gsc[0:32, 4:5]), waits=[("pe", g2m), ("ve", gv3)], inc="sc")
        def ltr(e):
            for tt in range(4):
                r = e.matmul(ps[3][:, tt * 32:tt * 32 + 32],
                             lgf[0:32, tt * 128:tt * 128 + 128], identf[0:32, 0:32],
                             start=True, stop=True)
            return r
        lm = s.emit("tensor", ltr, waits=[("sc", g2e)], inc="pe")
        lv = s.emit("vector", lambda e: e.tensor_copy(
            ltm[:].rearrange("p a c -> p (a c)"), ps[3][:, 0:128]),
            waits=[("pe", lm)], inc="ve")
        def smax(e):
            for tt in range(4):
                sl = ltm[:, tt, 0:20].rearrange("p (h j) -> p h j", j=5)
                e.tensor_reduce(w20[:, tt, 20:24], sl, axis=AX.X, op=ALU.max)
            gap(e)
            for tt in range(4):
                for h4 in range(4):
                    e.tensor_scalar(ltm[:, tt, h4 * 5:h4 * 5 + 5],
                                    ltm[:, tt, h4 * 5:h4 * 5 + 5],
                                    w20[:, tt, 20 + h4:21 + h4], None, op0=ALU.subtract)
            return e.tensor_copy(tmpcol[:], cstb[:, 0:1])
        sv1 = s.emit("vector", smax, waits=[("ve", lv)], inc="ve")
        se1 = s.emit("scalar", lambda e: e.activation(
            ltm[:].rearrange("p a c -> p (a c)"), ltm[:].rearrange("p a c -> p (a c)"),
            ACT.Exp), waits=[("ve", sv1)], inc="sc")
        def ssum(e):
            for tt in range(4):
                sl = ltm[:, tt, 0:20].rearrange("p (h j) -> p h j", j=5)
                e.tensor_reduce(w20[:, tt, 20:24], sl, axis=AX.X, op=ALU.add)
            gap(e)
            return e.tensor_copy(tmpcol[:], cstb[:, 0:1])
        sv2 = s.emit("vector", ssum, waits=[("sc", se1)], inc="ve")
        se2 = s.emit("scalar", lambda e: (
            e.activation(w20[:, 0, 24:28], w20[:, 0, 20:24], ACT.Ln),
            e.activation(w20[:, 1, 24:28], w20[:, 1, 20:24], ACT.Ln),
            e.activation(w20[:, 2, 24:28], w20[:, 2, 20:24], ACT.Ln),
            e.activation(w20[:, 3, 24:28], w20[:, 3, 20:24], ACT.Ln),
            e.activation(w20[:, 0, 28:32], w20[:, 0, 24:28], ACT.Exp, scale=-1.0),
            e.activation(w20[:, 1, 28:32], w20[:, 1, 24:28], ACT.Exp, scale=-1.0),
            e.activation(w20[:, 2, 28:32], w20[:, 2, 24:28], ACT.Exp, scale=-1.0),
            e.activation(w20[:, 3, 28:32], w20[:, 3, 24:28], ACT.Exp, scale=-1.0))[-1],
            waits=[("ve", sv2)], inc="sc")
        def wmk(e):
            for tt in range(4):
                for h4 in range(4):
                    e.tensor_scalar(ltm[:, tt, h4 * 5:h4 * 5 + 5],
                                    ltm[:, tt, h4 * 5:h4 * 5 + 5],
                                    w20[:, tt, 28 + h4:29 + h4], None, op0=ALU.mult)
            gap(e)
            for tt in range(4):
                e.scalar_tensor_tensor(ltm[:, tt, 0:20], ltm[:, tt, 0:20], 1.0,
                                       sexp[:, 0:20], op0=ALU.mult, op1=ALU.mult)
            gap(e)
            for tt in range(4):
                e.tensor_add(w20[:, tt, 0:20], ltm[:, tt, 0:20], fexp[:, 0:20])
            gap(e)
            return e.tensor_copy(tmpcol[:], cstb[:, 0:1])
        sv3 = s.emit("vector", wmk, waits=[("sc", se2), ("ve", gv3)], inc="ve")
        for tt in range(4):
            til = gb * 4 + tt
            def ocomb(e, tt=tt):
                for h4 in range(4):
                    e.tensor_scalar(oac2[:, h4 * 256:h4 * 256 + 256],
                                    btile[0][:, tt, h4 * 256:h4 * 256 + 256],
                                    w20[:, tt, h4 * 5:h4 * 5 + 1], None, op0=ALU.mult)
                srcs = [(btile[1][:, tt, :], 1), (btile[2][:, tt, :], 2),
                        (dtile[:, tt, :], 3), (btile[3][:, tt, :], 4)]
                bufs = [oac2, oacc, oac2, oacc, oac2]
                for step, (bsl, j) in enumerate(srcs):
                    src_acc = bufs[step]
                    dst_acc = bufs[step + 1]
                    for h4 in range(4):
                        e.scalar_tensor_tensor(dst_acc[:, h4 * 256:h4 * 256 + 256],
                                               bsl[:, h4 * 256:h4 * 256 + 256],
                                               w20[:, tt, h4 * 5 + j:h4 * 5 + j + 1],
                                               src_acc[:, h4 * 256:h4 * 256 + 256],
                                               op0=ALU.mult, op1=ALU.add)
                return e.tensor_copy(tmpcol[:], cstb[:, 0:1])
            ov = s.emit("vector", ocomb, waits=[("ve", sv3)], inc="ve")
            sq_o = s.emit("scalar", lambda e: e.activation(
                x2t[:].rearrange("p (a d) -> p a d", d=256),
                oac2[:].rearrange("p (a d) -> p a d", d=256), ACT.Square),
                waits=[("ve", ov)], inc="sc")
            rv1 = s.emit("vector", lambda e: e.tensor_reduce(
                w20[:, 0, 20:24] if False else ss20[:, 20:24],
                x2t[:].rearrange("p (a d) -> p a d", d=256), axis=AX.X, op=ALU.add),
                waits=[("sc", sq_o)], inc="ve")
            rs1 = s.emit("scalar", lambda e: (
                e.activation(ss20[:, 20:24], ss20[:, 20:24], ACT.Ln,
                             scale=1.0 / 256.0, bias=1e-5),
                e.activation(ss20[:, 20:24], ss20[:, 20:24], ACT.Exp, scale=-0.5))[-1],
                waits=[("ve", rv1)], inc="sc")
            rv2 = s.emit("vector", lambda e: [
                e.tensor_scalar(oacc[:, h4 * 256:h4 * 256 + 256],
                                oac2[:, h4 * 256:h4 * 256 + 256],
                                ss20[:, 20 + h4:21 + h4], None, op0=ALU.mult)
                for h4 in range(4)][-1], waits=[("sc", rs1)], inc="ve")
            def otr(e):
                for c8 in range(8):
                    r = e.matmul(ps[4 + c8 // 4][:, (c8 % 4) * 128:(c8 % 4) * 128 + 128],
                                 oacc[:, c8 * 128:c8 * 128 + 128], identf[:],
                                 start=True, stop=True)
                return r
            om = s.emit("tensor", otr, waits=[("ve", rv2)], inc="pe")
            ovv = s.emit("vector", lambda e, tt=tt: (
                e.tensor_copy(ofm[:, 0:4, tt * 128:tt * 128 + 128],
                              ps[4][:, :].rearrange("p (c t) -> p c t", t=128)),
                e.tensor_copy(ofm[:, 4:8, tt * 128:tt * 128 + 128],
                              ps[5][:, :].rearrange("p (c t) -> p c t", t=128)))[-1],
                waits=[("pe", om)], inc="ve")
            prevtile["ve"] = max(prevtile["ve"], ovv)
        fvk = {0: 0, 1: 0}
        evg = {0: 0, 1: 0}
        for oc in range(8):
            bank = 2 + oc % 2
            def owmm(e, oc=oc, bank=bank):
                for kc in range(8):
                    i = e.matmul(ps[bank][:, :], owb[:, kc, oc * 128:oc * 128 + 128],
                                 ofm[:, kc, :], start=(kc == 0), stop=(kc == 7))
                return i
            owm = s.emit("tensor", owmm, waits=[("ve", prevtile["ve"]),
                                                ("ve", owd), ("ve", fvk[oc % 2])],
                         inc="pe")
            lo = s.dma("sync", tq[oc % 2][:], rr(out_fm)[:, oc, gb * GW:gb * GW + GW],
                       waits=[("pe", owm), ("ve", fvk[oc % 2])], inc="sQ")
            fv = s.emit("vector", lambda e, oc=oc, bank=bank: e.tensor_add(
                ev[oc % 2][:], ps[bank][:, :], tq[oc % 2][:]),
                waits=[("sQ", lo), ("pe", owm), ("gQ", evg[oc % 2])], inc="ve")
            p3b_sp = s.dma("gpsimd", rr(out_fm)[:, oc, gb * GW:gb * GW + GW],
                           ev[oc % 2][:], waits=[("ve", fv)], inc="gQ")
            fvk[oc % 2] = fv
            evg[oc % 2] = p3b_sp
            prevtile["pe"] = owm

    s.emit("gpsimd", lambda e: None,
           waits=[("gQ", s.cnt["gQ"]), ("sQ", s.cnt["sQ"]), ("ve", s.cnt["ve"]),
                  ("sc", s.cnt["sc"]), ("pe", s.cnt["pe"]), ("cc", s.cnt["cc"])])


_cache = {}


def kernel(**inputs):
    x = np.ascontiguousarray(inputs["L0_attn_hidden_states"], dtype=np.float32)
    B, L, _ = x.shape
    w_up = np.ascontiguousarray(inputs["L0_ffn_w_up"], dtype=np.float32)
    w_gate = np.ascontiguousarray(inputs["L0_ffn_w_gate"], dtype=np.float32)
    w_down = np.ascontiguousarray(inputs["L0_ffn_w_down"], dtype=np.float32)

    qT = np.ascontiguousarray(inputs["L0_attn_q_w"].astype(np.float32).T)
    kT = np.ascontiguousarray(inputs["L0_attn_k_w"].astype(np.float32).T)
    vT = np.ascontiguousarray(inputs["L0_attn_v_w"].astype(np.float32).T)
    bT = np.ascontiguousarray(inputs["L0_attn_b_w"].astype(np.float32).T)
    convs = [np.asarray(inputs[k], np.float32) for k in
             ("L0_attn_convq_w", "L0_attn_convk_w", "L0_attn_convv_w")]
    firs = [np.asarray(inputs[k], np.float32).reshape(HID, -1) for k in
            ("L0_attn_fir_s", "L0_attn_fir_l", "L0_attn_fir_wd")]
    g1w = np.asarray(inputs["L0_attn_g1_w"], np.float32)
    g1b = np.asarray(inputs["L0_attn_g1_b"], np.float32)
    g2w = np.asarray(inputs["L0_attn_g2_w"], np.float32)
    g2b = np.asarray(inputs["L0_attn_g2_b"], np.float32)
    vb = np.asarray(inputs["L0_attn_value_bias"], np.float32)
    ltv = np.asarray(inputs["L0_attn_log_temp"], np.float32)
    flv = np.asarray(inputs["L0_attn_floor_logit"], np.float32)
    onw = np.asarray(inputs["L0_attn_onorm_w"], np.float32)
    owT = np.ascontiguousarray(inputs["L0_attn_o_w"].astype(np.float32).T)

    filt_all = convs + firs
    bands_full = np.zeros((6, HID, 256, 128), np.float32)
    s2 = np.arange(256)[:, None]
    tt_ = np.arange(128)[None, :]
    for f, wf in enumerate(filt_all):
        K = wf.shape[1]
        tau = s2 - 128 - tt_ + K - 1
        valid = (tau >= 0) & (tau < K)
        tcl = np.clip(tau, 0, K - 1)
        bands_full[f] = np.where(valid[None, :, :], wf[:, tcl], 0.0)

    g1xT = np.ascontiguousarray(g1w[:, 0:HID].T)
    g1sT = np.zeros((64, 2048), np.float32)
    for stat in range(3):
        for T in range(5):
            for h in range(4):
                g1sT[stat * 20 + T * 4 + h] = g1w[:, HID + T * 12 + stat * 4 + h]
    g2T = np.zeros((2048, 32), np.float32)
    g2T[:, 0:20] = g2w.T
    g2bc = np.zeros((32, 1), np.float32); g2bc[0:20, 0] = g2b
    vbc = np.zeros((32, 1), np.float32)
    for h in range(4):
        vbc[h * 5 + 4, 0] = vb[h]
    ltmask = np.zeros((4, 32), np.float32)
    fmask = np.zeros((4, 32), np.float32)
    for h in range(4):
        for j in range(5):
            ltmask[h, h * 5 + j] = 1.0
            if j < 4:
                fmask[h, h * 5 + j] = 1.0
    onormc = np.zeros((128, 8), np.float32)
    for kc in range(8):
        onormc[:, kc] = onw[(np.arange(128) + 128 * kc) % 256]
    bd32 = np.tril(np.ones((32, 32), np.float32), -1)
    bd128 = np.zeros((128, 128), np.float32)
    for gblk in range(4):
        bd128[gblk * 32:(gblk + 1) * 32, gblk * 32:(gblk + 1) * 32] = bd32
    m_bdtril = np.tile(bd128, (1, 4))
    m_ipack = np.tile(np.eye(128, dtype=np.float32), (1, 4))
    m_sbu = np.zeros((128, 128), np.float32)
    for a in range(4):
        for bb2 in range(4):
            if a < bb2:
                m_sbu[a * 32:a * 32 + 32, bb2 * 32:bb2 * 32 + 32] = 1.0
    m_triu = np.triu(np.ones((128, 128), np.float32), 0)
    ident = np.eye(128, dtype=np.float32)

    nc = build_program(dbg=DEBUG)

    cstv = np.zeros((128, 16), np.float32)
    for i, v in enumerate(CONSTS):
        cstv[:, i] = np.float32(v)

    in_maps = []
    for c in range(NCORE):
        b, hh = c // 2, c % 2
        t0 = hh * NT
        xs = np.zeros((HID, NTOT), np.float32)
        xs[:, HALO:] = x[b, t0:t0 + NT, :].T
        wq = np.zeros((HID, 1540), np.float32)
        wq[:, 0:512] = qT[:, 512 * hh:512 * hh + 512]
        wq[:, 512:1024] = kT[:, 512 * hh:512 * hh + 512]
        wq[:, 1024:1536] = vT[:, 512 * hh:512 * hh + 512]
        wq[:, 1536:1538] = bT[:, 2 * hh:2 * hh + 2]
        in_maps.append({
            "x_fm": xs, "w_up": w_up, "w_gate": w_gate, "w_down": w_down,
            "ones_in": np.ones((128, 1), np.float32),
            "onesr_in": np.ones((1, 128), np.float32),
            "cst_in": cstv,
            "wqkvbT": wq,
            "bands_in": np.ascontiguousarray(bands_full[:, 512 * hh:512 * hh + 512]),
            "o_wT_in": owT, "g1xT_in": g1xT, "g1sT_in": g1sT,
            "g1b_in": np.ascontiguousarray(g1b.reshape(16, 128).T), "g2T_in": g2T,
            "g2b_in": g2bc, "vb_in": vbc,
            "ltmask_in": ltmask, "fmask_in": fmask,
            "ltvec_in": ltv.reshape(4, 1), "flvec_in": flv.reshape(4, 1),
            "onormc_in": onormc, "m_bdtril_in": m_bdtril, "m_ipack_in": m_ipack,
            "m_sbu_in": m_sbu, "m_triu_in": m_triu, "ident_in": ident,
            "mh_in": np.ascontiguousarray(
                np.tile(np.array([1.0 - hh, float(hh)], np.float32), (128, 1))),
        })

    res = run_bass_kernel_spmd(nc, in_maps, core_ids=list(range(NCORE)))
    out = np.empty((B, L, HID), np.float32)
    for c in range(NCORE):
        b, hh = c // 2, c % 2
        out[b, hh * NT:(hh + 1) * NT, :] = res.results[c]["out_fm"].T
    if DEBUG:
        kernel.dbg = [{k: v for k, v in r.items() if k != "out_fm"} for r in res.results]
    return out
